# version 01 (final)
# baseline (speedup 1.0000x reference)
import numpy as np
import ml_dtypes
import concourse.bass as bass
import concourse.mybir as mybir
from concourse.bass_utils import run_bass_kernel_spmd
from contextlib import ExitStack

F32 = mybir.dt.float32
BF16 = mybir.dt.bfloat16
AF = mybir.ActivationFunctionType
ALU = mybir.AluOpType
AX = mybir.AxisListType
BFNP = ml_dtypes.bfloat16

NCORES = 8
B_, S_, D_ = 2, 8192, 1024
MTOK = 2048
DT = 8
W_ = MTOK
ENGINES = ("sync", "vector", "scalar", "tensor", "gpsimd")


class Sem:
    def __init__(self, h):
        self.h = h
        self.n = 0


class Bld:
    def __init__(self, nc, stack):
        self.nc = nc
        self.stack = stack
        self.q = {e: [] for e in ENGINES}
        self.sems = {}
        self.tiles = {}

    def sem(self, name):
        if name not in self.sems:
            h = self.stack.enter_context(self.nc.semaphore(name))
            self.sems[name] = Sem(h)
        return self.sems[name]

    def sbuf(self, name, shape, dt):
        if name in self.tiles:
            return self.tiles[name]
        t = self.stack.enter_context(self.nc.sbuf_tensor(name, list(shape), dt))
        self.tiles[name] = t
        return t

    def psum(self, name, shape, dt=F32):
        t = self.stack.enter_context(self.nc.psum_tensor(name, list(shape), dt))
        self.tiles[name] = t
        return t

    def on(self, eng, fn):
        self.q[eng].append(fn)

    def wait(self, eng, tok):
        if tok is None:
            return
        if isinstance(tok, list):
            for t in tok:
                self.wait(eng, t)
            return
        sem, n = tok
        if n <= 0:
            return
        assert n <= sem.n, f"wait {eng} for {n} but only {sem.n} emitted"
        self.on(eng, lambda e, s=sem.h, v=n: e.wait_ge(s, v))

    def dma(self, eng, out, in_, sem):
        sem.n += 16
        n = sem.n
        self.on(eng, lambda e, o=out, i=in_, s=sem.h: e.dma_start(out=o, in_=i)
                .then_inc(s, 16))
        return (sem, n)

    def op(self, eng, fn, sem=None):
        if sem is None:
            self.on(eng, fn)
            return None
        sem.n += 1
        n = sem.n
        self.on(eng, lambda e, f=fn, s=sem.h: f(e).then_inc(s, 1))
        return (sem, n)

    def cc(self, kind, op, groups, ins, outs, sem):
        sem.n += 1
        n = sem.n
        self.on("gpsimd", lambda e, s=sem.h: e.collective_compute(
            kind, op, replica_groups=groups, ins=ins, outs=outs).then_inc(s))
        return (sem, n)

    def barrier(self, engines=ENGINES):
        toks = [(s, s.n) for s in self.sems.values() if s.n > 0]
        for eng in engines:
            for t in toks:
                self.wait(eng, t)

    def finalize(self):
        block = self.stack.enter_context(self.nc.Block())
        for name in ENGINES:
            ops = self.q[name]
            if not ops:
                continue
            deco = getattr(block, name)

            def body(e, ops=ops):
                for fn in ops:
                    fn(e)
            deco(body)


def later(*toks):
    best = {}
    for t in toks:
        if t is None:
            continue
        if isinstance(t, list):
            for u in t:
                if u is None:
                    continue
                s, n = u
                if s not in best or best[s] < n:
                    best[s] = n
            continue
        s, n = t
        if s not in best or best[s] < n:
            best[s] = n
    return [(s, n) for s, n in best.items()]


def _t(a):
    return np.ascontiguousarray(np.asarray(a, np.float32).T)


def _pcol(v, p=128):
    v = np.asarray(v, np.float32)
    n = max(1, v.shape[0] // p)
    return np.ascontiguousarray(v.reshape(n, p).T)


def prep_inputs(inp):
    g = {}
    pc = [{} for _ in range(NCORES)]

    x0 = np.asarray(inp["L0_attn_x"], np.float32)
    x0T = np.ascontiguousarray(x0.transpose(0, 2, 1))

    for c in range(NCORES):
        b, q = c // 4, c % 4
        pc[c]["x0chunk"] = np.ascontiguousarray(
            x0T[b][:, MTOK * q:MTOK * (q + 1)])

    ohm = np.zeros((16, 16 * 128), np.float32)
    for n in range(16):
        ohm[n, n * 128:(n + 1) * 128] = 1.0
    g["m0_oh"] = ohm.astype(BFNP)
    for c in range(NCORES):
        bb, dr, hf = c // 4, (c % 4) // 2, c % 2
        oh_ = 1 - hf
        p = "f" if dr == 0 else "b"
        xt = x0T[bb]
        if dr == 1:
            xt = xt[:, ::-1]
        pc[c]["m0_xT"] = np.ascontiguousarray(xt)
        inw = np.asarray(inp[f"L0_attn_{p}_in_w"], np.float32)
        pc[c]["m0_inwT"] = np.ascontiguousarray(np.concatenate(
            [inw[1024 * hf:1024 * hf + 1024],
             inw[1024 * oh_:1024 * oh_ + 1024],
             inw[2048 + 1024 * hf:2048 + 1024 * hf + 1024]],
            0).T)
        cw = np.asarray(inp[f"L0_attn_{p}_conv_w"], np.float32)
        cw = np.concatenate([cw[1024 * hf:1024 * hf + 1024],
                             cw[1024 * oh_:1024 * oh_ + 1024]], 0)
        pc[c]["m0_convw"] = np.ascontiguousarray(
            cw.reshape(16, 128, 4).transpose(1, 0, 2).reshape(128, 64))
        cb = np.asarray(inp[f"L0_attn_{p}_conv_b"], np.float32)
        cb = np.concatenate([cb[1024 * hf:1024 * hf + 1024],
                             cb[1024 * oh_:1024 * oh_ + 1024]], 0)
        pc[c]["m0_convb"] = _pcol(cb)
        xp = np.asarray(inp[f"L0_attn_{p}_xproj_w"], np.float32)
        xp = np.concatenate([xp[:, 1024 * hf:1024 * hf + 1024],
                             xp[:, 1024 * oh_:1024 * oh_ + 1024]], 1)
        pc[c]["m0_xprojT"] = np.ascontiguousarray(xp.T)
        dtw = np.asarray(inp[f"L0_attn_{p}_dt_w"], np.float32)
        pc[c]["m0_dtwT"] = np.ascontiguousarray(
            dtw[1024 * hf:1024 * hf + 1024].T)
        pc[c]["m0_dtb"] = _pcol(
            np.asarray(inp[f"L0_attn_{p}_dt_b"], np.float32)
            [1024 * hf:1024 * hf + 1024])
        al = np.asarray(inp[f"L0_attn_{p}_A_log"], np.float32)
        pc[c]["m0_Alog"] = np.ascontiguousarray(
            al[1024 * hf:1024 * hf + 1024].reshape(8, 128, 16)
            .transpose(1, 0, 2).reshape(128, 128))
        pc[c]["m0_D"] = _pcol(
            np.asarray(inp[f"L0_attn_{p}_D"], np.float32)
            [1024 * hf:1024 * hf + 1024])
        ow = np.asarray(inp[f"L0_attn_{p}_out_w"], np.float32)
        pc[c]["m0_outwT"] = np.ascontiguousarray(
            ow.T[1024 * hf:1024 * hf + 1024])
        pc[c]["m0_flip"] = np.full((128, 1), float(dr), np.float32)

    g["f0_w1T"] = _t(inp["L0_ffn_w1_w"])
    g["f0_w2T"] = _t(inp["L0_ffn_w2_w"])
    g["f0_h1T"] = _t(inp["L0_ffn_h1_w"])
    g["f0_h2T"] = _t(inp["L0_ffn_h2_w"])
    g["f0_w3T"] = _t(inp["L0_ffn_w3_w"])
    g["f0_s1"] = _pcol(inp["L0_ffn_w1_s"])
    g["f0_s2"] = _pcol(inp["L0_ffn_w2_s"])
    g["f0_h1s"] = _pcol(inp["L0_ffn_h1_s"])
    g["f0_h2s"] = _pcol(inp["L0_ffn_h2_s"])
    g["f0_s3"] = _pcol(inp["L0_ffn_w3_s"])

    g["a1_WqT"] = _t(inp["L1_attn_Wq"])
    g["a1_WkT"] = _t(inp["L1_attn_Wk"])
    g["a1_WvT"] = _t(inp["L1_attn_Wv"])
    g["a1_WoT"] = _t(inp["L1_attn_Wo"])
    g["a1_bq"] = _pcol(inp["L1_attn_bq"])
    g["a1_bo"] = _pcol(inp["L1_attn_bo"])
    g["a1_bk"] = np.asarray(inp["L1_attn_bk"], np.float32).reshape(1, 1024)
    g["a1_bv"] = np.asarray(inp["L1_attn_bv"], np.float32).reshape(1, 1024)

    g["m1_W1"] = np.ascontiguousarray(np.asarray(inp["L1_ffn_W1"], np.float32))
    g["m1_W2"] = np.ascontiguousarray(np.asarray(inp["L1_ffn_W2"], np.float32))
    wr = np.asarray(inp["L1_ffn_Wr"], np.float32)
    g["m1_WrP"] = np.ascontiguousarray(
        wr.reshape(8, 128, 8).transpose(1, 0, 2).reshape(128, 64))
    g["m1_br"] = np.asarray(inp["L1_ffn_br"], np.float32).reshape(1, 8)
    b1 = np.asarray(inp["L1_ffn_b1"], np.float32)
    g["m1_b1pc"] = np.ascontiguousarray(np.concatenate(
        [b1[e].reshape(32, 128).T for e in range(8)], axis=1))
    b2 = np.asarray(inp["L1_ffn_b2"], np.float32)
    g["m1_b2pc"] = np.ascontiguousarray(np.concatenate(
        [b2[e].reshape(8, 128).T for e in range(8)], axis=1))

    g["a2_wqT"] = _t(inp["L2_attn_wq"])
    g["a2_wkT"] = _t(inp["L2_attn_wk"])
    g["a2_wvT"] = _t(inp["L2_attn_wv"])
    g["a2_eye64"] = np.eye(64, dtype=np.float32).astype(BFNP)
    a2woT = _t(inp["L2_attn_wo"])
    for c in range(NCORES):
        q = c % 4
        pc[c]["a2_woPT"] = np.ascontiguousarray(a2woT[256 * q:256 * (q + 1)])
    g["a2_bq"] = _pcol(inp["L2_attn_bq"])
    g["a2_bo"] = _pcol(inp["L2_attn_bo"])
    g["a2_bkc"] = _pcol(inp["L2_attn_bk"])
    g["a2_bv"] = np.asarray(inp["L2_attn_bv"], np.float32).reshape(1, 1024)
    c64 = np.asarray(inp["L2_attn_cos"], np.float32)[3199]
    s64 = np.asarray(inp["L2_attn_sin"], np.float32)[3199]
    ss = s64.copy()
    ss[0::2] *= -1.0
    g["a2_ropec"] = np.ascontiguousarray(np.tile(c64, 2)[:, None])
    g["a2_ropes"] = np.ascontiguousarray(np.tile(ss, 2)[:, None])

    g["m2_phi"] = np.ascontiguousarray(
        np.asarray(inp["L2_ffn_phi"], np.float32).reshape(1024, 2048))
    g["eye128b"] = np.eye(128, dtype=np.float32).astype(BFNP)
    w2f = np.asarray(inp["L2_ffn_W"], np.float32)
    b2f = np.asarray(inp["L2_ffn_b"], np.float32)
    for c in range(NCORES):
        q = c % 4
        pc[c]["m2_Wc"] = np.ascontiguousarray(w2f[16 * q:16 * (q + 1)])
        pc[c]["m2_bnc"] = np.ascontiguousarray(b2f[16 * q:16 * (q + 1)])
        onehot = np.zeros(8, np.float32)
        onehot[c ^ 4] = 1.0
        pc[c]["m2_maskc"] = np.ascontiguousarray(
            np.tile(onehot, (128, 1)))

    g["a3_wqkvT"] = _t(inp["L3_attn_w_qkv"])
    g["a3_woT"] = _t(inp["L3_attn_w_out"])
    g["a3_bout"] = _pcol(inp["L3_attn_b_out"])

    dk = np.outer(np.arange(1024), np.arange(1024)) % 1024
    ang = 2.0 * np.pi * dk.astype(np.float64) / 1024.0
    g["f3_C"] = np.cos(ang).astype(np.float32)
    g["f3_S"] = np.sin(ang).astype(np.float32)
    g["f3_WrT"] = _t(inp["L3_ffn_w_freq_real"])
    g["f3_WiT"] = _t(inp["L3_ffn_w_freq_imag"])
    g["f3_bias"] = _pcol(inp["L3_ffn_bias"])

    in_maps = []
    for c in range(NCORES):
        m = dict(g)
        m.update(pc[c])
        in_maps.append(m)
    return in_maps


def emit_ln(b, pref, xr, xl, after=(), mcw=512, dtiles=DT, dmodel=1024.0,
            wfree=W_, ab_out=None, write_xl=True):
    mc_n = wfree // mcw
    ones = b.sbuf(pref + "ones", [128, 1], F32)
    onesb = b.sbuf(pref + "onesb", [128, 1], BF16)
    onesr = b.sbuf(pref + "onesr", [1, 128], F32)
    x2 = b.sbuf(pref + "x2", [128, 2 * mcw], BF16)
    stat = b.sbuf(pref + "stat", [1, 3 * mcw], F32)
    rep = b.sbuf(pref + "rep", [128, 2 * mcw], F32) if write_xl else None
    ps_s = b.pool["e"][0:1, 0:mcw]
    ps_q = b.pool["f"][0:1, 0:mcw]
    ps_r = b.pool["r2"]
    sv = b.sem("v")
    ss = b.sem("s")
    st = b.sem("t")

    epsb = b.sbuf(pref + "epsb", [1, 1], F32)
    b.op("vector", lambda e: e.memset(ones[:, :], 1.0), sv)
    b.op("vector", lambda e: e.memset(onesb[:, :], 1.0), sv)
    t_eps = b.op("vector", lambda e: e.memset(epsb[:, :], 1e-5), sv)
    b.wait("scalar", t_eps)
    t_ones = b.op("vector", lambda e: e.memset(onesr[:, :], 1.0), sv)
    b.wait("tensor", t_ones)
    b.wait("scalar", later(*after))
    b.wait("tensor", later(*after))

    t_sq_read = None
    t_x2mm = [None, None]
    t_stat_read = None
    t_rep_read = None
    t_evac = None
    t_last = None
    for mc in range(mc_n):
        o = mc * mcw

        def xt(d, o=o):
            return xr[:, d * wfree + o:d * wfree + o + mcw]

        b.wait("tensor", t_stat_read)
        for d in range(dtiles):
            b.op("tensor", lambda e, d=d, xt=xt: e.matmul(
                ps_s, ones[:, :], xt(d),
                start=(d == 0), stop=(d == dtiles - 1)), None)
        for d in range(dtiles):
            p = d % 2
            b.wait("scalar", t_x2mm[p])
            t_sq = b.op("scalar", lambda e, d=d, p=p, xt=xt: e.activation(
                x2[:, p * mcw:(p + 1) * mcw], xt(d), AF.Square,
                bias=0.0, scale=1.0), ss)
            b.wait("tensor", t_sq)
            t_x2mm[p] = b.op("tensor", lambda e, d=d, p=p: e.matmul(
                ps_q, onesb[:, :], x2[:, p * mcw:(p + 1) * mcw],
                start=(d == 0), stop=(d == dtiles - 1)), st)
        t_sq_read = t_x2mm[(dtiles - 1) % 2]
        b.wait("vector", t_sq_read)
        b.op("vector", lambda e: e.tensor_scalar(
            stat[0:1, 0:mcw], ps_s, 1.0 / dmodel, None, ALU.mult), None)
        b.op("vector", lambda e: e.tensor_scalar(
            stat[0:1, mcw:2 * mcw], ps_q, 1.0 / dmodel, None, ALU.mult), None)
        b.op("vector", lambda e: e.tensor_tensor(
            stat[0:1, 2 * mcw:3 * mcw], stat[0:1, 0:mcw], stat[0:1, 0:mcw], ALU.mult), None)
        t_stat_read = b.op("vector", lambda e: e.tensor_tensor(
            stat[0:1, mcw:2 * mcw], stat[0:1, mcw:2 * mcw], stat[0:1, 2 * mcw:3 * mcw], ALU.subtract), sv)
        b.wait("scalar", t_stat_read)
        t_sqrt = b.op("scalar", lambda e: e.activation(
            stat[0:1, mcw:2 * mcw], stat[0:1, mcw:2 * mcw], AF.Sqrt, bias=epsb[0:1, 0:1],
            scale=1.0), ss)
        b.wait("vector", t_sqrt)
        b.op("vector", lambda e: e.reciprocal(stat[0:1, mcw:2 * mcw], stat[0:1, mcw:2 * mcw]),
             None)
        t_ab = b.op("vector", lambda e: e.scalar_tensor_tensor(
            stat[0:1, 0:mcw], stat[0:1, 0:mcw], -1.0, stat[0:1, mcw:2 * mcw],
            ALU.mult, ALU.mult), sv)
        if ab_out is not None:
            b.op("vector", lambda e, o=o: e.tensor_copy(
                ab_out[0:1, o:o + mcw], stat[0:1, mcw:2 * mcw]), None)
            t_ab = b.op("vector", lambda e, o=o: e.tensor_copy(
                ab_out[0:1, wfree + o:wfree + o + mcw], stat[0:1, 0:mcw]), sv)
        if not write_xl:
            t_stat_read = t_ab
            t_last = t_ab
            continue
        b.wait("tensor", later(t_ab, t_evac))
        b.op("tensor", lambda e: e.matmul(
            ps_r[:, 0:mcw], onesr[:, :], stat[0:1, mcw:2 * mcw],
            start=True, stop=True), None)
        t_repmm = b.op("tensor", lambda e: e.matmul(
            ps_r[:, mcw:2 * mcw], onesr[:, :], stat[0:1, 0:mcw],
            start=True, stop=True), st)
        b.wait("scalar", later(t_repmm, t_rep_read))
        b.op("scalar", lambda e: e.activation(
            rep[:, 0:mcw], ps_r[:, 0:mcw], AF.Copy, bias=0.0, scale=1.0),
            None)
        t_evac = b.op("scalar", lambda e: e.activation(
            rep[:, mcw:2 * mcw], ps_r[:, mcw:2 * mcw], AF.Copy,
            bias=0.0, scale=1.0), ss)
        b.wait("vector", t_evac)
        for d in range(dtiles):
            b.op("vector", lambda e, d=d, o=o, xt=xt: e.tensor_tensor(
                xl[:, d * wfree + o:d * wfree + o + mcw], xt(d),
                rep[:, 0:mcw], ALU.mult), None)
            t_last = b.op("vector", lambda e, d=d, o=o: e.tensor_tensor(
                xl[:, d * wfree + o:d * wfree + o + mcw],
                xl[:, d * wfree + o:d * wfree + o + mcw],
                rep[:, mcw:2 * mcw], ALU.add), sv)
        t_rep_read = t_last
    return t_last


class Lin:

    def __init__(self, b, pref, kt_n, psums, single_wb=False):
        self.b = b
        self.ws = b.sbuf(pref + "ws", [128, kt_n * 128], F32)
        self.n_wb = 1 if single_wb else 2
        self.wb = b.sbuf(pref + "wb", [128, self.n_wb * kt_n * 128], BF16)
        self.kt_max = kt_n
        self.psums = psums
        self.t_cast = None
        self.t_mm = [None, None]
        self.t_bank = [None] * len(psums)
        self.bank_i = 0
        self.oi = 0

    def otile(self, wdram, kt_n, ot, rhs_fn, evac_fn, ncol=512,
              rhs_ready=None, wait_extra=None, n_mc=1):
        b = self.b
        p = self.oi % self.n_wb
        sw = b.sem("w")
        sg = b.sem("g")
        st = b.sem("t")
        b.wait("sync", self.t_cast)
        t_w = b.dma(
            "sync",
            self.ws[:, 0:kt_n * 128].rearrange("p (k c) -> p k c", c=128),
            wdram[0:kt_n * 128, ot * 128:(ot + 1) * 128]
            .rearrange("(k p) c -> p k c", p=128),
            sw)
        b.wait("gpsimd", t_w)
        b.wait("gpsimd", self.t_mm[p])
        self.t_cast = b.op("gpsimd", lambda e, p=p, kt_n=kt_n: e.tensor_copy(
            self.wb[:, p * self.kt_max * 128:p * self.kt_max * 128 + kt_n * 128],
            self.ws[:, 0:kt_n * 128]), sg)
        b.wait("tensor", self.t_cast)
        if rhs_ready is not None:
            b.wait("tensor", rhs_ready)
        if wait_extra is not None:
            b.wait("tensor", wait_extra)
        t_ev = None
        for mc in range(n_mc):
            bk = self.bank_i % len(self.psums)
            ps = self.psums[bk]
            b.wait("tensor", self.t_bank[bk])
            t_mm = None
            for kt in range(kt_n):
                t_mm = b.op("tensor", lambda e, p=p, kt=kt, ps=ps, kt_n=kt_n,
                            mc=mc, ncol=ncol: e.matmul(
                    ps[:, 0:ncol],
                    self.wb[:, (p * self.kt_max + kt) * 128:
                            (p * self.kt_max + kt) * 128 + 128],
                    rhs_fn(kt, mc), start=(kt == 0), stop=(kt == kt_n - 1)),
                    st if kt == kt_n - 1 else None)
            self.t_mm[p] = t_mm
            t_ev = evac_fn(ps, t_mm, mc)
            self.t_bank[bk] = t_ev
            self.bank_i += 1
        self.oi += 1
        return t_ev


def emit_l0ffn(b, io, xr, xl, t_xl):
    sv = b.sem("v")
    ss = b.sem("s")
    scl = b.sbuf("f0scl", [128, 32 * 4 + 8], F32)
    sd = b.sem("d")
    t_scl = None
    for i, nm in enumerate(("f0_s1", "f0_s2", "f0_h1s", "f0_h2s")):
        t_scl = b.dma("gpsimd", scl[:, i * 32:(i + 1) * 32], io[nm], sd)
    t_scl = b.dma("gpsimd", scl[:, 128:136], io["f0_s3"], sd)
    G = b.sbuf("f0G", [128, 32 * 512], BF16)
    H = b.sbuf("f0H", [128, 32 * 512], BF16)
    ps = [b.pool["a"], b.pool["b"]]
    lin = Lin(b, "f0", 32, ps)
    b.wait("scalar", t_scl)
    b.wait("vector", t_scl)

    t_G_read = None
    t_H_read = None
    t_xr_tok = None
    for mc in range(4):
        mo = mc * 512

        def xlr(kt, mc, mo=mo):
            return xl[:, kt * W_ + mo:kt * W_ + mo + 512]

        b.wait("scalar", later(t_G_read, t_H_read))
        t_a = None
        for ot in range(32):
            def evac_a1(psb, tmm, mc_, ot=ot):
                b.wait("scalar", tmm)
                return b.op("scalar", lambda e, ot=ot, psb=psb: e.activation(
                    G[:, ot * 512:(ot + 1) * 512], psb[:, :], AF.Silu,
                    bias=0.0, scale=scl[:, ot:ot + 1]), ss)
            lin.otile(io["f0_w1T"], 8, ot, xlr, evac_a1,
                      rhs_ready=t_xl if (mc == 0 and ot == 0) else None)
        for ot in range(32):
            def evac_a2(psb, tmm, mc_, ot=ot):
                b.wait("scalar", tmm)
                return b.op("scalar", lambda e, ot=ot, psb=psb: e.activation(
                    H[:, ot * 512:(ot + 1) * 512], psb[:, :], AF.Copy,
                    bias=0.0, scale=scl[:, 32 + ot:32 + ot + 1]), ss)
            t_a = lin.otile(io["f0_w2T"], 8, ot, xlr, evac_a2)
        b.wait("vector", t_a)
        t_g = b.op("vector", lambda e: e.tensor_tensor(
            G[:, :], G[:, :], H[:, :], ALU.mult), sv)
        t_b = None
        for ot in range(32):
            def evac_b(psb, tmm, mc_, ot=ot):
                b.wait("scalar", tmm)
                return b.op("scalar", lambda e, ot=ot, psb=psb: e.activation(
                    H[:, ot * 512:(ot + 1) * 512], psb[:, :], AF.Gelu,
                    bias=0.0, scale=scl[:, 64 + ot:64 + ot + 1]), ss)
            t_b = lin.otile(io["f0_h1T"], 32, ot,
                            lambda kt, mc: G[:, kt * 512:kt * 512 + 512], evac_b,
                            rhs_ready=t_g if ot == 0 else None)
        t_c = None
        for ot in range(32):
            def evac_c(psb, tmm, mc_, ot=ot):
                b.wait("scalar", tmm)
                return b.op("scalar", lambda e, ot=ot, psb=psb: e.activation(
                    G[:, ot * 512:(ot + 1) * 512], psb[:, :], AF.Gelu,
                    bias=0.0, scale=scl[:, 96 + ot:96 + ot + 1]), ss)
            t_c = lin.otile(io["f0_h2T"], 32, ot,
                            lambda kt, mc: H[:, kt * 512:kt * 512 + 512], evac_c,
                            rhs_ready=t_b if ot == 0 else None)
            t_H_read = lin.t_mm[(lin.oi - 1) % 2]
        for ot in range(8):
            def evac_d(psb, tmm, mc_, ot=ot, mo=mo):
                b.wait("vector", tmm)
                return b.op("vector", lambda e, ot=ot, psb=psb, mo=mo:
                            e.scalar_tensor_tensor(
                    xr[:, ot * W_ + mo:ot * W_ + mo + 512], psb[:, :],
                    scl[:, 128 + ot:128 + ot + 1],
                    xr[:, ot * W_ + mo:ot * W_ + mo + 512],
                    ALU.mult, ALU.add), sv)
            t_xr_tok = lin.otile(io["f0_w3T"], 32, ot,
                                 lambda kt, mc: G[:, kt * 512:kt * 512 + 512],
                                 evac_d, rhs_ready=t_c if ot == 0 else None)
            t_G_read = lin.t_mm[(lin.oi - 1) % 2]
    return t_xr_tok


def emit_l1attn(b, io, xr, xl, t_xl, core_group=None):
    nc = b.nc
    sv = b.sem("v")
    ss = b.sem("s")
    st = b.sem("t")
    sg = b.sem("g")
    sw = b.sem("w")
    scc = b.sem("c")
    H, DH = 16, 64

    Q = b.sbuf("a1Q", [128, DT * W_], BF16)
    ATT = b.sbuf("a1ATT", [128, DT * 512], BF16)
    KW = b.sbuf("a1KW", [128, 8 * 512], BF16)
    VW = b.sbuf("a1VW", [128, 8 * 512], BF16)
    WS2 = b.sbuf("a1WS2", [128, 2048], F32)
    browf = WS2[0:1, 0:2048]
    k2 = b.sbuf("a1k2", [128, 2 * 512], BF16)
    v2 = b.sbuf("a1v2", [128, 2 * 512], BF16)
    U = b.sbuf("a1U", [128, 512], F32)
    E = b.sbuf("a1E", [128, 512], F32)
    KVB = b.sbuf("a1KVB", [128, H * 64 + H], F32)
    KVBb = b.sbuf("a1KVBb", [128, H * 64 + H], BF16)
    zrep = b.sbuf("a1zrep", [64, 512], F32)
    zr = b.sbuf("a1zr", [1, 512], F32)
    brow = b.sbuf("a1brow", [1, 2 * 1024], BF16)
    onesr = b.sbuf("a1onesr", [1, 128], BF16)
    onesrf = b.sbuf("a1onesrf", [1, 128], F32)
    onescb = b.sbuf("a1onescb", [128, 1], BF16)
    bocol = b.sbuf("a1bocol", [128, 8], F32)
    bqcol = b.sbuf("a1bqcol", [128, 8], F32)
    T1 = b.sbuf("a1T1", [128, 512], F32)
    ps_kv = b.pool["r2"][0:64, 0:H * 64]
    ps_ks = b.pool["e"][0:64, 0:H]
    ps_k = [b.pool["a"], b.pool["b"]]
    ps_v = [b.pool["c"], b.pool["d"]]
    ps_a = b.pool["c"][0:64, 0:512]
    ps_z = b.pool["e"][0:1, 0:512]
    ps_rz = b.pool["d"][0:64, 0:512]
    kv_in = nc.dram_tensor("a1kv_in", [64, H * 64 + H], F32, kind="Internal")
    kv_out = nc.dram_tensor("a1kv_out", [64, H * 64 + H], F32, kind="Internal")

    sd = b.sem("d")
    t_c = b.dma("gpsimd", browf[0:1, 0:1024], io["a1_bk"], sd)
    t_c = b.dma("gpsimd", browf[0:1, 1024:2048], io["a1_bv"], sd)
    t_c = b.dma("gpsimd", bocol[:, :], io["a1_bo"], sd)
    t_c = b.dma("gpsimd", bqcol[:, :], io["a1_bq"], sd)
    b.wait("vector", t_c)
    b.op("vector", lambda e: e.memset(onesr[:, :], 1.0), sv)
    b.op("vector", lambda e: e.memset(onesrf[:, :], 1.0), sv)
    b.op("vector", lambda e: e.memset(onescb[:, :], 1.0), sv)
    t_cst = b.op("vector", lambda e: e.tensor_copy(brow[:, :], browf[:, :]), sv)
    b.wait("tensor", t_cst)
    b.wait("scalar", t_c)

    ps_q = [b.pool["a"], b.pool["b"]]
    lin = Lin(b, "a1", 8, ps_q)
    ue_read = [None]

    def evac_q(psb, tmm, mc, ot):
        b.wait("scalar", tmm)
        b.wait("scalar", ue_read[0])
        b.op("scalar", lambda e: e.activation(
            U[:, :], psb[:, 0:512], AF.Identity, bias=bqcol[:, ot:ot + 1],
            scale=1.0), ss)
        tU = b.op("scalar", lambda e: e.activation(
            E[:, :], psb[:, 0:512], AF.Identity, bias=bqcol[:, ot:ot + 1],
            scale=1.0), ss)
        b.wait("vector", tU)
        tM = b.op("vector", lambda e: e.tensor_scalar(
            E[:, :], E[:, :], 0.0, None, ALU.min), sv)
        b.wait("scalar", tM)
        tE = b.op("scalar", lambda e: e.activation(
            E[:, :], E[:, :], AF.Exp, bias=0.0, scale=1.0), ss)
        b.wait("vector", tE)
        ue_read[0] = b.op("vector", lambda e, ot=ot, mc=mc: e.scalar_tensor_tensor(
            Q[:, ot * W_ + mc * 512:ot * W_ + mc * 512 + 512],
            U[:, :], 1.0, E[:, :], ALU.add, ALU.max), sv)
        return ue_read[0]

    for ot in range(8):
        lin.otile(io["a1_WqT"], 8, ot,
                  lambda kt, mc: xl[:, kt * W_ + mc * 512:kt * W_ + mc * 512 + 512],
                  lambda ps, tmm, mc, ot=ot: evac_q(ps, tmm, mc, ot),
                  rhs_ready=t_xl if ot == 0 else None, n_mc=4)

    b.barrier()
    t_kv_mm = None
    t_k2_ev = [None, None]
    t_v2_ev = [None, None]
    t_kvmt = [None, None]
    t_kv_groups = []
    for half in range(2):
        st_now = (st, st.n) if st.n > 0 else None
        b.wait("sync", st_now)
        b.wait("gpsimd", st_now)
        t_lw = None
        for wname, dst in (("a1_WkT", KW), ("a1_WvT", VW)):
            for sub in range(2):
                b.wait("sync", lin.t_cast)
                tw = None
                for kt in range(4):
                    tw = b.dma("sync", WS2[:, kt * 512:(kt + 1) * 512],
                               io[wname][(sub * 4 + kt) * 128:
                                         (sub * 4 + kt) * 128 + 128,
                                         half * 512:(half + 1) * 512], sw)
                b.wait("gpsimd", tw)
                lin.t_cast = b.op(
                    "gpsimd", lambda e, dst=dst, sub=sub: e.tensor_copy(
                        dst[:, sub * 4 * 512:(sub * 4 + 4) * 512],
                        WS2[:, :]), sg)
                t_lw = lin.t_cast
        b.wait("tensor", t_lw)
        for mt in range(16):
            p = mt % 2
            for (wbuf, psl, t_ev, browo) in ((KW, ps_k, t_k2_ev, 0),
                                             (VW, ps_v, t_v2_ev, 1024)):
                b.wait("tensor", t_ev[p])
                for d in range(DT):
                    b.op("tensor", lambda e, d=d, mt=mt, p=p, wbuf=wbuf,
                         psl=psl: e.matmul(
                        psl[p][:, :], xl[:, d * W_ + mt * 128:d * W_ + mt * 128 + 128],
                        wbuf[:, d * 512:(d + 1) * 512],
                        start=(d == 0), stop=False), None)
                b.op("tensor", lambda e, p=p, psl=psl, browo=browo, half=half:
                     e.matmul(
                    psl[p][:, :], onesr[0:1, :],
                    brow[0:1, browo + half * 512:browo + half * 512 + 512],
                    start=False, stop=True),
                    st)
            t_kmm = (st, st.n - 1)
            t_vmm = (st, st.n)
            b.wait("scalar", t_kmm)
            b.wait("scalar", ue_read[0])
            b.wait("scalar", t_kvmt[p])
            b.wait("vector", t_kvmt[p])
            b.op("scalar", lambda e, p=p: e.activation(
                U[:, :], ps_k[p][:, :], AF.Copy, bias=0.0, scale=1.0), ss)
            tU = b.op("scalar", lambda e, p=p: e.activation(
                E[:, :], ps_k[p][:, :], AF.Copy, bias=0.0, scale=1.0), ss)
            b.wait("vector", tU)
            tM = b.op("vector", lambda e: e.tensor_scalar(
                E[:, :], E[:, :], 0.0, None, ALU.min), sv)
            b.wait("scalar", tM)
            tE = b.op("scalar", lambda e: e.activation(
                E[:, :], E[:, :], AF.Exp, bias=0.0, scale=1.0), ss)
            b.wait("vector", tE)
            t_k2_ev[p] = b.op("vector", lambda e, p=p: e.scalar_tensor_tensor(
                k2[:, p * 512:(p + 1) * 512], U[:, :], 1.0, E[:, :],
                ALU.add, ALU.max), sv)
            ue_read[0] = t_k2_ev[p]
            b.wait("scalar", t_vmm)
            t_v2_ev[p] = b.op("scalar", lambda e, p=p: e.activation(
                v2[:, p * 512:(p + 1) * 512], ps_v[p][:, :], AF.Copy,
                bias=0.0, scale=1.0), ss)
            b.wait("tensor", later(t_k2_ev[p], t_v2_ev[p]))
            for h8 in range(8):
                h = half * 8 + h8
                b.op("tensor", lambda e, p=p, h8=h8, h=h, mt=mt: e.matmul(
                    ps_kv[:, h * 64:h * 64 + 64],
                    k2[:, p * 512 + h8 * 64:p * 512 + h8 * 64 + 64],
                    v2[:, p * 512 + h8 * 64:p * 512 + h8 * 64 + 64],
                    start=(mt == 0 and h8 == 0), stop=(mt == 15),
                    skip_group_check=True), None)
                t_kv_mm = b.op("tensor", lambda e, p=p, h8=h8, h=h, mt=mt,
                               half=half: e.matmul(
                    ps_ks[:, h:h + 1],
                    k2[:, p * 512 + h8 * 64:p * 512 + h8 * 64 + 64],
                    onescb[:, :],
                    start=(mt == 0 and h8 == 0 and half == 0),
                    stop=(mt == 15), skip_group_check=True),
                    st if h8 == 7 else None)
            t_kvmt[p] = t_kv_mm
        t_kv_groups.append(t_kv_mm)

    b.wait("scalar", later(*t_kv_groups))
    b.op("scalar", lambda e: e.activation(
        KVB[0:64, 0:H * 64], ps_kv[:, :], AF.Copy, bias=0.0, scale=1.0), ss)
    t_kvev = b.op("scalar", lambda e: e.activation(
        KVB[0:64, H * 64:H * 64 + H], ps_ks[:, :], AF.Copy,
        bias=0.0, scale=1.0), ss)
    b.wait("gpsimd", t_kvev)
    t_st = b.dma("gpsimd", kv_in.ap(), KVB[0:64, :], sd)
    b.wait("gpsimd", t_st)
    groups = [[0, 1, 2, 3], [4, 5, 6, 7]]
    t_ar = b.cc("AllReduce", ALU.add, groups,
                [kv_in.ap().opt()], [kv_out.ap().opt()], scc)
    b.wait("gpsimd", t_ar)
    b.dma("gpsimd", KVB[0:64, :], kv_out.ap(), sd)
    t_kvld = b.dma("gpsimd", KVB[64:128, :], kv_out.ap(), sd)
    b.wait("vector", t_kvld)
    t_kvld = b.op("vector", lambda e: e.tensor_copy(KVBb[:, :], KVB[:, :]), sv)

    b.barrier()
    t_psa_read = None
    t_att_free = None
    t_zrep_ev = None
    t_zr_read = None
    t_xr_tok = None
    b.wait("tensor", t_kvld)
    for sc in range(4):
        for h in range(H):
            base = 64 * (h % 2)
            dtile = h // 2
            qsl = lambda h=h, sc=sc, base=base, dtile=dtile: Q[
                base:base + 64, dtile * W_ + sc * 512:dtile * W_ + sc * 512 + 512]
            b.wait("tensor", t_psa_read)
            t_amm = b.op("tensor", lambda e, h=h, base=base, qsl=qsl: e.matmul(
                ps_a[:, :], KVBb[base:base + 64, h * 64:h * 64 + 64],
                qsl(), start=True, stop=True), st)
            b.wait("tensor", t_zr_read)
            t_zmm = b.op("tensor", lambda e, h=h, base=base, qsl=qsl: e.matmul(
                ps_z[:, :], KVBb[base:base + 64, H * 64 + h:H * 64 + h + 1],
                qsl(), start=True, stop=True), st)
            b.wait("vector", t_zmm)
            b.op("vector", lambda e: e.tensor_scalar(
                zr[:, :], ps_z[:, :], 1e-6, None, ALU.add), None)
            t_zr = b.op("vector", lambda e: e.reciprocal(zr[:, :], zr[:, :]),
                        sv)
            b.wait("tensor", later(t_zr, t_zrep_ev))
            t_zr_read = b.op("tensor", lambda e: e.matmul(
                ps_rz[:, :], onesrf[0:1, 0:64], zr[0:1, :],
                start=True, stop=True), st)
            b.wait("scalar", t_zr_read)
            b.wait("scalar", t_psa_read)
            t_zrep_ev = b.op("scalar", lambda e: e.activation(
                zrep[:, :], ps_rz[:, :], AF.Copy, bias=0.0, scale=1.0), ss)
            b.wait("vector", later(t_amm, t_zrep_ev))
            if h == 0:
                b.wait("vector", t_att_free)
            t_psa_read = b.op("vector", lambda e, h=h, base=base: e.tensor_tensor(
                ATT[base:base + 64, (h // 2) * 512:(h // 2) * 512 + 512],
                ps_a[:, :], zrep[:, :], ALU.mult), sv)
        def evac_o(psb, tmm, mc_, ot, sc=sc):
            b.wait("scalar", tmm)
            b.wait("scalar", ue_read[0])
            t1 = b.op("scalar", lambda e, ot=ot: e.activation(
                T1[:, :], psb[:, 0:512], AF.Identity, bias=bocol[:, ot:ot + 1],
                scale=1.0), ss)
            b.wait("vector", t1)
            ue_read[0] = b.op("vector", lambda e, ot=ot, sc=sc: e.tensor_tensor(
                xr[:, ot * W_ + sc * 512:ot * W_ + sc * 512 + 512],
                xr[:, ot * W_ + sc * 512:ot * W_ + sc * 512 + 512],
                T1[:, :], ALU.add), sv)
            return ue_read[0]

        for ot in range(8):
            t_xr_tok = lin.otile(
                io["a1_WoT"], 8, ot,
                lambda kt, mc: ATT[:, kt * 512:kt * 512 + 512],
                lambda ps, tmm, mc, ot=ot: evac_o(ps, tmm, mc, ot),
                rhs_ready=t_psa_read if ot == 0 else None, n_mc=1)
        t_att_free = lin.t_mm[(lin.oi - 1) % 2]
    return t_xr_tok


DEBUG_TAPS = []


def emit_l1ffn(b, io, xr, xl, t_ln):
    sv = b.sem("v")
    ss = b.sem("s")
    st = b.sem("t")
    sd = b.sem("d")
    E = 8

    LNAB = b.tiles["LNAB"]
    LT = b.sbuf("m1LT", [32, W_], F32)
    SH = b.sbuf("m1SH", [32, W_], F32)
    MK = b.sbuf("m1MK", [32, W_], BF16)
    OUT = b.sbuf("m1OUT", [128, DT * 512], F32)
    MX = OUT[0:32, 0:W_]
    AREP = b.sbuf("m1AREP", [128, 512], BF16)
    BREP = b.sbuf("m1BREP", [128, 512], BF16)
    XIT = b.sbuf("m1XIT", [128, DT * 512], BF16)
    HE = b.sbuf("m1HE", [128, 32 * 512], BF16)
    T1 = b.sbuf("m1T1", [128, 512], F32)
    rowm = b.sbuf("m1rowm", [1, 1024], F32)
    mrow = b.sbuf("m1mrow", [1, 512], BF16)
    onesr = b.sbuf("m1onesr", [1, 128], F32)
    ones512 = b.sbuf("m1ones512", [1, 512], F32)
    wsum = b.sbuf("m1wsum", [1, 8], F32)
    brr = b.sbuf("m1brr", [1, 8], F32)
    WRS = b.sbuf("m1WRS", [128, 64], F32)
    b1c = b.sbuf("m1b1c", [128, E * 32], F32)
    b2c = b.sbuf("m1b2c", [128, E * 8], F32)

    t_c = b.dma("gpsimd", WRS[:, :], io["m1_WrP"], sd)
    t_c = b.dma("gpsimd", brr[:, :], io["m1_br"], sd)
    t_c = b.dma("gpsimd", b1c[:, :], io["m1_b1pc"], sd)
    t_c = b.dma("gpsimd", b2c[:, :], io["m1_b2pc"], sd)
    b.wait("vector", t_c)
    b.op("vector", lambda e: e.memset(onesr[:, :], 1.0), sv)
    b.op("vector", lambda e: e.memset(ones512[:, :], 1.0), sv)
    b.op("vector", lambda e: e.memset(LT[:, :], 0.0), sv)
    t_z = b.op("vector", lambda e: e.memset(SH[:, :], 0.0), sv)

    ones_col = b.tiles["lnones"]
    ps_raw = b.pool["c"]
    ps_aff = b.pool["d"]
    ps_ar = b.pool["f"]
    ps_ws = b.pool["e"][0:1, 0:8]
    b.wait("tensor", later(t_ln, t_z))
    b.wait("scalar", t_c)
    t_ws = None
    for d in range(E):
        t_ws = b.op("tensor", lambda e, d=d: e.matmul(
            ps_ws, ones_col[:, :], WRS[:, d * 8:(d + 1) * 8],
            start=(d == 0), stop=(d == DT - 1)), st)
    b.wait("scalar", t_ws)
    t_wse = b.op("scalar", lambda e: e.activation(
        wsum[:, :], ps_ws, AF.Copy, bias=0.0, scale=1.0), ss)
    b.wait("tensor", t_wse)

    t_ps_free = [None, None, None]
    t_sh_free = None
    for mc in range(4):
        mo = mc * 512
        b.wait("tensor", later(*t_ps_free))
        for d in range(DT):
            b.op("tensor", lambda e, d=d, mo=mo: e.matmul(
                ps_raw[0:8, 0:512], WRS[:, d * 8:(d + 1) * 8],
                xr[:, d * W_ + mo:d * W_ + mo + 512],
                start=(d == 0), stop=(d == DT - 1)), None)
        t_raw = b.op("tensor", lambda e, mo=mo: e.matmul(
            ps_ar[0:8, 0:512], onesr[0:1, 0:8], LNAB[0:1, mo:mo + 512],
            start=True, stop=True), st)
        b.op("tensor", lambda e, mo=mo: e.matmul(
            ps_aff[0:8, 0:512], wsum[0:1, :],
            LNAB[0:1, W_ + mo:W_ + mo + 512], start=True, stop=False), None)
        t_aff = b.op("tensor", lambda e: e.matmul(
            ps_aff[0:8, 0:512], brr[0:1, :], ones512[0:1, :],
            start=False, stop=True), st)
        b.wait("scalar", later(t_raw, t_sh_free))
        t_e1 = b.op("scalar", lambda e, mo=mo: e.activation(
            SH[0:8, mo:mo + 512], ps_ar[0:8, 0:512], AF.Copy,
            bias=0.0, scale=1.0), ss)
        b.wait("vector", t_e1)
        t_l1 = b.op("vector", lambda e, mo=mo: e.tensor_tensor(
            LT[0:8, mo:mo + 512], ps_raw[0:8, 0:512], SH[0:8, mo:mo + 512],
            ALU.mult), sv)
        b.wait("scalar", later(t_aff, t_l1))
        t_e2 = b.op("scalar", lambda e, mo=mo: e.activation(
            SH[0:8, mo:mo + 512], ps_aff[0:8, 0:512], AF.Copy,
            bias=0.0, scale=1.0), ss)
        b.wait("vector", t_e2)
        t_l2 = b.op("vector", lambda e, mo=mo: e.tensor_tensor(
            LT[0:8, mo:mo + 512], LT[0:8, mo:mo + 512], SH[0:8, mo:mo + 512],
            ALU.add), sv)
        t_ps_free = [t_l1, t_l2, t_l1]
        t_sh_free = t_l2
    b.wait("vector", t_l2)
    b.op("vector", lambda e: e.tensor_copy(MX[:, :], LT[:, :]), None)
    for sw_ in (1, 2, 4):
        m32 = [(i ^ sw_) for i in range(32)]
        b.op("vector", lambda e, m32=m32: e.stream_shuffle(
            SH[:, :], MX[:, :], m32), None)
        b.op("vector", lambda e: e.tensor_tensor(
            MX[:, :], MX[:, :], SH[:, :], ALU.max), None)
    t_mk = b.op("vector", lambda e: e.tensor_tensor(
        MK[:, :], LT[:, :], MX[:, :], ALU.is_ge), sv)
    b.barrier()

    lin = Lin(b, "m1", 32, [b.pool["a"], b.pool["b"]], single_wb=True)
    ue = [None]
    t_s1mm = None
    t_s2mm = None
    t_xi = None
    t_xr_tok = None
    for mc in range(4):
        mo = mc * 512
        b.op("vector", lambda e: e.memset(OUT[:, :], 0.0), None)
        for ex in range(E):
            t_mr = b.dma("gpsimd", mrow[0:1, 0:512],
                         MK[ex:ex + 1, mo:mo + 512], sd)
            b.wait("vector", t_mr)
            b.op("vector", lambda e, mo=mo: e.tensor_tensor(
                rowm[0:1, 0:512], LNAB[0:1, mo:mo + 512], mrow[0:1, 0:512],
                ALU.mult), None)
            t_rw = b.op("vector", lambda e, mo=mo: e.tensor_tensor(
                rowm[0:1, 512:1024], LNAB[0:1, W_ + mo:W_ + mo + 512],
                mrow[0:1, 0:512], ALU.mult), sv)
            b.wait("tensor", t_rw)
            b.wait("tensor", t_xi)
            b.op("tensor", lambda e: e.matmul(
                b.pool["r2"][:, 0:512], onesr[:, :], rowm[0:1, 0:512],
                start=True, stop=True), None)
            t_rp = b.op("tensor", lambda e: e.matmul(
                b.pool["r2"][:, 512:1024], onesr[:, :], rowm[0:1, 512:1024],
                start=True, stop=True), st)
            b.wait("scalar", later(t_rp, t_xi))
            b.op("scalar", lambda e: e.activation(
                AREP[:, 0:512], b.pool["r2"][:, 0:512],
                AF.Copy, bias=0.0, scale=1.0), None)
            t_ev = b.op("scalar", lambda e: e.activation(
                BREP[:, 0:512], b.pool["r2"][:, 512:1024],
                AF.Copy, bias=0.0, scale=1.0), ss)
            b.wait("vector", later(t_ev, t_s1mm))
            for d in range(DT):
                b.op("vector", lambda e, d=d, mo=mo: e.tensor_tensor(
                    XIT[:, d * 512:(d + 1) * 512],
                    xr[:, d * W_ + mo:d * W_ + mo + 512],
                    AREP[:, 0:512], ALU.mult), None)
                t_xi = b.op("vector", lambda e, d=d: e.tensor_tensor(
                    XIT[:, d * 512:(d + 1) * 512],
                    XIT[:, d * 512:(d + 1) * 512],
                    BREP[:, 0:512], ALU.add), sv)
            b.wait("scalar", t_s2mm)
            t_b = None
            for ot in range(32):
                def ev1(psb, tmm, mc_, ot=ot, ex=ex):
                    b.wait("scalar", tmm)
                    return b.op("scalar", lambda e, ot=ot, ex=ex, psb=psb:
                                e.activation(
                        HE[:, ot * 512:(ot + 1) * 512], psb[:, 0:512],
                        AF.Relu, bias=b1c[:, ex * 32 + ot:ex * 32 + ot + 1],
                        scale=1.0), ss)
                t_b = lin.otile(io["m1_W1"][ex], 8, ot,
                                lambda kt, mc: XIT[:, kt * 512:kt * 512 + 512],
                                ev1, rhs_ready=t_xi if ot == 0 else None)
            t_s1mm = lin.t_mm[(lin.oi - 1) % lin.n_wb]
            for ot in range(DT):
                def ev2(psb, tmm, mc_, ot=ot, ex=ex):
                    b.wait("scalar", tmm)
                    b.wait("scalar", ue[0])
                    t1 = b.op("scalar", lambda e, ot=ot, ex=ex, psb=psb:
                              e.activation(
                        T1[:, :], psb[:, 0:512], AF.Identity,
                        bias=b2c[:, ex * 8 + ot:ex * 8 + ot + 1],
                        scale=1.0), ss)
                    b.wait("vector", t1)
                    ue[0] = b.op("vector", lambda e, ot=ot: e.tensor_tensor(
                        OUT[:, ot * 512:(ot + 1) * 512],
                        OUT[:, ot * 512:(ot + 1) * 512],
                        T1[:, :], ALU.add), sv)
                    return ue[0]
                lin.otile(io["m1_W2"][ex], 32, ot,
                          lambda kt, mc: HE[:, kt * 512:kt * 512 + 512],
                          ev2, rhs_ready=t_b if ot == 0 else None)
            t_s2mm = lin.t_mm[(lin.oi - 1) % lin.n_wb]
        for ot in range(DT):
            t_xr_tok = b.op("vector", lambda e, ot=ot, mo=mo: e.tensor_tensor(
                xr[:, ot * W_ + mo:ot * W_ + mo + 512],
                xr[:, ot * W_ + mo:ot * W_ + mo + 512],
                OUT[:, ot * 512:(ot + 1) * 512], ALU.add), sv)
    return t_xr_tok



def emit_l3ffn(b, io, xr, xl, t_xl):
    sv = b.sem("v")
    ss = b.sem("s")
    st = b.sem("t")
    sd = b.sem("d")
    sw = b.sem("w")
    sg = b.sem("g")

    CF = b.sbuf("f3CF", [128, DT * 1024], BF16)
    SF = b.sbuf("f3SF", [128, DT * 1024], BF16)
    TrR = b.sbuf("f3TrR", [128, DT * 1024], BF16)
    TiR = b.sbuf("f3TiR", [128, DT * 1024], BF16)
    MR = b.sbuf("f3MR", [128, DT * 1024], BF16)
    WW = b.sbuf("f3WW", [128, 3072], BF16)
    WSF = b.sbuf("f3WSF", [128, 1024], F32)
    T1 = b.sbuf("f3T1", [128, 512], F32)
    biasc = b.sbuf("f3biasc", [128, 8], F32)
    sc1024 = b.sbuf("f3sc", [128, 1], F32)

    t_c = b.dma("gpsimd", biasc[:, :], io["f3_bias"], sd)
    b.wait("vector", t_c)
    t_sc = b.op("vector", lambda e: e.memset(sc1024[:, :], 1.0 / 1024.0), sv)
    b.wait("scalar", t_sc)

    t_cast = None
    for nm, dst, neg in (("f3_C", CF, False), ("f3_S", SF, False)):
        for j in range(DT):
            b.wait("sync", t_cast)
            t_w = b.dma("sync", WSF[:, :],
                        io[nm][j * 128:(j + 1) * 128, :], sw)
            b.wait("gpsimd", t_w)
            if neg:
                t_cast = b.op("gpsimd", lambda e, dst=dst, j=j: e.tensor_scalar(
                    dst[:, j * 1024:(j + 1) * 1024], WSF[:, :], -1.0, None,
                    ALU.mult), sg)
            else:
                t_cast = b.op("gpsimd", lambda e, dst=dst, j=j: e.tensor_copy(
                    dst[:, j * 1024:(j + 1) * 1024], WSF[:, :]), sg)
    b.wait("tensor", t_cast)
    b.wait("tensor", t_xl)

    ps = [b.pool["a"], b.pool["b"]]
    t_bank = [None, None]
    bank_i = [0]
    t_ww_cast = t_cast
    t_ww_read = None

    def phase_T():
        nonlocal t_ww_cast, t_ww_read
        for kt in range(DT):
            t_w = None
            for wi, wd in enumerate(("f3_WrT", "f3_WiT")):
                b.wait("sync", t_ww_cast)
                t_w = b.dma("sync",
                            WSF[:, :].rearrange("p (k c) -> p k c", c=128),
                            io[wd][:, kt * 128:(kt + 1) * 128]
                            .rearrange("(k p) c -> p k c", p=128), sw)
                b.wait("gpsimd", t_w)
                b.wait("gpsimd", t_ww_read)
                t_ww_cast = b.op("gpsimd", lambda e, wi=wi: e.tensor_copy(
                    WW[:, wi * 1024:(wi + 1) * 1024], WSF[:, :]), sg)
                if wi == 1:
                    t_ww_cast = b.op("gpsimd", lambda e: e.tensor_scalar(
                        WW[:, 2048:3072], WSF[:, :], -1.0, None, ALU.mult), sg)
            b.wait("tensor", t_ww_cast)
            for comp, (la, ra, lb, rb, dst) in enumerate(
                    ((0, CF, 2, SF, TrR), (0, SF, 1, CF, TiR))):
                for oc in range(2):
                    bk = bank_i[0] % 2
                    psb = ps[bk]
                    b.wait("tensor", t_bank[bk])
                    t_mm = None
                    for kp in range(DT):
                        b.op("tensor", lambda e, kp=kp, oc=oc, psb=psb, ra=ra,
                             la=la: e.matmul(
                            psb[:, 0:512],
                            WW[:, la * 1024 + kp * 128:la * 1024 + kp * 128 + 128],
                            ra[:, kp * 1024 + oc * 512:kp * 1024 + oc * 512 + 512],
                            start=(kp == 0), stop=False), None)
                        t_mm = b.op("tensor", lambda e, kp=kp, oc=oc, psb=psb,
                                    rb=rb, lb=lb: e.matmul(
                            psb[:, 0:512],
                            WW[:, lb * 1024 + kp * 128:lb * 1024 + kp * 128 + 128],
                            rb[:, kp * 1024 + oc * 512:kp * 1024 + oc * 512 + 512],
                            start=False, stop=(kp == DT - 1)),
                            st if kp == DT - 1 else None)
                    t_ww_read = t_mm
                    b.wait("scalar", t_mm)
                    t_ev = b.op("scalar", lambda e, kt=kt, oc=oc, psb=psb,
                                dst=dst: e.activation(
                        dst[:, kt * 1024 + oc * 512:kt * 1024 + oc * 512 + 512],
                        psb[:, 0:512], AF.Identity, bias=0.0,
                        scale=sc1024[:, 0:1]), ss)
                    t_bank[bk] = t_ev
                    bank_i[0] += 1

    def phase_M():
        for dd in range(DT):
            for oc in range(2):
                bk = bank_i[0] % 2
                psb = ps[bk]
                b.wait("tensor", t_bank[bk])
                t_mm = None
                for kk in range(DT):
                    b.op("tensor", lambda e, kk=kk, dd=dd, oc=oc, psb=psb:
                         e.matmul(
                        psb[:, 0:512], CF[:, kk * 1024 + dd * 128:kk * 1024 + dd * 128 + 128],
                        TrR[:, kk * 1024 + oc * 512:kk * 1024 + oc * 512 + 512],
                        start=(kk == 0), stop=False), None)
                    t_mm = b.op("tensor", lambda e, kk=kk, dd=dd, oc=oc,
                                psb=psb: e.matmul(
                        psb[:, 0:512], SF[:, kk * 1024 + dd * 128:kk * 1024 + dd * 128 + 128],
                        TiR[:, kk * 1024 + oc * 512:kk * 1024 + oc * 512 + 512],
                        start=False, stop=(kk == DT - 1)),
                        st if kk == DT - 1 else None)
                b.wait("scalar", t_mm)
                t_ev = b.op("scalar", lambda e, dd=dd, oc=oc, psb=psb:
                            e.activation(
                    MR[:, dd * 1024 + oc * 512:dd * 1024 + oc * 512 + 512],
                    psb[:, 0:512], AF.Copy, bias=0.0, scale=1.0), ss)
                t_bank[bk] = t_ev
                bank_i[0] += 1

    def phase_Y():
        ue = [None]
        t_last = None
        for ot in range(DT):
            for mc in range(4):
                mo = mc * 512
                bk = bank_i[0] % 2
                psb = ps[bk]
                b.wait("tensor", t_bank[bk])
                t_mm = None
                for dd in range(DT):
                    t_mm = b.op("tensor", lambda e, dd=dd, ot=ot, mo=mo,
                                psb=psb: e.matmul(
                        psb[:, 0:512],
                        MR[:, dd * 1024 + ot * 128:dd * 1024 + ot * 128 + 128],
                        xl[:, dd * W_ + mo:dd * W_ + mo + 512],
                        start=(dd == 0), stop=(dd == DT - 1)),
                        st if dd == DT - 1 else None)
                b.wait("scalar", t_mm)
                b.wait("scalar", ue[0])
                t1 = b.op("scalar", lambda e, ot=ot, psb=psb: e.activation(
                    T1[:, :], psb[:, 0:512], AF.Identity,
                    bias=biasc[:, ot:ot + 1], scale=1.0), ss)
                b.wait("vector", t1)
                ue[0] = b.op("vector", lambda e, ot=ot, mo=mo: e.tensor_tensor(
                    xr[:, ot * W_ + mo:ot * W_ + mo + 512],
                    xr[:, ot * W_ + mo:ot * W_ + mo + 512],
                    T1[:, :], ALU.add), sv)
                t_bank[bk] = ue[0]
                bank_i[0] += 1
                t_last = ue[0]
        return t_last

    phase_T()
    b.barrier()
    phase_M()
    b.barrier()
    out = phase_Y()
    return out


def emit_l2attn(b, io, xr, xl, t_xl):
    nc = b.nc
    sv = b.sem("v")
    ss_ = b.sem("s")
    st = b.sem("t")
    sd = b.sem("d")
    sw = b.sem("w")
    sg = b.sem("g")
    scc = b.sem("c")

    OM = b.tiles["XL"]
    QB = b.sbuf("a2QB", [128, DT * W_], BF16)
    KB = b.sbuf("a2KB", [128, DT * W_], BF16)
    VHB = b.sbuf("a2VHB", [128, 4096], BF16)
    WVh = VHB
    QD = b.sbuf("a2QD", [128, 512], BF16)
    PT = b.sbuf("a2PT", [128, 2 * 512], BF16)
    QF = b.sbuf("a2QF", [128, 512], F32)
    T1 = b.sbuf("a2T1", [128, 512], F32)
    RT = T1
    zrep = b.sbuf("a2zrep", [64, 512], F32)
    rz = b.sbuf("a2rz", [1, 512], F32)
    WOPB = b.sbuf("a2WOPB", [128, 2 * 1024], BF16)
    IB = b.sbuf("a2IB", [128, 64], BF16)
    brow = b.sbuf("a2brow", [1, 1024], BF16)
    bqc = b.sbuf("a2bqc", [128, 8], F32)
    bkc = b.sbuf("a2bkc", [128, 8], F32)
    boc = b.sbuf("a2boc", [128, 8], F32)
    ropec = b.sbuf("a2ropec", [128, 2], F32)
    ropes = b.sbuf("a2ropes", [128, 2], F32)
    neg20 = b.sbuf("a2neg20", [128, 1], F32)
    onesr = b.sbuf("a2onesr", [1, 128], BF16)
    onesrf = b.sbuf("a2onesrf", [1, 128], F32)
    onescb = b.sbuf("a2onescb", [128, 1], BF16)
    vloc = nc.dram_tensor("a2vloc", [MTOK, 1024], F32, kind="Internal").ap()
    rs_in = nc.dram_tensor("a2rs_in", [4096, MTOK], F32, kind="Internal")
    rs_out = nc.dram_tensor("a2rs_out", [1024, MTOK], F32, kind="Internal")

    t_c = b.dma("gpsimd", bqc[:, :], io["a2_bq"], sd)
    t_c = b.dma("gpsimd", bkc[:, :], io["a2_bkc"], sd)
    t_c = b.dma("gpsimd", boc[:, :], io["a2_bo"], sd)
    t_c = b.dma("gpsimd", ropec[:, 0:1], io["a2_ropec"], sd)
    t_c = b.dma("gpsimd", ropes[:, 0:1], io["a2_ropes"], sd)
    t_c = b.dma("gpsimd", IB[0:64, :], io["a2_eye64"], sd)
    t_c = b.dma("gpsimd", IB[64:128, :], io["a2_eye64"], sd)
    t_c = b.dma("gpsimd", T1[0:1, 0:512], io["a2_bv"][0:1, 0:512], sd)
    t_c = b.dma("gpsimd", QF[0:1, 0:512], io["a2_bv"][0:1, 512:1024], sd)
    b.wait("vector", t_c)
    b.op("vector", lambda e: e.memset(neg20[:, :], -20.0), sv)
    b.op("vector", lambda e: e.memset(onesr[:, :], 1.0), sv)
    b.op("vector", lambda e: e.memset(onesrf[:, :], 1.0), sv)
    b.op("vector", lambda e: e.memset(onescb[:, :], 1.0), sv)
    b.op("vector", lambda e: e.tensor_scalar(
        ropec[:, 1:2], ropec[:, 0:1], 0.125, None, ALU.mult), sv)
    b.op("vector", lambda e: e.tensor_scalar(
        ropes[:, 1:2], ropes[:, 0:1], 0.125, None, ALU.mult), sv)
    t_br = b.op("vector", lambda e: e.tensor_copy(
        brow[0:1, 0:512], T1[0:1, 0:512]), sv)
    t_br = b.op("vector", lambda e: e.tensor_copy(
        brow[0:1, 512:1024], QF[0:1, 0:512]), sv)
    b.wait("tensor", t_br)
    b.wait("scalar", t_c)

    lin = Lin(b, "a2", 8, [b.pool["e"], b.pool["f"]])
    qf_free = [None]
    qf_free[0] = t_br

    def rope_evac(psb, tmm, biascol, dst_fn, ci):
        b.wait("scalar", tmm)
        b.wait("scalar", qf_free[0])
        t1 = b.op("scalar", lambda e, psb=psb, biascol=biascol: e.activation(
            QF[:, :], psb[:, 0:512], AF.Identity, bias=biascol, scale=1.0),
            ss_)
        b.wait("vector", t1)
        b.op("vector", lambda e: e.stream_shuffle(
            RT[:, :], QF[:, :], [i ^ 1 for i in range(32)]), None)
        b.op("vector", lambda e, ci=ci: e.tensor_scalar(
            RT[:, :], RT[:, :], ropes[:, ci:ci + 1], None, ALU.mult), None)
        out = b.op("vector", lambda e, ci=ci, dst_fn=dst_fn:
                   e.scalar_tensor_tensor(
            dst_fn(), QF[:, :], ropec[:, ci:ci + 1], RT[:, :],
            ALU.mult, ALU.add), sv)
        qf_free[0] = out
        return out

    for ot in range(DT):
        def evq(psb, tmm, mc, ot=ot):
            return rope_evac(
                psb, tmm, bqc[:, ot:ot + 1],
                lambda ot=ot, mc=mc: QB[:, ot * W_ + mc * 512:
                                        ot * W_ + mc * 512 + 512], 1)
        lin.otile(io["a2_wqT"], 8, ot,
                  lambda kt, mc: xl[:, kt * W_ + mc * 512:kt * W_ + mc * 512 + 512],
                  evq, rhs_ready=t_xl if ot == 0 else None, n_mc=4)
    for ot in range(DT):
        def evk(psb, tmm, mc, ot=ot):
            return rope_evac(
                psb, tmm, bkc[:, ot:ot + 1],
                lambda ot=ot, mc=mc: KB[:, ot * W_ + mc * 512:
                                        ot * W_ + mc * 512 + 512], 0)
        lin.otile(io["a2_wkT"], 8, ot,
                  lambda kt, mc: xl[:, kt * W_ + mc * 512:kt * W_ + mc * 512 + 512],
                  evk, n_mc=4)
    for oc in range(2):
        t_wv = None
        for j in range(DT):
            b.wait("sync", later(t_wv, lin.t_cast, qf_free[0]))
            t_w = b.dma("sync", QF[:, :],
                        io["a2_wvT"][j * 128:(j + 1) * 128,
                                     oc * 512:(oc + 1) * 512], sw)
            b.wait("gpsimd", t_w)
            t_wv = b.op("gpsimd", lambda e, j=j: e.tensor_copy(
                WVh[:, j * 512:(j + 1) * 512], QF[:, :]), sg)
        b.wait("tensor", t_wv)
        qf_free[0] = t_wv
        for mt in range(16):
            bk_ = lin.bank_i % 2
            psb = lin.psums[bk_]
            b.wait("tensor", lin.t_bank[bk_])
            for d in range(DT):
                b.op("tensor", lambda e, d=d, mt=mt, psb=psb: e.matmul(
                    psb[:, 0:512],
                    xl[:, d * W_ + mt * 128:d * W_ + mt * 128 + 128],
                    WVh[:, d * 512:(d + 1) * 512],
                    start=(d == 0), stop=False), None)
            t_mm = b.op("tensor", lambda e, oc=oc, psb=psb: e.matmul(
                psb[:, 0:512], onesr[0:1, :],
                brow[0:1, oc * 512:oc * 512 + 512],
                start=False, stop=True), st)
            b.wait("scalar", later(t_mm, qf_free[0]))
            t1 = b.op("scalar", lambda e, psb=psb: e.activation(
                QF[:, :], psb[:, 0:512], AF.Copy, bias=0.0, scale=1.0), ss_)
            b.wait("gpsimd", t1)
            t_vev = b.dma("gpsimd", vloc[mt * 128:(mt + 1) * 128,
                                         oc * 512:oc * 512 + 512],
                          QF[:, :], sd)
            qf_free[0] = t_vev
            lin.t_bank[bk_] = t1
            lin.bank_i += 1

    b.barrier()

    ps_s = [b.pool["a"], b.pool["b"]]
    ps_o = b.pool["c"]
    ps_z = b.pool["d"][0:1, 0:512]
    ps_qd = b.pool["d"]
    ps_r = b.pool["r2"][0:64, 0:512]
    t_vcast = None
    t_vread = None
    t_exp = [None, None]
    t_pt_read = [None, None]
    t_om_tt = None
    t_zrep_read = None
    t_qd_ev = None
    t_kc = None
    for hh in range(4):
        b.wait("sync", t_vread)
        for qtr in range(8):
            b.wait("sync", t_vcast)
            t_w = None
            for f_ in range(2):
                t_w = b.dma("sync",
                            QF[:, f_ * 256:(f_ + 1) * 256]
                            .rearrange("p (tb d) -> p tb d", d=64),
                            vloc[512 * hh:512 * hh + 512,
                                 qtr * 128 + f_ * 64:qtr * 128 + f_ * 64 + 64]
                            .rearrange("(tb t) d -> t tb d", t=128), sw)
            b.wait("gpsimd", t_w)
            t_vcast = b.op("gpsimd", lambda e, qtr=qtr: e.tensor_copy(
                VHB[:, qtr * 512:(qtr + 1) * 512], QF[:, :]), sg)
        b.wait("tensor", t_vcast)
        for uc in range(16):
            fb = uc
            qb_sl = QB[64 * (fb % 2):64 * (fb % 2) + 64,
                       (fb // 2) * W_ + 512 * hh:(fb // 2) * W_ + 512 * hh + 512]
            b.wait("vector", t_qd_ev)
            b.wait("vector", t_kc)
            t_cp = b.op("vector", lambda e, fb=fb, qb_sl=qb_sl: e.tensor_copy(
                QD[64 * (fb % 2):64 * (fb % 2) + 64, :], qb_sl), sv)
            b.wait("tensor", later(t_cp, t_om_tt, t_zrep_read))
            t_mv = b.op("tensor", lambda e, fb=fb, qb_sl=qb_sl: e.matmul(
                ps_qd[64 * ((fb + 1) % 2):64 * ((fb + 1) % 2) + 64, 0:512],
                IB[64 * (fb % 2):64 * (fb % 2) + 64, :], qb_sl,
                start=True, stop=True), st)
            b.wait("scalar", t_mv)
            t_qd_ev = b.op("scalar", lambda e, fb=fb: e.activation(
                QD[64 * ((fb + 1) % 2):64 * ((fb + 1) % 2) + 64, :],
                ps_qd[64 * ((fb + 1) % 2):64 * ((fb + 1) % 2) + 64, 0:512],
                AF.Copy, bias=0.0, scale=1.0), ss_)
            b.wait("tensor", t_qd_ev)
            for kc in range(64):
                fbk, tb = kc // 4, kc % 4
                kbase = 64 * (fbk % 2)
                p = kc % 2
                b.wait("tensor", t_exp[p])
                t_s = b.op("tensor", lambda e, kc=kc, p=p, fbk=fbk, tb=tb,
                           kbase=kbase, hh=hh: e.matmul(
                    ps_s[p][:, 0:512],
                    KB[kbase:kbase + 64,
                       (fbk // 2) * W_ + 512 * hh + tb * 128:
                       (fbk // 2) * W_ + 512 * hh + tb * 128 + 128],
                    QD[kbase:kbase + 64, :], start=True, stop=True), st)
                b.wait("scalar", later(t_s, t_pt_read[p]))
                t_exp[p] = b.op("scalar", lambda e, p=p: e.activation(
                    PT[:, p * 512:(p + 1) * 512], ps_s[p][:, 0:512],
                    AF.Exp, bias=neg20[:, 0:1], scale=1.0), ss_)
                b.wait("tensor", t_exp[p])
                b.op("tensor", lambda e, kc=kc, p=p: e.matmul(
                    ps_o[0:64, 0:512], VHB[:, kc * 64:kc * 64 + 64],
                    PT[:, p * 512:(p + 1) * 512],
                    start=(kc == 0), stop=(kc == 63),
                    skip_group_check=True), None)
                t_kc = b.op("tensor", lambda e, kc=kc, p=p: e.matmul(
                    ps_z[:, :], onescb[:, :], PT[:, p * 512:(p + 1) * 512],
                    start=(kc == 0), stop=(kc == 63),
                    skip_group_check=True), st)
                t_pt_read[p] = t_kc
            b.wait("vector", t_kc)
            t_rz = b.op("vector", lambda e: e.reciprocal(
                rz[:, :], ps_z[:, :]), sv)
            b.wait("tensor", later(t_rz, t_zrep_read))
            t_zmm = b.op("tensor", lambda e: e.matmul(
                ps_r[:, :], onesrf[0:1, 0:64], rz[0:1, :],
                start=True, stop=True), st)
            b.wait("scalar", later(t_zmm, t_om_tt))
            t_ze = b.op("scalar", lambda e: e.activation(
                zrep[:, :], ps_r[:, :], AF.Copy, bias=0.0, scale=1.0), ss_)
            t_zrep_read = t_zmm
            b.wait("vector", t_ze)
            base = 64 * (hh % 2)
            t_om_tt = b.op("vector", lambda e, hh=hh, fb=fb, base=base:
                           e.tensor_tensor(
                OM[base:base + 64,
                   (hh // 2) * 8192 + fb:(hh // 2) * 8192 + fb + 8177:16],
                ps_o[0:64, 0:512], zrep[:, :], ALU.mult), sv)
        t_vread = t_kc

    b.barrier()
    t_wc = None
    for wr in range(4):
        b.wait("sync", t_wc)
        t_w = b.dma("sync", QF[:, :],
                    io["a2_woPT"][(wr // 2) * 128:(wr // 2) * 128 + 128,
                                  (wr % 2) * 512:(wr % 2) * 512 + 512], sw)
        b.wait("gpsimd", t_w)
        t_wc = b.op("gpsimd", lambda e, wr=wr: e.tensor_copy(
            WOPB[:, wr * 512:(wr + 1) * 512], QF[:, :]), sg)
    b.wait("tensor", t_wc)
    b.barrier()
    psp = [b.pool["e"], b.pool["f"]]
    t_pb = [None, None]
    t_ev = None
    for tc in range(16):
        for et in range(DT):
            bk_ = (tc * DT + et) % 2
            psb = psp[bk_]
            b.wait("tensor", t_pb[bk_])
            t_mm = None
            for ft in range(2):
                t_mm = b.op("tensor", lambda e, ft=ft, et=et, tc=tc, psb=psb:
                            e.matmul(
                    psb[:, 0:512],
                    WOPB[:, ft * 1024 + et * 128:ft * 1024 + et * 128 + 128],
                    OM[:, ft * 8192 + tc * 512:ft * 8192 + tc * 512 + 512],
                    start=(ft == 0), stop=(ft == 1)), st)
            b.wait("scalar", later(t_mm, t_ev))
            t1 = b.op("scalar", lambda e, psb=psb: e.activation(
                QF[:, :], psb[:, 0:512], AF.Copy, bias=0.0, scale=1.0), ss_)
            b.wait("gpsimd", t1)
            t_ev = b.dma("gpsimd",
                         rs_in.ap()[(tc // 4) * 1024 + et * 128:
                                    (tc // 4) * 1024 + et * 128 + 128,
                                    (tc % 4) * 512:(tc % 4) * 512 + 512],
                         QF[:, :], sd)
            t_pb[bk_] = t1
    b.barrier(["gpsimd"])
    groups = [[0, 1, 2, 3], [4, 5, 6, 7]]
    t_rs = b.cc("ReduceScatter", ALU.add, groups, [rs_in.ap().opt()],
                [rs_out.ap().opt()], scc)
    b.wait("sync", t_rs)
    ue = [None]
    t_stg = None
    t_xr_tok = None
    for et in range(DT):
        for c4 in range(4):
            b.wait("sync", later(ue[0], t_stg))
            t_stg = b.dma("sync", QF[:, :],
                          rs_out.ap()[et * 128:(et + 1) * 128,
                                      c4 * 512:c4 * 512 + 512], sw)
            b.wait("scalar", later(t_stg, ue[0]))
            t1 = b.op("scalar", lambda e, et=et: e.activation(
                T1[:, :], QF[:, :], AF.Identity, bias=boc[:, et:et + 1],
                scale=1.0), ss_)
            b.wait("vector", t1)
            ue[0] = b.op("vector", lambda e, et=et, c4=c4: e.tensor_tensor(
                xr[:, et * W_ + c4 * 512:et * W_ + c4 * 512 + 512],
                xr[:, et * W_ + c4 * 512:et * W_ + c4 * 512 + 512],
                T1[:, :], ALU.add), sv)
            t_xr_tok = ue[0]
    return t_xr_tok


def emit_l2ffn(b, io, xr, xl, t_xl):
    nc = b.nc
    sv = b.sem("v")
    ss_ = b.sem("s")
    st = b.sem("t")
    sd = b.sem("d")
    sw = b.sem("w")
    sg = b.sem("g")
    scc = b.sem("c")

    PHIB = b.sbuf("m2PHIB", [128, DT * 2048], BF16)
    XTM = b.sbuf("m2XTM", [128, 16 * 1024], BF16)
    C2 = b.tiles["lnrep"]
    LP = b.sbuf("m2LP", [128, 512], F32)
    ML = b.sbuf("m2ML", [128, 512], F32)
    M2 = b.sbuf("m2M2", [128, 512], F32)
    E0 = b.sbuf("m2E0", [128, 512], F32)
    DTI = b.sbuf("m2DTI", [128, 512], BF16)
    DB = b.sbuf("m2DB", [128, 2 * 128], BF16)
    SLB = b.sbuf("m2SLB", [32, 1024], BF16)
    SLT = b.sbuf("m2SLT", [128, 256], BF16)
    YBT = b.sbuf("m2YBT", [128, 2 * 128], BF16)
    CTT = b.sbuf("m2CTT", [128, 2 * 512], BF16)
    zrep = b.sbuf("m2zrep", [128, 512], F32)
    zrow = b.sbuf("m2zrow", [1, 2048], F32)
    bnrow = b.sbuf("m2bnrow", [1, 1024], BF16)
    eye = b.sbuf("m2eye", [128, 128], BF16)
    maskc = b.sbuf("m2maskc", [128, 8], F32)
    b0t = b.sbuf("m2b0t", [128, 1], F32)
    ph2 = zrow
    onesr = b.sbuf("m2onesr", [1, 128], BF16)
    onesrf = b.sbuf("m2onesrf", [1, 128], F32)
    onescbf = b.sbuf("m2onescbf", [128, 1], BF16)
    QF = b.sbuf("m2QF", [128, 512], F32)
    T1 = M2
    lloc = nc.dram_tensor("m2lloc", [MTOK, 2048], F32, kind="Internal")
    lag_out = nc.dram_tensor("m2lag", [8 * MTOK, 2048], F32, kind="Internal")
    dispd = nc.dram_tensor("m2dispd", [MTOK, 2048], BF16, kind="Internal").ap()
    srs_in = nc.dram_tensor("m2srs_in", [2048, 1024], F32, kind="Internal")
    srs_out = nc.dram_tensor("m2srs_out", [512, 1024], F32, kind="Internal")
    yag_in = nc.dram_tensor("m2yag_in", [512, 1024], F32, kind="Internal")
    yag_out = nc.dram_tensor("m2yag_out", [2048, 1024], F32, kind="Internal")
    ctd = nc.dram_tensor("m2ctd", [2048, MTOK], BF16, kind="Internal").ap()

    t_c = b.dma("gpsimd", eye[:, :], io["eye128b"], sd)
    t_c = b.dma("gpsimd", maskc[:, :], io["m2_maskc"], sd)
    b.wait("vector", t_c)
    b.op("vector", lambda e: e.memset(onesr[:, :], 1.0), sv)
    b.op("vector", lambda e: e.memset(onesrf[:, :], 1.0), sv)
    t_o = b.op("vector", lambda e: e.memset(onescbf[:, :], 1.0), sv)

    ps_a = b.pool["a"]
    ps_b = b.pool["b"]
    ps_p2 = b.pool["e"][0:1, 0:512]
    t_cast = None
    t_ph = None
    for j in range(DT):
        for cq in range(4):
            b.wait("sync", t_cast)
            t_w = b.dma("sync", QF[:, :],
                        io["m2_phi"][j * 128:(j + 1) * 128,
                                     cq * 512:(cq + 1) * 512], sw)
            b.wait("gpsimd", t_w)
            t_cast = b.op("gpsimd", lambda e, j=j, cq=cq: e.tensor_copy(
                PHIB[:, j * 2048 + cq * 512:j * 2048 + cq * 512 + 512],
                QF[:, :]), sg)
    b.wait("tensor", later(t_cast, t_o, t_xl))
    RM = b.sbuf("m2RM", [128, 16], F32)
    RMB = b.sbuf("m2RMB", [128, 16], BF16)
    NMB = b.sbuf("m2NMB", [1, 2048], BF16)
    RMc = b.sbuf("m2RMc", [128, 1], F32)
    t_rminit = b.op("vector", lambda e: e.memset(RM[:, :], -1e30), sv)
    b.wait("vector", t_rminit)

    t_pb = [None, None]
    t_ev = None
    for mt in range(16):
        for dt_ in range(DT):
            k = mt * DT + dt_
            bk_ = k % 2
            psb = [ps_a, ps_b][bk_]
            b.wait("tensor", t_pb[bk_])
            t_mm = b.op("tensor", lambda e, mt=mt, dt_=dt_, psb=psb: e.matmul(
                psb[:, 0:128], xl[:, dt_ * W_ + mt * 128:dt_ * W_ + mt * 128 + 128],
                eye[:, :], start=True, stop=True), st)
            b.wait("scalar", t_mm)
            t_ev = b.op("scalar", lambda e, mt=mt, dt_=dt_, psb=psb:
                        e.activation(
                XTM[:, mt * 1024 + dt_ * 128:mt * 1024 + dt_ * 128 + 128],
                psb[:, 0:128], AF.Copy, bias=0.0, scale=1.0), ss_)
            t_pb[bk_] = t_ev

    t_rmx = None
    for mt in range(16):
        for npc in range(4):
            k = mt * 4 + npc
            bk_ = k % 2
            psb = [ps_a, ps_b][bk_]
            b.wait("tensor", t_pb[bk_])
            t_mm = None
            for dt_ in range(DT):
                t_mm = b.op("tensor", lambda e, mt=mt, npc=npc, dt_=dt_,
                            psb=psb: e.matmul(
                    psb[:, 0:512],
                    xl[:, dt_ * W_ + mt * 128:dt_ * W_ + mt * 128 + 128],
                    PHIB[:, dt_ * 2048 + npc * 512:dt_ * 2048 + npc * 512 + 512],
                    start=(dt_ == 0), stop=(dt_ == DT - 1)), st)
            b.wait("scalar", later(t_mm, t_ev, t_rmx))
            t1 = b.op("scalar", lambda e, psb=psb: e.activation(
                QF[:, :], psb[:, 0:512], AF.Copy, bias=0.0, scale=1.0), ss_)
            b.wait("gpsimd", t1)
            t_ev = b.dma("gpsimd",
                         lloc.ap()[mt * 128:(mt + 1) * 128,
                                   npc * 512:(npc + 1) * 512], QF[:, :], sd)
            b.wait("vector", t1)
            b.op("vector", lambda e: e.reduce_max(
                RMc[:, :], QF[:, :], axis=AX.X), None)
            t_rmx = b.op("vector", lambda e, mt=mt: e.tensor_tensor(
                RM[:, mt:mt + 1], RM[:, mt:mt + 1], RMc[:, :], ALU.max), sv)
            t_pb[bk_] = t_ev
    t_rmb = b.op("vector", lambda e: e.tensor_copy(RMB[:, :], RM[:, :]), sv)
    b.wait("tensor", t_rmb)
    t_nme = None
    for mt in range(16):
        bk_ = mt % 2
        psb = [ps_a, ps_b][bk_]
        b.wait("tensor", t_pb[bk_])
        t_tm = b.op("tensor", lambda e, mt=mt, psb=psb: e.matmul(
            psb[0:1, 0:128], RMB[:, mt:mt + 1], eye[:, :],
            start=True, stop=True), st)
        b.wait("scalar", later(t_tm, t_nme))
        t_nme = b.op("scalar", lambda e, mt=mt, psb=psb: e.activation(
            QF[0:1, 0:128], psb[0:1, 0:128], AF.Copy, bias=0.0, scale=1.0),
            ss_)
        b.wait("vector", t_nme)
        t_nme = b.op("vector", lambda e, mt=mt: e.tensor_scalar(
            NMB[0:1, mt * 128:(mt + 1) * 128], QF[0:1, 0:128], -1.0, None,
            ALU.mult), sv)
        t_pb[bk_] = t_nme
    b.barrier(["gpsimd"])
    t_ag = b.cc("AllGather", ALU.bypass, [list(range(8))],
                [lloc.ap().opt()], [lag_out.ap().opt()], scc)
    b.barrier()

    t_lpw = None
    t_dout = None
    for mt in range(16):
        for npc in range(4):
            t_ld = None
            for r in range(8):
                pr = r % 2
                b.wait("sync", t_lpw)
                t_ld = b.dma("sync", C2[:, pr * 512:(pr + 1) * 512],
                             lag_out.ap()[r * MTOK + mt * 128:
                                          r * MTOK + (mt + 1) * 128,
                                          npc * 512:(npc + 1) * 512], sw)
                b.wait("vector", t_ld)
                if r == 0:
                    t_lpw = b.op("vector", lambda e, pr=pr: e.tensor_scalar(
                        LP[:, :], C2[:, pr * 512:(pr + 1) * 512],
                        maskc[:, 0:1], None, ALU.mult), sv)
                else:
                    t_lpw = b.op("vector", lambda e, r=r, pr=pr:
                                 e.scalar_tensor_tensor(
                        LP[:, :], C2[:, pr * 512:(pr + 1) * 512],
                        maskc[:, r:r + 1], LP[:, :], ALU.mult, ALU.add), sv)
            b.wait("sync", t_dout)
            t_ml = b.dma("sync", ML[:, :],
                         lloc.ap()[mt * 128:(mt + 1) * 128,
                                   npc * 512:(npc + 1) * 512], sw)
            b.wait("vector", t_ml)
            b.op("vector", lambda e: e.tensor_tensor(
                M2[:, :], ML[:, :], LP[:, :], ALU.max), None)
            t_s0 = b.op("vector", lambda e: e.tensor_tensor(
                E0[:, :], ML[:, :], M2[:, :], ALU.subtract), sv)
            b.wait("scalar", t_s0)
            t_e0 = b.op("scalar", lambda e: e.activation(
                E0[:, :], E0[:, :], AF.Exp, bias=0.0, scale=1.0), ss_)
            t_s1 = b.op("vector", lambda e: e.tensor_tensor(
                LP[:, :], LP[:, :], M2[:, :], ALU.subtract), sv)
            b.wait("scalar", t_s1)
            t_e1 = b.op("scalar", lambda e: e.activation(
                LP[:, :], LP[:, :], AF.Exp, bias=0.0, scale=1.0), ss_)
            b.wait("vector", later(t_e0, t_e1))
            b.op("vector", lambda e: e.tensor_tensor(
                M2[:, :], E0[:, :], LP[:, :], ALU.add), None)
            b.op("vector", lambda e: e.reciprocal(M2[:, :], M2[:, :]), None)
            t_d = b.op("vector", lambda e: e.tensor_tensor(
                DTI[:, :], E0[:, :], M2[:, :], ALU.mult), sv)
            b.wait("gpsimd", t_d)
            t_dout = b.dma("gpsimd",
                           dispd[mt * 128:(mt + 1) * 128,
                                 npc * 512:(npc + 1) * 512], DTI[:, :], sd)
    b.barrier()

    t_db = [None, None]
    t_ev = None
    t_pb = [None, None]
    t_rmx = None
    for npt in range(16):
        for dc in range(2):
            k = npt * 2 + dc
            bk_ = k % 2
            psb = [ps_a, ps_b][bk_]
            b.wait("tensor", t_pb[bk_])
            t_mm = None
            for mt in range(16):
                pr = mt % 2
                b.wait("sync", t_db[pr])
                t_ld = b.dma("sync", DB[:, pr * 128:(pr + 1) * 128],
                             dispd[mt * 128:(mt + 1) * 128,
                                   npt * 128:(npt + 1) * 128], sw)
                b.wait("tensor", t_ld)
                t_mm = b.op("tensor", lambda e, mt=mt, npt=npt, dc=dc, pr=pr,
                            psb=psb: e.matmul(
                    psb[:, 0:512], DB[:, pr * 128:(pr + 1) * 128],
                    XTM[:, mt * 1024 + dc * 512:mt * 1024 + dc * 512 + 512],
                    start=(mt == 0), stop=(mt == 15)),
                    st)
                t_db[pr] = t_mm
            b.wait("scalar", later(t_mm, t_ev))
            t1 = b.op("scalar", lambda e, psb=psb: e.activation(
                QF[:, :], psb[:, 0:512], AF.Copy, bias=0.0, scale=1.0), ss_)
            b.wait("gpsimd", t1)
            t_ev = b.dma("gpsimd",
                         srs_in.ap()[npt * 128:(npt + 1) * 128,
                                     dc * 512:(dc + 1) * 512], QF[:, :], sd)
            t_pb[bk_] = t1
    b.barrier(["gpsimd"])
    t_rs = b.cc("ReduceScatter", ALU.add, [[0, 1, 2, 3], [4, 5, 6, 7]],
                [srs_in.ap().opt()], [srs_out.ap().opt()], scc)
    b.barrier()

    WnB = XTM[:, 0:8192]
    t_sl = None
    t_ev = None
    t_slc = None
    t_tr = None
    t_pb = [None, None]
    for nl in range(16):
        b.wait("sync", later(t_slc, t_ev))
        t_sl = b.dma("sync", C2[0:32, 0:1024],
                     srs_out.ap()[nl * 32:(nl + 1) * 32, :], sw)
        t_b1 = b.dma("sync", QF[0:1, 0:512],
                     io["m2_bnc"][nl:nl + 1, 0:512], sw)
        t_b2 = b.dma("sync", T1[0:1, 0:512],
                     io["m2_bnc"][nl:nl + 1, 512:1024], sw)
        b.wait("vector", later(t_sl, t_b2, t_tr, t_pb[0], t_pb[1]))
        b.op("vector", lambda e: e.tensor_copy(
            bnrow[0:1, 0:512], QF[0:1, 0:512]), None)
        t_bnr = b.op("vector", lambda e: e.tensor_copy(
            bnrow[0:1, 512:1024], T1[0:1, 0:512]), sv)
        t_slc = b.op("vector", lambda e: e.tensor_copy(
            SLB[:, :], C2[0:32, 0:1024]), sv)
        b.wait("tensor", t_slc)
        for dchunk in range(DT):
            bk_ = dchunk % 2
            psb = [ps_a, ps_b][bk_]
            b.wait("tensor", t_pb[bk_])
            t_mm = b.op("tensor", lambda e, dchunk=dchunk, psb=psb: e.matmul(
                psb[:, 0:32], SLB[:, dchunk * 128:(dchunk + 1) * 128],
                eye[0:32, 0:32], start=True, stop=True), st)
            b.wait("scalar", t_mm)
            t_tr = b.op("scalar", lambda e, dchunk=dchunk, psb=psb:
                        e.activation(
                SLT[:, dchunk * 32:(dchunk + 1) * 32], psb[:, 0:32],
                AF.Copy, bias=0.0, scale=1.0), ss_)
            t_pb[bk_] = t_tr
        t_wl = None
        t_wc = None
        for dt_ in range(DT):
            b.wait("sync", t_wc)
            if dt_ == 0:
                b.wait("sync", later(t_ev, t_bnr))
            t_wl = b.dma("sync", QF[:, :],
                         io["m2_Wc"][nl, dt_ * 128:(dt_ + 1) * 128, 0:512], sw)
            b.wait("gpsimd", t_wl)
            t_wc = b.op("gpsimd", lambda e, dt_=dt_: e.tensor_copy(
                WnB[:, dt_ * 1024:dt_ * 1024 + 512], QF[:, :]), sg)
            b.wait("sync", t_wc)
            t_wl = b.dma("sync", QF[:, :],
                         io["m2_Wc"][nl, dt_ * 128:(dt_ + 1) * 128,
                                     512:1024], sw)
            b.wait("gpsimd", t_wl)
            t_wc = b.op("gpsimd", lambda e, dt_=dt_: e.tensor_copy(
                WnB[:, dt_ * 1024 + 512:dt_ * 1024 + 1024], QF[:, :]),
                sg)
        b.wait("tensor", later(t_wc, t_tr))
        for ec in range(2):
            bk_ = ec % 2
            psb = [ps_a, ps_b][bk_]
            b.wait("tensor", t_pb[bk_])
            for dt_ in range(DT):
                b.op("tensor", lambda e, dt_=dt_, ec=ec, psb=psb: e.matmul(
                    psb[0:32, 0:512], SLT[:, dt_ * 32:(dt_ + 1) * 32],
                    WnB[:, dt_ * 1024 + ec * 512:dt_ * 1024 + ec * 512 + 512],
                    start=(dt_ == 0), stop=False), None)
            t_mm = b.op("tensor", lambda e, ec=ec, psb=psb: e.matmul(
                psb[0:32, 0:512], onesr[0:1, 0:32],
                bnrow[0:1, ec * 512:ec * 512 + 512],
                start=False, stop=True), st)
            b.wait("scalar", later(t_mm, t_ev))
            t1 = b.op("scalar", lambda e, psb=psb: e.activation(
                QF[0:32, :], psb[0:32, 0:512], AF.Copy, bias=0.0, scale=1.0),
                ss_)
            b.wait("gpsimd", t1)
            t_ev = b.dma("gpsimd",
                         yag_in.ap()[nl * 32:(nl + 1) * 32,
                                     ec * 512:(ec + 1) * 512],
                         QF[0:32, :], sd)
            t_pb[bk_] = t1
    b.barrier(["gpsimd"])
    t_yag = b.cc("AllGather", ALU.bypass, [[0, 1, 2, 3], [4, 5, 6, 7]],
                 [yag_in.ap().opt()], [yag_out.ap().opt()], scc)
    b.barrier()

    ps_z2 = b.pool["e"][0:1, 0:512]
    ps_zr = b.pool["f"]
    t_ctev = None
    t_zc = None
    t_zread = [None, None]
    for mc in range(4):
        t_zmm = None
        for npt in range(16):
            bk_ = npt % 2
            psb = [ps_a, ps_b][bk_]
            b.wait("tensor", t_pb[bk_])
            t_mm = None
            for dt_ in range(DT):
                b.op("tensor", lambda e, npt=npt, mc=mc, dt_=dt_,
                     psb=psb: e.matmul(
                    psb[:, 0:512],
                    PHIB[:, dt_ * 2048 + npt * 128:dt_ * 2048 + npt * 128 + 128],
                    xl[:, dt_ * W_ + mc * 512:dt_ * W_ + mc * 512 + 512],
                    start=(dt_ == 0), stop=False), None)
            t_mm = b.op("tensor", lambda e, npt=npt, mc=mc, psb=psb: e.matmul(
                psb[:, 0:512], onesr[0:1, :],
                NMB[0:1, mc * 512:mc * 512 + 512],
                start=False, stop=True), st)
            pr = npt % 2
            b.wait("scalar", later(t_mm, t_ctev, t_zread[pr]))
            t_ct = b.op("scalar", lambda e, pr=pr, psb=psb: e.activation(
                CTT[:, pr * 512:(pr + 1) * 512], psb[:, 0:512],
                AF.Exp, bias=0.0, scale=1.0), ss_)
            b.wait("gpsimd", t_ct)
            t_ctev = b.dma("gpsimd",
                           ctd[npt * 128:(npt + 1) * 128,
                               mc * 512:(mc + 1) * 512],
                           CTT[:, pr * 512:(pr + 1) * 512], sd)
            t_pb[bk_] = t_ctev
            b.wait("tensor", later(t_ct, t_zc))
            t_zmm = b.op("tensor", lambda e, pr=pr, npt=npt: e.matmul(
                ps_z2, onescbf[:, :], CTT[:, pr * 512:(pr + 1) * 512],
                start=(npt == 0), stop=(npt == 15), skip_group_check=True),
                st)
            t_zread[pr] = t_zmm
        b.wait("vector", t_zmm)
        t_zc = b.op("vector", lambda e, mc=mc: e.tensor_copy(
            zrow[0:1, mc * 512:(mc + 1) * 512], ps_z2), sv)
    b.wait("vector", t_zc)
    t_zrec = b.op("vector", lambda e: e.reciprocal(
        zrow[0:1, :], zrow[0:1, :]), sv)
    b.barrier()

    t_yb = [None, None]
    t_cb = [None, None]
    t_yc_last = [None]
    ue = [None]
    t_zev = None
    t_xr_tok = None
    for mc in range(4):
        b.wait("tensor", later(t_zrec, t_zev))
        t_zm = b.op("tensor", lambda e, mc=mc: e.matmul(
            ps_zr[:, 0:512], onesrf[:, :], zrow[0:1, mc * 512:mc * 512 + 512],
            start=True, stop=True), st)
        b.wait("scalar", later(t_zm, ue[0]))
        t_zev = b.op("scalar", lambda e: e.activation(
            zrep[:, :], ps_zr[:, 0:512], AF.Copy, bias=0.0, scale=1.0), ss_)
        for et in range(DT):
            bk_ = et % 2
            psb = [ps_a, ps_b][bk_]
            b.wait("tensor", t_pb[bk_])
            t_mm = None
            for npt in range(16):
                pr = npt % 2
                b.wait("sync", later(t_yb[pr], t_yc_last[0]))
                t_y = b.dma("sync", QF[:, 0:128],
                            yag_out.ap()[npt * 128:(npt + 1) * 128,
                                         et * 128:(et + 1) * 128], sw)
                b.wait("gpsimd", t_y)
                b.wait("gpsimd", t_yb[pr])
                t_yc = b.op("gpsimd", lambda e, pr=pr: e.tensor_copy(
                    YBT[:, pr * 128:(pr + 1) * 128], QF[:, 0:128]), sg)
                t_yc_last[0] = t_yc
                b.wait("sync", t_cb[pr])
                t_ctl = b.dma("sync", CTT[:, pr * 512:(pr + 1) * 512],
                              ctd[npt * 128:(npt + 1) * 128,
                                  mc * 512:(mc + 1) * 512], sw)
                b.wait("tensor", later(t_yc, t_ctl))
                t_mm = b.op("tensor", lambda e, npt=npt, pr=pr, psb=psb:
                            e.matmul(
                    psb[:, 0:512], YBT[:, pr * 128:(pr + 1) * 128],
                    CTT[:, pr * 512:(pr + 1) * 512],
                    start=(npt == 0), stop=(npt == 15)), st)
                t_yb[pr] = t_mm
                t_cb[pr] = t_mm
            b.wait("vector", later(t_mm, t_zev, ue[0]))
            b.op("vector", lambda e, psb=psb: e.tensor_tensor(
                T1[:, :], psb[:, 0:512], zrep[:, :], ALU.mult), None)
            ue[0] = b.op("vector", lambda e, et=et, mc=mc: e.tensor_tensor(
                xr[:, et * W_ + mc * 512:et * W_ + mc * 512 + 512],
                xr[:, et * W_ + mc * 512:et * W_ + mc * 512 + 512],
                T1[:, :], ALU.add), sv)
            t_pb[bk_] = ue[0]
            t_xr_tok = ue[0]
    return t_xr_tok



def emit_l3attn(b, io, xr, xl, t_xl):
    nc = b.nc
    sv = b.sem("v")
    ss_ = b.sem("s")
    st = b.sem("t")
    sd = b.sem("d")
    sw = b.sem("w")
    sg = b.sem("g")
    scc = b.sem("c")

    OM = b.tiles["XL"]
    QB = b.sbuf("a3QB", [128, DT * W_], BF16)
    K2H = b.sbuf("a3K2H", [128, 8192], BF16)
    VHB = b.sbuf("a3VHB", [128, 4096], BF16)
    KSF = b.sbuf("a3KSF", [128, 2048], F32)
    KDG = b.sbuf("a3KDG", [128, 512], F32)
    PT = b.sbuf("a3PT", [128, 1024], BF16)
    QF = b.sbuf("a3QF", [128, 512], F32)
    T1 = b.sbuf("a3T1", [128, 512], F32)
    DG1 = b.sbuf("a3DG1", [128, 512], F32)
    zrep = b.sbuf("a3zrep", [128, 512], F32)
    rz = b.sbuf("a3rz", [1, 1536], F32)
    SAC = b.sbuf("a3SAC", [1, 128], F32)
    RS1 = b.sbuf("a3RS1", [1, 768], F32)
    KB2 = b.sbuf("a3KB2", [128, 64], F32)
    KRW = b.sbuf("a3KRW", [1, 512], F32)
    eighth = b.sbuf("a3eighth", [128, 1], F32)
    neg20 = b.sbuf("a3neg20", [128, 1], F32)
    bigt = b.sbuf("a3bigt", [128, 1], F32)
    nbigt = b.sbuf("a3nbigt", [128, 1], F32)
    onesrf = b.sbuf("a3onesrf", [1, 128], F32)
    onescb = b.sbuf("a3onescb", [128, 1], BF16)
    onescf = b.sbuf("a3onescf", [128, 1], F32)
    boc = b.sbuf("a3boc", [128, 8], F32)
    kag_in = nc.dram_tensor("a3kag_in", [1024, MTOK], F32, kind="Internal")
    kag_out = nc.dram_tensor("a3kag_out", [4096, MTOK], F32, kind="Internal")
    vag_in = nc.dram_tensor("a3vag_in", [MTOK, 1024], F32, kind="Internal")
    vag_out = nc.dram_tensor("a3vag_out", [S_, 1024], F32, kind="Internal")
    dmd = nc.dram_tensor("a3dmd", [16, MTOK], F32, kind="Internal").ap()
    sar_in = nc.dram_tensor("a3sar_in", [1, 128], F32, kind="Internal")
    sar_out = nc.dram_tensor("a3sar_out", [1, 128], F32, kind="Internal")
    kp_in = nc.dram_tensor("a3kp_in", [16, MTOK], F32, kind="Internal")
    kp_out = nc.dram_tensor("a3kp_out", [64, MTOK], F32, kind="Internal")
    groups = [[0, 1, 2, 3], [4, 5, 6, 7]]

    t_c = b.dma("gpsimd", boc[:, :], io["a3_bout"], sd)
    b.wait("vector", t_c)
    b.op("vector", lambda e: e.memset(eighth[:, :], 0.125), sv)
    b.op("vector", lambda e: e.memset(neg20[:, :], -20.0), sv)
    b.op("vector", lambda e: e.memset(bigt[:, :], 1e30), sv)
    b.op("vector", lambda e: e.memset(nbigt[:, :], -1e30), sv)
    b.op("vector", lambda e: e.memset(onesrf[:, :], 1.0), sv)
    b.op("vector", lambda e: e.memset(onescb[:, :], 1.0), sv)
    t_ms = b.op("vector", lambda e: e.memset(onescf[:, :], 1.0), sv)
    b.wait("tensor", t_ms)
    b.wait("scalar", t_ms)

    lin = Lin(b, "a3", 8, [b.pool["e"], b.pool["f"]])
    qf_free = [None]

    for ot in range(DT):
        def evq(psb, tmm, mc, ot=ot):
            b.wait("scalar", tmm)
            return b.op("scalar", lambda e, ot=ot, mc=mc, psb=psb:
                        e.activation(
                QB[:, ot * W_ + mc * 512:ot * W_ + mc * 512 + 512],
                psb[:, 0:512], AF.Copy, bias=0.0, scale=1.0), ss_)
        lin.otile(io["a3_wqkvT"], 8, ot,
                  lambda kt, mc: xl[:, kt * W_ + mc * 512:kt * W_ + mc * 512 + 512],
                  evq, rhs_ready=t_xl if ot == 0 else None, n_mc=4)
    for ot in range(DT):
        def evk(psb, tmm, mc, ot=ot):
            b.wait("scalar", later(tmm, qf_free[0]))
            tk = b.op("scalar", lambda e, psb=psb: e.activation(
                QF[:, :], psb[:, 0:512], AF.Copy, bias=0.0, scale=1.0), ss_)
            b.wait("gpsimd", tk)
            qf_free[0] = b.dma(
                "gpsimd", kag_in.ap()[ot * 128:(ot + 1) * 128,
                                      mc * 512:mc * 512 + 512], QF[:, :], sd)
            return qf_free[0]
        lin.otile(io["a3_wqkvT"][:, 1024:2048], 8, ot,
                  lambda kt, mc: xl[:, kt * W_ + mc * 512:kt * W_ + mc * 512 + 512],
                  evk, n_mc=4)
    t_wvread = None
    for oc in range(2):
        t_wv = None
        for kt in range(DT):
            b.wait("sync", later(qf_free[0], t_wv, t_wvread))
            t_w = b.dma("sync", QF[:, :],
                        io["a3_wqkvT"][kt * 128:(kt + 1) * 128,
                                       2048 + oc * 512:2048 + oc * 512 + 512],
                        sw)
            b.wait("gpsimd", t_w)
            b.wait("gpsimd", t_wvread)
            t_wv = b.op("gpsimd", lambda e, kt=kt: e.tensor_copy(
                VHB[:, kt * 512:(kt + 1) * 512], QF[:, :]), sg)
        b.wait("tensor", t_wv)
        for mt in range(16):
            bk_ = lin.bank_i % 2
            psb = lin.psums[bk_]
            b.wait("tensor", lin.t_bank[bk_])
            t_mm = None
            for kt in range(DT):
                t_mm = b.op("tensor", lambda e, kt=kt, mt=mt, psb=psb:
                            e.matmul(
                    psb[:, 0:512],
                    xl[:, kt * W_ + mt * 128:kt * W_ + mt * 128 + 128],
                    VHB[:, kt * 512:(kt + 1) * 512],
                    start=(kt == 0), stop=(kt == DT - 1)),
                    st if kt == DT - 1 else None)
            if mt == 15:
                t_wvread = t_mm
            b.wait("scalar", later(t_mm, qf_free[0]))
            t1 = b.op("scalar", lambda e, psb=psb: e.activation(
                QF[:, :], psb[:, 0:512], AF.Copy, bias=0.0, scale=1.0), ss_)
            b.wait("gpsimd", t1)
            qf_free[0] = b.dma(
                "gpsimd", vag_in.ap()[mt * 128:(mt + 1) * 128,
                                      oc * 512:oc * 512 + 512], QF[:, :], sd)
            lin.t_bank[bk_] = t1
            lin.bank_i += 1
    b.barrier(["gpsimd"])
    b.cc("AllGather", ALU.bypass, groups, [kag_in.ap().opt()],
         [kag_out.ap().opt()], scc)
    b.cc("AllGather", ALU.bypass, groups, [vag_in.ap().opt()],
         [vag_out.ap().opt()], scc)
    b.barrier()

    t_kvcast = [None]
    t_kvread = [None]

    def load_k2h(j):
        for r in range(4):
            b.wait("sync", t_kvcast[0])
            t_w = b.dma("sync", KSF[:, :],
                        kag_out.ap()[r * 1024 + j * 128:
                                     r * 1024 + (j + 1) * 128, :], sw)
            b.wait("gpsimd", later(t_w, t_kvread[0]))
            t_kvcast[0] = b.op("gpsimd", lambda e, r=r: e.tensor_copy(
                K2H[:, r * 2048:(r + 1) * 2048], KSF[:, :]), sg)

    ps_s = [b.pool["a"], b.pool["b"]]
    ps_z = b.pool["d"][0:1, 0:512]
    ps_dg = b.pool["c"][0:1, 0:512]
    t_exp = [None, None]
    t_ptread = [None, None]
    t_psread = None
    t_dgread = None
    t_kdgread = None
    t_dmw = None
    t_sq = None
    for j in range(DT):
        load_k2h(j)
        b.wait("tensor", t_kvcast[0])
        for hh in range(2):
            base = 64 * hh
            h = 2 * j + hh
            for qc in range(4):
                b.wait("sync", t_kdgread)
                t_kdg = b.dma("sync", KDG[:, :],
                              kag_in.ap()[j * 128:(j + 1) * 128,
                                          qc * 512:qc * 512 + 512], sw)
                qsl = QB[base:base + 64,
                         j * W_ + qc * 512:j * W_ + qc * 512 + 512]
                b.wait("tensor", t_psread)
                t_kc = None
                for kc in range(64):
                    p = kc % 2
                    b.wait("tensor", t_exp[p])
                    t_s = b.op("tensor", lambda e, kc=kc, p=p, base=base,
                               qsl=qsl: e.matmul(
                        ps_s[p][:, 0:512],
                        K2H[base:base + 64, kc * 128:kc * 128 + 128],
                        qsl, start=True, stop=True), st)
                    b.wait("scalar", later(t_s, t_ptread[p]))
                    t_exp[p] = b.op("scalar", lambda e, p=p: e.activation(
                        PT[:, p * 512:(p + 1) * 512], ps_s[p][:, 0:512],
                        AF.Exp, bias=neg20[:, 0:1], scale=eighth[:, 0:1]),
                        ss_)
                    b.wait("tensor", t_exp[p])
                    t_kc = b.op("tensor", lambda e, kc=kc, p=p: e.matmul(
                        ps_z, onescb[:, :], PT[:, p * 512:(p + 1) * 512],
                        start=(kc == 0), stop=(kc == 63),
                        skip_group_check=True), st)
                    t_ptread[p] = t_kc
                b.wait("vector", later(t_kdg, t_dgread))
                t_d1 = b.op("vector", lambda e, base=base, qsl=qsl:
                            e.tensor_tensor(
                    DG1[base:base + 64, :], qsl, KDG[base:base + 64, :],
                    ALU.mult), sv)
                t_kdgread = t_d1
                b.wait("tensor", t_d1)
                t_dg = b.op("tensor", lambda e, base=base: e.matmul(
                    ps_dg, onescf[base:base + 64, 0:1],
                    DG1[base:base + 64, :], start=True, stop=True), st)
                t_dgread = t_dg
                b.wait("vector", later(t_kc, t_dmw))
                t_r1 = b.op("vector", lambda e: e.reciprocal(
                    rz[0:1, 0:512], ps_z), sv)
                b.wait("scalar", later(t_dg, t_sq))
                t_e1 = b.op("scalar", lambda e: e.activation(
                    rz[0:1, 512:1024], ps_dg, AF.Exp,
                    bias=neg20[0:1, 0:1], scale=eighth[0:1, 0:1]), ss_)
                t_psread = later(t_r1, t_e1)
                b.wait("vector", t_e1)
                t_dm = b.op("vector", lambda e: e.tensor_tensor(
                    rz[0:1, 1024:1536], rz[0:1, 512:1024], rz[0:1, 0:512],
                    ALU.mult), sv)
                t_sq = b.op("vector", lambda e: e.tensor_tensor(
                    rz[0:1, 512:1024], rz[0:1, 1024:1536],
                    rz[0:1, 1024:1536], ALU.mult), sv)
                b.op("vector", lambda e, h=h, qc=qc: e.reduce_sum(
                    SAC[0:1, qc * 16 + h:qc * 16 + h + 1],
                    rz[0:1, 1024:1536], axis=AX.X), sv)
                t_s2 = b.op("vector", lambda e, h=h, qc=qc: e.reduce_sum(
                    SAC[0:1, 64 + qc * 16 + h:64 + qc * 16 + h + 1],
                    rz[0:1, 512:1024], axis=AX.X), sv)
                b.wait("gpsimd", t_dm)
                t_dmw = b.dma("gpsimd", dmd[h:h + 1, qc * 512:qc * 512 + 512],
                              rz[0:1, 1024:1536], sd)
                t_kvread[0] = t_kc
    b.barrier(["gpsimd"])
    b.dma("gpsimd", sar_in.ap(), SAC[0:1, :], sd)
    b.cc("AllReduce", ALU.add, groups, [sar_in.ap().opt()],
         [sar_out.ap().opt()], scc)
    b.barrier()
    t_sl = b.dma("sync", SAC[0:1, :], sar_out.ap(), sw)
    b.wait("vector", t_sl)
    def _vstep(fn):
        tok = b.op("vector", fn, sv)
        b.wait("vector", tok)
        return tok
    _vstep(lambda e: e.tensor_tensor(
        RS1[0:1, 0:16], SAC[0:1, 0:16], SAC[0:1, 16:32], ALU.add))
    _vstep(lambda e: e.tensor_tensor(
        RS1[0:1, 64:80], SAC[0:1, 32:48], SAC[0:1, 48:64], ALU.add))
    _vstep(lambda e: e.tensor_tensor(
        RS1[0:1, 0:16], RS1[0:1, 0:16], RS1[0:1, 64:80], ALU.add))
    _vstep(lambda e: e.tensor_tensor(
        RS1[0:1, 128:144], SAC[0:1, 64:80], SAC[0:1, 80:96], ALU.add))
    _vstep(lambda e: e.tensor_tensor(
        RS1[0:1, 192:208], SAC[0:1, 96:112], SAC[0:1, 112:128], ALU.add))
    _vstep(lambda e: e.tensor_tensor(
        RS1[0:1, 128:144], RS1[0:1, 128:144], RS1[0:1, 192:208], ALU.add))
    _vstep(lambda e: e.tensor_scalar(
        RS1[0:1, 256:272], RS1[0:1, 0:16], 1.0 / 8192.0, None, ALU.mult))
    _vstep(lambda e: e.tensor_tensor(
        RS1[0:1, 320:336], RS1[0:1, 256:272], RS1[0:1, 256:272], ALU.mult))
    _vstep(lambda e: e.tensor_scalar(
        RS1[0:1, 384:400], RS1[0:1, 128:144], 1.0 / 8191.0, None, ALU.mult))
    _vstep(lambda e: e.tensor_scalar(
        RS1[0:1, 448:464], RS1[0:1, 320:336], -8192.0 / 8191.0, None,
        ALU.mult))
    t_var = _vstep(lambda e: e.tensor_tensor(
        RS1[0:1, 384:400], RS1[0:1, 384:400], RS1[0:1, 448:464], ALU.add))
    b.wait("scalar", t_var)
    t_sd = b.op("scalar", lambda e: e.activation(
        RS1[0:1, 512:528], RS1[0:1, 384:400], AF.Sqrt, bias=0.0,
        scale=1.0), ss_)
    b.wait("vector", t_sd)
    _vstep(lambda e: e.reciprocal(
        RS1[0:1, 576:592], RS1[0:1, 512:528]))
    _vstep(lambda e: e.tensor_tensor(
        RS1[0:1, 640:656], RS1[0:1, 256:272], RS1[0:1, 576:592], ALU.mult))
    t_nmi = _vstep(lambda e: e.tensor_scalar(
        RS1[0:1, 640:656], RS1[0:1, 640:656], -1.0, None, ALU.mult))
    b.wait("scalar", t_nmi)
    t_kw = None
    t_krd = None
    for h in range(16):
        for qc in range(4):
            b.wait("sync", t_krd)
            t_ld = b.dma("sync", rz[0:1, 1024:1536],
                         dmd[h:h + 1, qc * 512:qc * 512 + 512], sw)
            b.wait("scalar", later(t_ld, t_kw))
            t_az = b.op("scalar", lambda e, h=h: e.activation(
                rz[0:1, 0:512], rz[0:1, 1024:1536], AF.Abs,
                bias=RS1[0:1, 640 + h:641 + h],
                scale=RS1[0:1, 576 + h:577 + h]), ss_)
            t_krd = t_az
            b.wait("vector", t_az)
            t_kp = b.op("vector", lambda e: e.tensor_scalar(
                rz[0:1, 512:1024], rz[0:1, 0:512], 1.5, None, ALU.is_le),
                sv)
            b.wait("gpsimd", t_kp)
            t_kw = b.dma("gpsimd",
                         kp_in.ap()[h:h + 1, qc * 512:qc * 512 + 512],
                         rz[0:1, 512:1024], sd)
    b.barrier(["gpsimd"])
    b.cc("AllGather", ALU.bypass, groups, [kp_in.ap().opt()],
         [kp_out.ap().opt()], scc)
    b.barrier()

    psO = b.pool["r2"][:, 0:512]
    ps_c = b.pool["c"]
    t_exp = [None, None]
    t_ptread = [None, None]
    t_psread = None
    t_kb2read = None
    t_kbps = None
    t_krwread = None
    t_vread = None
    t_omw = None
    t_zev = None
    t_rzread = None
    t_kvread[0] = None
    t_kvcast[0] = t_kvcast[0]
    for j in range(DT):
        load_k2h(j)
        b.wait("tensor", t_kvcast[0])
        for hh in range(2):
            base = 64 * hh
            h = 2 * j + hh
            for grp in range(4):
                b.wait("sync", t_kvcast[0])
                t_w = b.dma("sync",
                            KSF[:, 0:1024].rearrange("p (c d) -> p c d", d=64),
                            vag_out.ap()[grp * 2048:(grp + 1) * 2048,
                                         64 * h:64 * h + 64]
                            .rearrange("(c t) d -> t c d", t=128), sw)
                b.wait("gpsimd", later(t_w, t_vread))
                t_kvcast[0] = b.op("gpsimd", lambda e, grp=grp: e.tensor_copy(
                    VHB[:, grp * 1024:(grp + 1) * 1024], KSF[:, 0:1024]), sg)
            b.wait("tensor", later(t_zev, t_kb2read))
            t_bm = None
            for r in range(4):
                for pc_ in range(4):
                    b.wait("sync", t_krwread)
                    t_kr = b.dma("sync", KRW[0:1, :],
                                 kp_out.ap()[16 * r + h:16 * r + h + 1,
                                             pc_ * 512:pc_ * 512 + 512], sw)
                    b.wait("tensor", t_kr)
                    for i in range(4):
                        kc = 16 * r + pc_ * 4 + i
                        t_bm = b.op("tensor", lambda e, kc=kc, i=i: e.matmul(
                            ps_c[:, kc:kc + 1],
                            KRW[0:1, i * 128:(i + 1) * 128],
                            onescf[0:1, 0:1], start=(kc == 0), stop=True,
                            skip_group_check=True), st)
                    t_krwread = t_bm
            b.wait("scalar", t_bm)
            t_kbev = b.op("scalar", lambda e: e.activation(
                KB2[:, :], ps_c[:, 0:64], AF.Identity,
                bias=nbigt[:, 0:1], scale=bigt[:, 0:1]), ss_)
            t_kbps = t_kbev
            b.wait("tensor", t_kvcast[0])
            for qc in range(4):
                b.wait("sync", t_rzread)
                t_kl = b.dma("sync", rz[0:1, 1024:1536],
                             kp_in.ap()[h:h + 1, qc * 512:qc * 512 + 512],
                             sw)
                qsl = QB[base:base + 64,
                         j * W_ + qc * 512:j * W_ + qc * 512 + 512]
                b.wait("tensor", later(t_psread, t_omw))
                b.wait("scalar", t_kbev)
                t_kc = None
                t_pv = None
                for kc in range(64):
                    p = kc % 2
                    b.wait("tensor", t_exp[p])
                    t_s = b.op("tensor", lambda e, kc=kc, p=p, base=base,
                               qsl=qsl: e.matmul(
                        ps_s[p][:, 0:512],
                        K2H[base:base + 64, kc * 128:kc * 128 + 128],
                        qsl, start=True, stop=True), st)
                    b.wait("scalar", later(t_s, t_ptread[p]))
                    t_exp[p] = b.op("scalar", lambda e, p=p, kc=kc:
                                    e.activation(
                        PT[:, p * 512:(p + 1) * 512], ps_s[p][:, 0:512],
                        AF.Exp, bias=KB2[:, kc:kc + 1],
                        scale=eighth[:, 0:1]), ss_)
                    t_kb2read = t_exp[p]
                    b.wait("tensor", t_exp[p])
                    t_pv = b.op("tensor", lambda e, kc=kc, p=p, base=base:
                                e.matmul(
                        psO[base:base + 64, 0:512] if False else
                        b.pool["r2"][base:base + 64, 0:512],
                        VHB[:, kc * 64:kc * 64 + 64],
                        PT[:, p * 512:(p + 1) * 512],
                        start=(kc == 0), stop=(kc == 63),
                        skip_group_check=True), st)
                    t_kc = b.op("tensor", lambda e, kc=kc, p=p: e.matmul(
                        ps_z, onescb[:, :], PT[:, p * 512:(p + 1) * 512],
                        start=(kc == 0), stop=(kc == 63),
                        skip_group_check=True), st)
                    t_ptread[p] = later(t_kc, t_pv)
                t_vread = t_pv
                t_kvread[0] = t_kc
                b.wait("vector", later(t_kc, t_kl))
                b.op("vector", lambda e: e.tensor_scalar(
                    rz[0:1, 0:512], ps_z, 1e-30, None, ALU.add), sv)
                b.op("vector", lambda e: e.reciprocal(
                    rz[0:1, 512:1024], rz[0:1, 0:512]), sv)
                t_z2 = b.op("vector", lambda e: e.tensor_tensor(
                    rz[0:1, 0:512], rz[0:1, 512:1024], rz[0:1, 1024:1536],
                    ALU.mult), sv)
                t_rzread = t_z2
                t_psread = t_z2
                b.wait("tensor", later(t_z2, t_zev, t_kbps))
                t_rp = b.op("tensor", lambda e, base=base: e.matmul(
                    ps_c[base:base + 64, 0:512], onesrf[0:1, 0:64],
                    rz[0:1, 0:512], start=True, stop=True), st)
                b.wait("scalar", t_rp)
                t_ze = b.op("scalar", lambda e, base=base: e.activation(
                    zrep[base:base + 64, :], ps_c[base:base + 64, 0:512],
                    AF.Copy, bias=0.0, scale=1.0), ss_)
                t_zev = t_ze
                b.wait("vector", later(t_ze, t_pv))
                t_om = b.op("vector", lambda e, base=base, j=j, qc=qc:
                            e.tensor_tensor(
                    OM[base:base + 64,
                       j * W_ + qc * 512:j * W_ + qc * 512 + 512],
                    b.pool["r2"][base:base + 64, 0:512],
                    zrep[base:base + 64, :], ALU.mult), sv)
                t_omw = t_om
    b.barrier()

    t_t1read = [None]
    t_last = None
    for ot in range(DT):
        def evo(psb, tmm, mc, ot=ot):
            b.wait("scalar", later(tmm, t_t1read[0]))
            t1 = b.op("scalar", lambda e, psb=psb, ot=ot: e.activation(
                T1[:, :], psb[:, 0:512], AF.Identity,
                bias=boc[:, ot:ot + 1], scale=1.0), ss_)
            b.wait("vector", t1)
            tok = b.op("vector", lambda e, ot=ot, mc=mc: e.tensor_tensor(
                xr[:, ot * W_ + mc * 512:ot * W_ + mc * 512 + 512],
                xr[:, ot * W_ + mc * 512:ot * W_ + mc * 512 + 512],
                T1[:, :], ALU.add), sv)
            t_t1read[0] = tok
            return tok
        t_last = lin.otile(io["a3_woT"], 8, ot,
                           lambda kt, mc: OM[:, kt * W_ + mc * 512:
                                             kt * W_ + mc * 512 + 512],
                           evo, n_mc=4)
    return t_last



def run_l0attn(in_maps, xchunks, trace=False):
    nc = bass.Bass("TRN2", target_bir_lowering=False, debug=False)
    io = {}
    io["xin"] = nc.dram_tensor("xin", [D_, MTOK], F32,
                               kind="ExternalInput").ap()
    wkeys = ["m0_xT", "m0_inwT", "m0_convw", "m0_convb", "m0_xprojT",
             "m0_dtwT", "m0_dtb", "m0_Alog", "m0_D", "m0_outwT", "m0_flip",
             "m0_oh"]
    for k in wkeys:
        a = in_maps[0][k]
        dt_ = BF16 if a.dtype == BFNP else F32
        io[k] = nc.dram_tensor(k, list(a.shape), dt_,
                               kind="ExternalInput").ap()
    io["xout"] = nc.dram_tensor("xout", [D_, MTOK], F32,
                                kind="ExternalOutput").ap()
    yloc = nc.dram_tensor("m0yloc", [1024, S_], F32, kind="Internal").ap()
    rs_in = nc.dram_tensor("m0rsin", [4096, MTOK], F32, kind="Internal")
    rs_out = nc.dram_tensor("m0rsout", [1024, MTOK], F32, kind="Internal")

    with ExitStack() as stack:
        b = Bld(nc, stack)
        pool = {k: b.psum("P" + k, [128, 512], F32) for k in "abcdef"}
        pool["r2"] = b.psum("Pr2", [128, 1024], F32)
        b.pool = pool
        sv = b.sem("v")
        ss_ = b.sem("s")
        st = b.sem("t")
        sd = b.sem("d")
        sw = b.sem("w")
        sg = b.sem("g")
        scc = b.sem("c")
        sx = b.sem("x")
        so = b.sem("o")

        XRC = b.sbuf("XRC", [128, 8 * 512], F32)
        XLC = b.sbuf("XLC", [128, 8 * 512], BF16)
        XHC = b.sbuf("XHC", [128, 16 * 516], BF16)
        XC = b.sbuf("XC", [128, 16 * 512], BF16)
        ZHC = b.sbuf("ZHC", [128, 8 * 512], BF16)
        XCF = b.sbuf("XCF", [128, 512], F32)
        XPJ = b.sbuf("XPJ", [128, 16 * 96], BF16)
        DTWB = b.sbuf("DTWB", [64, 1024], BF16)
        OWB = b.sbuf("OWB", [128, 64 * 128], BF16)
        OHB = b.sbuf("OHB", [16, 16 * 128], BF16)
        PRJB = b.sbuf("PRJB", [64, 512], BF16)
        BPS = b.sbuf("BPS", [16, 1024], BF16)
        DL1 = b.sbuf("DL1", [128, 512], F32)
        DL2 = b.sbuf("DL2", [128, 512], F32)
        DL3 = b.sbuf("DL3", [128, 512], F32)
        DLT = b.sbuf("DLT", [128, 512], F32)
        U = b.sbuf("U", [128, 512], F32)
        DA = b.sbuf("DA", [128, 512], F32)
        DBU = b.sbuf("DBU", [128, 512], F32)
        HS = b.sbuf("HS", [128, 512], F32)
        Y = b.sbuf("Y", [128, 512], F32)
        TY = b.sbuf("TY", [128, 512], F32)
        SZ = b.sbuf("SZ", [128, 512], F32)
        YB = b.sbuf("YB", [128, 8 * 512], BF16)
        QFM = b.sbuf("QFM", [128, 512], F32)
        WSM = b.sbuf("WSM", [128, 2048], F32)
        HCAR = b.sbuf("HCAR", [128, 128], F32)
        ACOL = b.sbuf("ACOL", [128, 128], F32)
        CW = b.sbuf("CW", [128, 64], F32)
        CB = b.sbuf("CB", [128, 16], F32)
        DTB = b.sbuf("DTB", [128, 8], F32)
        DCOL = b.sbuf("DCOL", [128, 8], F32)
        FLC = b.sbuf("FLC", [128, 1], F32)
        negone = b.sbuf("negone", [128, 1], F32)
        BL1 = b.sbuf("BL1", [128, 2048], F32)
        BL2 = b.sbuf("BL2", [128, 2048], F32)

        b.dma("gpsimd", CW[:, :], io["m0_convw"], sd)
        b.dma("gpsimd", CB[:, :], io["m0_convb"], sd)
        b.dma("gpsimd", DTB[:, :], io["m0_dtb"], sd)
        b.dma("gpsimd", DCOL[:, :], io["m0_D"], sd)
        b.dma("gpsimd", FLC[:, :], io["m0_flip"], sd)
        t_al = b.dma("gpsimd", ACOL[:, :], io["m0_Alog"], sd)
        t_oh = b.dma("gpsimd", OHB[:, :], io["m0_oh"], sd)
        b.wait("scalar", t_al)
        t_ae = b.op("scalar", lambda e: e.activation(
            ACOL[:, :], ACOL[:, :], AF.Exp, bias=0.0, scale=1.0), ss_)
        b.wait("vector", t_ae)
        b.op("vector", lambda e: e.memset(negone[:, :], -1.0), sv)
        b.op("vector", lambda e: e.memset(HCAR[:, :], 0.0), sv)
        for ti in range(16):
            b.op("vector", lambda e, ti=ti: e.memset(
                XHC[:, ti * 516:ti * 516 + 3], 0.0), sv)
        t_an = b.op("vector", lambda e: e.tensor_scalar(
            ACOL[:, :], ACOL[:, :], -1.0, None, ALU.mult), sv)
        b.wait("scalar", t_an)
        t_xpw = b.dma("sync", WSM[:, 0:1536].rearrange(
            "p (k c) -> p k c", c=96),
            io["m0_xprojT"][0:2048, 0:96].rearrange("(k p) c -> p k c", p=128),
            sw)
        b.wait("gpsimd", t_xpw)
        t_xpc = b.op("gpsimd", lambda e: e.tensor_copy(
            XPJ[:, :], WSM[:, 0:1536]), sg)
        b.wait("sync", t_xpc)
        t_dtw = b.dma("sync", WSM[0:64, 0:1024], io["m0_dtwT"], sw)
        b.wait("gpsimd", t_dtw)
        t_dtc = b.op("gpsimd", lambda e: e.tensor_copy(
            DTWB[:, :], WSM[0:64, 0:1024]), sg)
        t_owc = t_dtc
        for kt in range(8):
            b.wait("sync", t_owc)
            t_w = b.dma("sync", WSM[:, 0:1024],
                        io["m0_outwT"][kt * 128:(kt + 1) * 128, :], sw)
            b.wait("gpsimd", t_w)
            t_owc = b.op("gpsimd", lambda e, kt=kt: e.tensor_copy(
                OWB[:, kt * 1024:(kt + 1) * 1024], WSM[:, 0:1024]), sg)
        b.barrier()

        lin = Lin(b, "m0", 8, [pool["e"], pool["f"]])
        t_carry = [None] * 16
        t_xc = [None] * 16
        t_yb = [None] * 8
        t_prj_read = None
        t_bps_read = None
        t_dl_read = None
        t_psd_read = None
        t_psa_read = None
        t_psb_read = None
        t_psc_read = None
        t_da_read = None
        t_ev_ow = [None, None]
        t_qfm = [None]

        for ci in range(16):
            if ci > 0:
                for ti in range(16):
                    t_carry[ti] = b.op("vector", lambda e, ti=ti:
                                       e.tensor_copy(
                        XHC[:, ti * 516:ti * 516 + 3],
                        XHC[:, ti * 516 + 512:ti * 516 + 515]), sv)
            t_in = None
            for d in range(8):
                t_in = b.dma("sync", XRC[:, d * 512:(d + 1) * 512],
                             io["m0_xT"][d * 128:(d + 1) * 128,
                                         ci * 512:ci * 512 + 512], sx)
            t_ln = emit_ln(b, "mn", XRC, XLC, after=[t_in], mcw=512,
                           dtiles=8, dmodel=1024.0, wfree=512)
            for ot in range(24):
                def evi(psb, tmm, mc, ot=ot):
                    b.wait("scalar", tmm)
                    if ot < 16:
                        b.wait("scalar", t_carry[ot])
                        return b.op("scalar", lambda e, ot=ot, psb=psb:
                                    e.activation(
                            XHC[:, ot * 516 + 3:ot * 516 + 515],
                            psb[:, 0:512], AF.Copy, bias=0.0, scale=1.0),
                            ss_)
                    return b.op("scalar", lambda e, ot=ot, psb=psb:
                                e.activation(
                        ZHC[:, (ot - 16) * 512:(ot - 15) * 512],
                        psb[:, 0:512], AF.Copy, bias=0.0, scale=1.0), ss_)
                lin.otile(io["m0_inwT"], 8, ot,
                          lambda kt, mc: XLC[:, kt * 512:(kt + 1) * 512],
                          evi, rhs_ready=t_ln if ot == 0 else None, n_mc=1)
            for ti in range(16):
                b.wait("vector", (b.sems["s"], b.sems["s"].n))
                t_c0 = b.op("vector", lambda e, ti=ti: e.tensor_scalar(
                    XCF[:, :], XHC[:, ti * 516:ti * 516 + 512],
                    CW[:, ti * 4:ti * 4 + 1], None, ALU.mult), sv)
                for k_ in range(1, 4):
                    t_c0 = b.op("vector", lambda e, ti=ti, k_=k_:
                                e.scalar_tensor_tensor(
                        XCF[:, :], XHC[:, ti * 516 + k_:ti * 516 + k_ + 512],
                        CW[:, ti * 4 + k_:ti * 4 + k_ + 1], XCF[:, :],
                        ALU.mult, ALU.add), sv)
                b.wait("scalar", t_c0)
                t_xc[ti] = b.op("scalar", lambda e, ti=ti: e.activation(
                    XC[:, ti * 512:(ti + 1) * 512], XCF[:, :], AF.Silu,
                    bias=CB[:, ti:ti + 1], scale=1.0), ss_)
            b.wait("tensor", later(*t_xc))
            b.wait("tensor", later(t_psa_read, t_psb_read, t_psc_read))
            t_ma = t_mb = t_mc = None
            for kt in range(16):
                t_ma = b.op("tensor", lambda e, kt=kt: e.matmul(
                    pool["a"][0:64, 0:512], XPJ[:, kt * 96:kt * 96 + 64],
                    XC[:, kt * 512:(kt + 1) * 512],
                    start=(kt == 0), stop=(kt == 15)), st)
            for kt in range(16):
                t_mb = b.op("tensor", lambda e, kt=kt: e.matmul(
                    pool["b"][0:16, 0:512], XPJ[:, kt * 96 + 64:kt * 96 + 80],
                    XC[:, kt * 512:(kt + 1) * 512],
                    start=(kt == 0), stop=(kt == 15)), st)
            for kt in range(16):
                t_mc = b.op("tensor", lambda e, kt=kt: e.matmul(
                    pool["c"][0:16, 0:512], XPJ[:, kt * 96 + 80:kt * 96 + 96],
                    XC[:, kt * 512:(kt + 1) * 512],
                    start=(kt == 0), stop=(kt == 15)), st)
            b.wait("scalar", later(t_ma, t_prj_read))
            t_prjb = b.op("scalar", lambda e: e.activation(
                PRJB[:, :], pool["a"][0:64, 0:512], AF.Copy, bias=0.0,
                scale=1.0), ss_)
            b.wait("scalar", later(t_mb, t_mc, t_bps_read))
            b.op("scalar", lambda e: e.activation(
                BPS[0:16, 0:512], pool["b"][0:16, 0:512], AF.Copy, bias=0.0,
                scale=1.0), ss_)
            t_bps = b.op("scalar", lambda e: e.activation(
                BPS[0:16, 512:1024], pool["c"][0:16, 0:512], AF.Copy,
                bias=0.0, scale=1.0), ss_)
            t_psa_read = t_prjb
            for ct in range(8):
                b.wait("tensor", later(t_prjb, t_psd_read))
                t_md = None
                for _ in range(1):
                    t_md = b.op("tensor", lambda e, ct=ct: e.matmul(
                        pool["d"][:, 0:512], DTWB[0:64, ct * 128:ct * 128 + 128],
                        PRJB[:, :], start=True, stop=True), st)
                t_prj_read = t_md
                b.wait("scalar", later(t_md, t_dl_read))
                t_x1 = b.op("scalar", lambda e, ct=ct: e.activation(
                    DL1[:, :], pool["d"][:, 0:512], AF.Identity,
                    bias=DTB[:, ct:ct + 1], scale=1.0), ss_)
                t_x2 = b.op("scalar", lambda e, ct=ct: e.activation(
                    DL2[:, :], pool["d"][:, 0:512], AF.Abs,
                    bias=DTB[:, ct:ct + 1], scale=1.0), ss_)
                t_psd_read = later(t_x1, t_x2)
                t_e = b.op("scalar", lambda e: e.activation(
                    DL3[:, :], DL2[:, :], AF.Exp, bias=0.0,
                    scale=negone[:, 0:1]), ss_)
                t_l = b.op("scalar", lambda e: e.activation(
                    DL2[:, :], DL3[:, :], AF.Ln, bias=1.0, scale=1.0), ss_)
                b.wait("vector", t_x1)
                b.op("vector", lambda e: e.tensor_scalar(
                    DL1[:, :], DL1[:, :], 0.0, None, ALU.max), sv)
                b.wait("vector", t_l)
                t_dlt = b.op("vector", lambda e: e.tensor_tensor(
                    DLT[:, :], DL1[:, :], DL2[:, :], ALU.add), sv)
                t_dl_read = t_dlt
                t_u = b.op("vector", lambda e, ct=ct: e.tensor_tensor(
                    U[:, :], DLT[:, :], XC[:, ct * 512:(ct + 1) * 512],
                    ALU.mult), sv)
                t_y = b.op("vector", lambda e, ct=ct: e.tensor_scalar(
                    Y[:, :], XC[:, ct * 512:(ct + 1) * 512],
                    DCOL[:, ct:ct + 1], None, ALU.mult), sv)
                for n in range(16):
                    b.wait("tensor", later(t_bps, t_psc_read, t_u))
                    t_br = b.op("tensor", lambda e, n=n: e.matmul(
                        pool["c"][:, 0:512], OHB[:, n * 128:(n + 1) * 128],
                        BPS[0:16, 0:512], start=True, stop=True), st)
                    t_bps_read = t_br
                    b.wait("vector", t_br)
                    t_dbu = b.op("vector", lambda e: e.tensor_tensor(
                        DBU[:, :], U[:, :], pool["c"][:, 0:512], ALU.mult),
                        sv)
                    t_psc_read = t_dbu
                    b.wait("scalar", later(t_dlt, t_da_read))
                    t_da = b.op("scalar", lambda e, ct=ct, n=n: e.activation(
                        DA[:, :], DLT[:, :], AF.Exp, bias=0.0,
                        scale=ACOL[:, ct * 16 + n:ct * 16 + n + 1]), ss_)
                    b.wait("vector", t_da)
                    t_scan = b.op("vector", lambda e, ct=ct, n=n:
                                  e.tensor_tensor_scan(
                        HS[:, :], DA[:, :], DBU[:, :],
                        HCAR[:, ct * 16 + n:ct * 16 + n + 1],
                        ALU.mult, ALU.add), sv)
                    t_da_read = t_scan
                    b.wait("tensor", later(t_scan, t_psb_read))
                    t_cr = b.op("tensor", lambda e, n=n: e.matmul(
                        pool["b"][:, 0:512], OHB[:, n * 128:(n + 1) * 128],
                        BPS[0:16, 512:1024], start=True, stop=True), st)
                    b.wait("vector", t_cr)
                    t_ty = b.op("vector", lambda e: e.tensor_tensor(
                        TY[:, :], HS[:, :], pool["b"][:, 0:512], ALU.mult),
                        sv)
                    t_psb_read = t_ty
                    t_y = b.op("vector", lambda e: e.tensor_tensor(
                        Y[:, :], Y[:, :], TY[:, :], ALU.add), sv)
                    b.op("vector", lambda e, ct=ct, n=n: e.tensor_copy(
                        HCAR[:, ct * 16 + n:ct * 16 + n + 1],
                        HS[:, 511:512]), sv)
                b.wait("scalar", t_y)
                t_sz = b.op("scalar", lambda e, ct=ct: e.activation(
                    SZ[:, :], ZHC[:, ct * 512:(ct + 1) * 512], AF.Silu,
                    bias=0.0, scale=1.0), ss_)
                b.wait("vector", t_sz)
                t_yb[ct] = b.op("vector", lambda e, ct=ct: e.tensor_tensor(
                    YB[:, ct * 512:(ct + 1) * 512], Y[:, :], SZ[:, :],
                    ALU.mult), sv)
            for ot in range(8):
                pp = ot % 2
                psb = pool["e"] if pp == 0 else pool["f"]
                b.wait("tensor", later(t_ev_ow[pp], *t_yb))
                t_mo = None
                for kt in range(8):
                    t_mo = b.op("tensor", lambda e, kt=kt, ot=ot, psb=psb:
                                e.matmul(
                        psb[:, 0:512],
                        OWB[:, (kt * 8 + ot) * 128:(kt * 8 + ot) * 128 + 128],
                        YB[:, kt * 512:(kt + 1) * 512],
                        start=(kt == 0), stop=(kt == 7)), st)
                b.wait("scalar", later(t_mo, t_qfm[0]))
                t_ev = b.op("scalar", lambda e, psb=psb: e.activation(
                    QFM[:, :], psb[:, 0:512], AF.Copy, bias=0.0, scale=1.0),
                    ss_)
                t_ev_ow[pp] = t_ev
                b.wait("gpsimd", t_ev)
                t_qfm[0] = b.dma("gpsimd",
                                 yloc[ot * 128:(ot + 1) * 128,
                                      ci * 512:ci * 512 + 512], QFM[:, :],
                                 sd)
            b.barrier()
        t_bl = None
        for tc2 in range(4):
            for ot in range(8):
                b.wait("sync", t_bl)
                b.dma("sync", BL1[:, :],
                      yloc[ot * 128:(ot + 1) * 128,
                           tc2 * 2048:tc2 * 2048 + 2048], sw)
                t_l2 = b.dma("sync", BL2[:, :],
                             yloc[ot * 128:(ot + 1) * 128,
                                  S_ - tc2 * 2048 - 2048:S_ - tc2 * 2048],
                             sw)
                b.wait("vector", t_l2)
                b.op("vector", lambda e: e.tensor_copy(
                    XRC[:, 0:2048], BL2[:, ::-1]), sv)
                b.op("vector", lambda e: e.tensor_tensor(
                    BL2[:, :], XRC[:, 0:2048], BL1[:, :], ALU.subtract), sv)
                t_bd = b.op("vector", lambda e: e.scalar_tensor_tensor(
                    BL1[:, :], BL2[:, :], FLC[:, 0:1], BL1[:, :],
                    ALU.mult, ALU.add), sv)
                b.wait("gpsimd", t_bd)
                t_bl = b.dma("gpsimd",
                             rs_in.ap()[tc2 * 1024 + ot * 128:
                                        tc2 * 1024 + (ot + 1) * 128, :],
                             BL1[:, :], sd)
        b.barrier(["gpsimd"])
        b.cc("ReduceScatter", ALU.add, [[0, 1, 2, 3], [4, 5, 6, 7]],
             [rs_in.ap().opt()], [rs_out.ap().opt()], scc)
        b.barrier()
        t_out = None
        t_pr = None
        for d in range(8):
            b.wait("sync", later(t_pr, t_out))
            b.dma("sync", BL1[:, :], io["xin"][d * 128:(d + 1) * 128, :], sw)
            t_r = b.dma("sync", BL2[:, :],
                        rs_out.ap()[d * 128:(d + 1) * 128, :], sw)
            b.wait("vector", t_r)
            t_pr = b.op("vector", lambda e: e.tensor_tensor(
                BL1[:, :], BL1[:, :], BL2[:, :], ALU.add), sv)
            b.wait("gpsimd", t_pr)
            t_out = b.dma("gpsimd", io["xout"][d * 128:(d + 1) * 128, :],
                          BL1[:, :], sd)
        dbg_names = []
        if DEBUG_TAPS:
            b.barrier(["sync"])
            for i_, (tn, plo, phi, clo, chi) in enumerate(DEBUG_TAPS):
                t = b.tiles[tn]
                dn = f"dbg{i_}_{tn}"
                od = nc.dram_tensor(dn, [phi - plo, chi - clo],
                                    t.ap().dtype, kind="ExternalOutput").ap()
                t_out = b.dma("sync", od, t[plo:phi, clo:chi], so)
                dbg_names.append(dn)
        b.wait("sync", t_out)
        b.finalize()
    maps = []
    for c in range(NCORES):
        m = {"xin": xchunks[c]}
        for k in wkeys:
            m[k] = in_maps[c][k]
        maps.append(m)
    res = run_bass_kernel_spmd(nc, maps, core_ids=list(range(NCORES)),
                               trace=trace)
    if DEBUG_TAPS:
        run_l0attn.dbg = [{dn: res.results[c][dn] for dn in dbg_names}
                          for c in range(NCORES)]
    return [res.results[c]["xout"] for c in range(NCORES)]


def _launch_block(name, emit_fn, weight_keys, in_maps, xchunks,
                  extra_loads=(), trace=False, ln_mode="xl"):
    nc = bass.Bass("TRN2", target_bir_lowering=False, debug=False)
    io = {}
    io["xin"] = nc.dram_tensor("xin", [D_, MTOK], F32, kind="ExternalInput").ap()
    for k in weight_keys:
        a = in_maps[0][k]
        dt_ = BF16 if a.dtype == BFNP else F32
        io[k] = nc.dram_tensor(k, list(a.shape), dt_, kind="ExternalInput").ap()
    io["xout"] = nc.dram_tensor("xout", [D_, MTOK], F32,
                                kind="ExternalOutput").ap()
    with ExitStack() as stack:
        b = Bld(nc, stack)
        xr = b.sbuf("XR", [128, DT * W_], F32)
        xl = b.sbuf("XL", [128, DT * W_], BF16) if ln_mode == "xl" else None
        pool = {k: b.psum("P" + k, [128, 512], F32) for k in "abcdef"}
        pool["r2"] = b.psum("Pr2", [128, 1024], F32)
        b.pool = pool
        sx = b.sem("x")
        t_in = None
        for d in range(DT):
            t_in = b.dma("sync", xr[:, d * W_:(d + 1) * W_],
                         io["xin"][d * 128:(d + 1) * 128, :], sx)
        if ln_mode == "ab":
            ab = b.sbuf("LNAB", [1, 2 * W_], F32)
            t_ln = emit_ln(b, "ln", xr, None, after=[t_in], ab_out=ab,
                           write_xl=False)
        else:
            t_ln = emit_ln(b, "ln", xr, xl, after=[t_in])
        b.barrier()
        t_done = emit_fn(b, io, xr, xl, t_ln)
        b.wait("sync", t_done)
        for snm in ("v", "s", "t"):
            if snm in b.sems:
                b.wait("sync", (b.sems[snm], b.sems[snm].n))
        so = b.sem("o")
        t_out = None
        for d in range(DT):
            t_out = b.dma("sync", io["xout"][d * 128:(d + 1) * 128, :],
                          xr[:, d * W_:(d + 1) * W_], so)
        dbg_names = []
        if DEBUG_TAPS:
            b.barrier(["sync"])
            for i, (tn, plo, phi, clo, chi) in enumerate(DEBUG_TAPS):
                t = b.tiles[tn]
                dn = f"dbg{i}_{tn}"
                dt_ = t.ap().dtype
                od = nc.dram_tensor(dn, [phi - plo, chi - clo], dt_,
                                    kind="ExternalOutput").ap()
                t_out = b.dma("sync", od, t[plo:phi, clo:chi], so)
                dbg_names.append(dn)
        b.wait("sync", t_out)
        b.finalize()
    maps = []
    for c in range(NCORES):
        m = {"xin": xchunks[c]}
        for k in weight_keys:
            m[k] = in_maps[c][k]
        maps.append(m)
    res = run_bass_kernel_spmd(nc, maps, core_ids=list(range(NCORES)),
                               trace=trace)
    if DEBUG_TAPS:
        _launch_block.dbg = [{dn: res.results[c][dn] for dn in dbg_names}
                             for c in range(NCORES)]
    return [res.results[c]["xout"] for c in range(NCORES)]


def run_l0ffn(in_maps, xchunks):
    keys = ["f0_w1T", "f0_w2T", "f0_h1T", "f0_h2T", "f0_w3T",
            "f0_s1", "f0_s2", "f0_h1s", "f0_h2s", "f0_s3"]
    return _launch_block("l0ffn", emit_l0ffn, keys, in_maps, xchunks)


def run_l1attn(in_maps, xchunks):
    keys = ["a1_WqT", "a1_WkT", "a1_WvT", "a1_WoT", "a1_bq", "a1_bo",
            "a1_bk", "a1_bv"]
    return _launch_block("l1attn", emit_l1attn, keys, in_maps, xchunks)


def run_l1ffn(in_maps, xchunks):
    keys = ["m1_W1", "m1_W2", "m1_WrP", "m1_br", "m1_b1pc", "m1_b2pc"]
    return _launch_block("l1ffn", emit_l1ffn, keys, in_maps, xchunks,
                         ln_mode="ab")


def run_l2ffn(in_maps, xchunks):
    keys = ["m2_phi", "eye128b", "m2_Wc", "m2_bnc", "m2_maskc"]
    return _launch_block("l2ffn", emit_l2ffn, keys, in_maps, xchunks)


def run_l2attn(in_maps, xchunks):
    keys = ["a2_wqT", "a2_wkT", "a2_wvT", "a2_woPT", "a2_bq", "a2_bkc",
            "a2_bo", "a2_bv", "a2_ropec", "a2_ropes", "a2_eye64"]
    return _launch_block("l2attn", emit_l2attn, keys, in_maps, xchunks)


def run_l3attn(in_maps, xchunks):
    keys = ["a3_wqkvT", "a3_woT", "a3_bout"]
    return _launch_block("l3attn", emit_l3attn, keys, in_maps, xchunks)


def run_l3ffn(in_maps, xchunks):
    keys = ["f3_C", "f3_S", "f3_WrT", "f3_WiT", "f3_bias"]
    return _launch_block("l3ffn", emit_l3ffn, keys, in_maps, xchunks)


def kernel(**inputs):
    in_maps = prep_inputs(inputs)
    x0 = np.asarray(inputs["L0_attn_x"], np.float32)
    x0T = np.ascontiguousarray(x0.transpose(0, 2, 1))
    chs = [np.ascontiguousarray(
        x0T[c // 4][:, MTOK * (c % 4):MTOK * (c % 4 + 1)])
        for c in range(NCORES)]
    chs = run_l0attn(in_maps, chs)
    chs = run_l0ffn(in_maps, chs)
    chs = run_l1attn(in_maps, chs)
    chs = run_l1ffn(in_maps, chs)
    chs = run_l2attn(in_maps, chs)
    chs = run_l2ffn(in_maps, chs)
    chs = run_l3attn(in_maps, chs)
    chs = run_l3ffn(in_maps, chs)
    out = np.empty((2, 8192, 1024), np.float32)
    for c in range(NCORES):
        out[c // 4, MTOK * (c % 4):MTOK * (c % 4 + 1)] = \
            np.asarray(chs[c], np.float32).T
    return out


# version 02
# speedup vs baseline: 1.0174x; 1.0174x over previous
import numpy as np
import ml_dtypes
import concourse.bass as bass
import concourse.mybir as mybir
from concourse.bass_utils import run_bass_kernel_spmd
from contextlib import ExitStack

F32 = mybir.dt.float32
BF16 = mybir.dt.bfloat16
AF = mybir.ActivationFunctionType
ALU = mybir.AluOpType
AX = mybir.AxisListType
BFNP = ml_dtypes.bfloat16

NCORES = 8
B_, S_, D_ = 2, 8192, 1024
MTOK = 2048
DT = 8
W_ = MTOK
ENGINES = ("sync", "vector", "scalar", "tensor", "gpsimd")


class Sem:
    def __init__(self, h):
        self.h = h
        self.n = 0


class Bld:
    def __init__(self, nc, stack):
        self.nc = nc
        self.stack = stack
        self.q = {e: [] for e in ENGINES}
        self.sems = {}
        self.tiles = {}

    def sem(self, name):
        if name not in self.sems:
            h = self.stack.enter_context(self.nc.semaphore(name))
            self.sems[name] = Sem(h)
        return self.sems[name]

    def sbuf(self, name, shape, dt):
        if name in self.tiles:
            return self.tiles[name]
        t = self.stack.enter_context(self.nc.sbuf_tensor(name, list(shape), dt))
        self.tiles[name] = t
        return t

    def psum(self, name, shape, dt=F32):
        t = self.stack.enter_context(self.nc.psum_tensor(name, list(shape), dt))
        self.tiles[name] = t
        return t

    def on(self, eng, fn):
        self.q[eng].append(fn)

    def wait(self, eng, tok):
        if tok is None:
            return
        if isinstance(tok, list):
            for t in tok:
                self.wait(eng, t)
            return
        sem, n = tok
        if n <= 0:
            return
        assert n <= sem.n, f"wait {eng} for {n} but only {sem.n} emitted"
        self.on(eng, lambda e, s=sem.h, v=n: e.wait_ge(s, v))

    def dma(self, eng, out, in_, sem):
        sem.n += 16
        n = sem.n
        self.on(eng, lambda e, o=out, i=in_, s=sem.h: e.dma_start(out=o, in_=i)
                .then_inc(s, 16))
        return (sem, n)

    def op(self, eng, fn, sem=None):
        if sem is None:
            self.on(eng, fn)
            return None
        sem.n += 1
        n = sem.n
        self.on(eng, lambda e, f=fn, s=sem.h: f(e).then_inc(s, 1))
        return (sem, n)

    def cc(self, kind, op, groups, ins, outs, sem):
        sem.n += 1
        n = sem.n
        self.on("gpsimd", lambda e, s=sem.h: e.collective_compute(
            kind, op, replica_groups=groups, ins=ins, outs=outs).then_inc(s))
        return (sem, n)

    def barrier(self, engines=ENGINES):
        toks = [(s, s.n) for s in self.sems.values() if s.n > 0]
        for eng in engines:
            for t in toks:
                self.wait(eng, t)

    def finalize(self):
        block = self.stack.enter_context(self.nc.Block())
        for name in ENGINES:
            ops = self.q[name]
            if not ops:
                continue
            deco = getattr(block, name)

            def body(e, ops=ops):
                for fn in ops:
                    fn(e)
            deco(body)


def later(*toks):
    best = {}
    for t in toks:
        if t is None:
            continue
        if isinstance(t, list):
            for u in t:
                if u is None:
                    continue
                s, n = u
                if s not in best or best[s] < n:
                    best[s] = n
            continue
        s, n = t
        if s not in best or best[s] < n:
            best[s] = n
    return [(s, n) for s, n in best.items()]


def _t(a):
    return np.ascontiguousarray(np.asarray(a, np.float32).T)


def _pcol(v, p=128):
    v = np.asarray(v, np.float32)
    n = max(1, v.shape[0] // p)
    return np.ascontiguousarray(v.reshape(n, p).T)


def prep_inputs(inp):
    g = {}
    pc = [{} for _ in range(NCORES)]

    x0 = np.asarray(inp["L0_attn_x"], np.float32)
    x0T = np.ascontiguousarray(x0.transpose(0, 2, 1))

    for c in range(NCORES):
        b, q = c // 4, c % 4
        pc[c]["x0chunk"] = np.ascontiguousarray(
            x0T[b][:, MTOK * q:MTOK * (q + 1)])

    ohm = np.zeros((16, 16 * 128), np.float32)
    for n in range(16):
        ohm[n, n * 128:(n + 1) * 128] = 1.0
    g["m0_oh"] = ohm.astype(BFNP)
    for c in range(NCORES):
        bb, dr, hf = c // 4, (c % 4) // 2, c % 2
        oh_ = 1 - hf
        p = "f" if dr == 0 else "b"
        xt = x0T[bb]
        if dr == 1:
            xt = xt[:, ::-1]
        pc[c]["m0_xT"] = np.ascontiguousarray(xt)
        inw = np.asarray(inp[f"L0_attn_{p}_in_w"], np.float32)
        pc[c]["m0_inwT"] = np.ascontiguousarray(np.concatenate(
            [inw[1024 * hf:1024 * hf + 1024],
             inw[1024 * oh_:1024 * oh_ + 1024],
             inw[2048 + 1024 * hf:2048 + 1024 * hf + 1024]],
            0).T)
        cw = np.asarray(inp[f"L0_attn_{p}_conv_w"], np.float32)
        cw = np.concatenate([cw[1024 * hf:1024 * hf + 1024],
                             cw[1024 * oh_:1024 * oh_ + 1024]], 0)
        pc[c]["m0_convw"] = np.ascontiguousarray(
            cw.reshape(16, 128, 4).transpose(1, 0, 2).reshape(128, 64))
        cb = np.asarray(inp[f"L0_attn_{p}_conv_b"], np.float32)
        cb = np.concatenate([cb[1024 * hf:1024 * hf + 1024],
                             cb[1024 * oh_:1024 * oh_ + 1024]], 0)
        pc[c]["m0_convb"] = _pcol(cb)
        xp = np.asarray(inp[f"L0_attn_{p}_xproj_w"], np.float32)
        xp = np.concatenate([xp[:, 1024 * hf:1024 * hf + 1024],
                             xp[:, 1024 * oh_:1024 * oh_ + 1024]], 1)
        pc[c]["m0_xprojT"] = np.ascontiguousarray(xp.T)
        dtw = np.asarray(inp[f"L0_attn_{p}_dt_w"], np.float32)
        pc[c]["m0_dtwT"] = np.ascontiguousarray(
            dtw[1024 * hf:1024 * hf + 1024].T)
        pc[c]["m0_dtb"] = _pcol(
            np.asarray(inp[f"L0_attn_{p}_dt_b"], np.float32)
            [1024 * hf:1024 * hf + 1024])
        al = np.asarray(inp[f"L0_attn_{p}_A_log"], np.float32)
        pc[c]["m0_Alog"] = np.ascontiguousarray(
            al[1024 * hf:1024 * hf + 1024].reshape(8, 128, 16)
            .transpose(1, 0, 2).reshape(128, 128))
        pc[c]["m0_D"] = _pcol(
            np.asarray(inp[f"L0_attn_{p}_D"], np.float32)
            [1024 * hf:1024 * hf + 1024])
        ow = np.asarray(inp[f"L0_attn_{p}_out_w"], np.float32)
        pc[c]["m0_outwT"] = np.ascontiguousarray(
            ow.T[1024 * hf:1024 * hf + 1024])
        pc[c]["m0_flip"] = np.full((128, 1), float(dr), np.float32)

    g["f0_w1T"] = _t(inp["L0_ffn_w1_w"])
    g["f0_w2T"] = _t(inp["L0_ffn_w2_w"])
    g["f0_h1T"] = _t(inp["L0_ffn_h1_w"])
    g["f0_h2T"] = _t(inp["L0_ffn_h2_w"])
    g["f0_w3T"] = _t(inp["L0_ffn_w3_w"])
    g["f0_s1"] = _pcol(inp["L0_ffn_w1_s"])
    g["f0_s2"] = _pcol(inp["L0_ffn_w2_s"])
    g["f0_h1s"] = _pcol(inp["L0_ffn_h1_s"])
    g["f0_h2s"] = _pcol(inp["L0_ffn_h2_s"])
    g["f0_s3"] = _pcol(inp["L0_ffn_w3_s"])

    g["a1_WqT"] = _t(inp["L1_attn_Wq"])
    g["a1_WkT"] = _t(inp["L1_attn_Wk"])
    g["a1_WvT"] = _t(inp["L1_attn_Wv"])
    g["a1_WoT"] = _t(inp["L1_attn_Wo"])
    g["a1_bq"] = _pcol(inp["L1_attn_bq"])
    g["a1_bo"] = _pcol(inp["L1_attn_bo"])
    g["a1_bk"] = np.asarray(inp["L1_attn_bk"], np.float32).reshape(1, 1024)
    g["a1_bv"] = np.asarray(inp["L1_attn_bv"], np.float32).reshape(1, 1024)

    g["m1_W1"] = np.ascontiguousarray(np.asarray(inp["L1_ffn_W1"], np.float32))
    g["m1_W2"] = np.ascontiguousarray(np.asarray(inp["L1_ffn_W2"], np.float32))
    wr = np.asarray(inp["L1_ffn_Wr"], np.float32)
    g["m1_WrP"] = np.ascontiguousarray(
        wr.reshape(8, 128, 8).transpose(1, 0, 2).reshape(128, 64))
    g["m1_br"] = np.asarray(inp["L1_ffn_br"], np.float32).reshape(1, 8)
    b1 = np.asarray(inp["L1_ffn_b1"], np.float32)
    g["m1_b1pc"] = np.ascontiguousarray(np.concatenate(
        [b1[e].reshape(32, 128).T for e in range(8)], axis=1))
    b2 = np.asarray(inp["L1_ffn_b2"], np.float32)
    g["m1_b2pc"] = np.ascontiguousarray(np.concatenate(
        [b2[e].reshape(8, 128).T for e in range(8)], axis=1))

    g["a2_wqT"] = _t(inp["L2_attn_wq"])
    g["a2_wkT"] = _t(inp["L2_attn_wk"])
    g["a2_wvT"] = _t(inp["L2_attn_wv"])
    g["a2_eye64"] = np.eye(64, dtype=np.float32).astype(BFNP)
    a2woT = _t(inp["L2_attn_wo"])
    for c in range(NCORES):
        q = c % 4
        pc[c]["a2_woPT"] = np.ascontiguousarray(a2woT[256 * q:256 * (q + 1)])
    g["a2_bq"] = _pcol(inp["L2_attn_bq"])
    g["a2_bo"] = _pcol(inp["L2_attn_bo"])
    g["a2_bkc"] = _pcol(inp["L2_attn_bk"])
    g["a2_bv"] = np.asarray(inp["L2_attn_bv"], np.float32).reshape(1, 1024)
    c64 = np.asarray(inp["L2_attn_cos"], np.float32)[3199]
    s64 = np.asarray(inp["L2_attn_sin"], np.float32)[3199]
    ss = s64.copy()
    ss[0::2] *= -1.0
    g["a2_ropec"] = np.ascontiguousarray(np.tile(c64, 2)[:, None])
    g["a2_ropes"] = np.ascontiguousarray(np.tile(ss, 2)[:, None])

    g["m2_phi"] = np.ascontiguousarray(
        np.asarray(inp["L2_ffn_phi"], np.float32).reshape(1024, 2048))
    g["eye128b"] = np.eye(128, dtype=np.float32).astype(BFNP)
    w2f = np.asarray(inp["L2_ffn_W"], np.float32)
    b2f = np.asarray(inp["L2_ffn_b"], np.float32)
    for c in range(NCORES):
        q = c % 4
        pc[c]["m2_Wc"] = np.ascontiguousarray(w2f[16 * q:16 * (q + 1)])
        pc[c]["m2_bnc"] = np.ascontiguousarray(b2f[16 * q:16 * (q + 1)])
        onehot = np.zeros(8, np.float32)
        onehot[c ^ 4] = 1.0
        pc[c]["m2_maskc"] = np.ascontiguousarray(
            np.tile(onehot, (128, 1)))

    g["a3_wqkvT"] = _t(inp["L3_attn_w_qkv"])
    g["a3_woT"] = _t(inp["L3_attn_w_out"])
    g["a3_bout"] = _pcol(inp["L3_attn_b_out"])

    dk = np.outer(np.arange(1024), np.arange(1024)) % 1024
    ang = 2.0 * np.pi * dk.astype(np.float64) / 1024.0
    g["f3_C"] = np.cos(ang).astype(np.float32)
    g["f3_S"] = np.sin(ang).astype(np.float32)
    g["f3_WrT"] = _t(inp["L3_ffn_w_freq_real"])
    g["f3_WiT"] = _t(inp["L3_ffn_w_freq_imag"])
    g["f3_bias"] = _pcol(inp["L3_ffn_bias"])

    in_maps = []
    for c in range(NCORES):
        m = dict(g)
        m.update(pc[c])
        in_maps.append(m)
    return in_maps


def emit_ln(b, pref, xr, xl, after=(), mcw=512, dtiles=DT, dmodel=1024.0,
            wfree=W_, ab_out=None, write_xl=True):
    mc_n = wfree // mcw
    ones = b.sbuf(pref + "ones", [128, 1], F32)
    onesb = b.sbuf(pref + "onesb", [128, 1], BF16)
    onesr = b.sbuf(pref + "onesr", [1, 128], F32)
    x2 = b.sbuf(pref + "x2", [128, 2 * mcw], BF16)
    stat = b.sbuf(pref + "stat", [1, 3 * mcw], F32)
    rep = b.sbuf(pref + "rep", [128, 2 * mcw], F32) if write_xl else None
    ps_s = b.pool["e"][0:1, 0:mcw]
    ps_q = b.pool["f"][0:1, 0:mcw]
    ps_r = b.pool["r2"]
    sv = b.sem("v")
    ss = b.sem("s")
    st = b.sem("t")

    epsb = b.sbuf(pref + "epsb", [1, 1], F32)
    b.op("vector", lambda e: e.memset(ones[:, :], 1.0), sv)
    b.op("vector", lambda e: e.memset(onesb[:, :], 1.0), sv)
    t_eps = b.op("vector", lambda e: e.memset(epsb[:, :], 1e-5), sv)
    b.wait("scalar", t_eps)
    t_ones = b.op("vector", lambda e: e.memset(onesr[:, :], 1.0), sv)
    b.wait("tensor", t_ones)
    b.wait("scalar", later(*after))
    b.wait("tensor", later(*after))

    t_sq_read = None
    t_x2mm = [None, None]
    t_stat_read = None
    t_rep_read = None
    t_evac = None
    t_last = None
    for mc in range(mc_n):
        o = mc * mcw

        def xt(d, o=o):
            return xr[:, d * wfree + o:d * wfree + o + mcw]

        b.wait("tensor", t_stat_read)
        for d in range(dtiles):
            b.op("tensor", lambda e, d=d, xt=xt: e.matmul(
                ps_s, ones[:, :], xt(d),
                start=(d == 0), stop=(d == dtiles - 1)), None)
        for d in range(dtiles):
            p = d % 2
            b.wait("scalar", t_x2mm[p])
            t_sq = b.op("scalar", lambda e, d=d, p=p, xt=xt: e.activation(
                x2[:, p * mcw:(p + 1) * mcw], xt(d), AF.Square,
                bias=0.0, scale=1.0), ss)
            b.wait("tensor", t_sq)
            t_x2mm[p] = b.op("tensor", lambda e, d=d, p=p: e.matmul(
                ps_q, onesb[:, :], x2[:, p * mcw:(p + 1) * mcw],
                start=(d == 0), stop=(d == dtiles - 1)), st)
        t_sq_read = t_x2mm[(dtiles - 1) % 2]
        b.wait("vector", t_sq_read)
        b.op("vector", lambda e: e.tensor_scalar(
            stat[0:1, 0:mcw], ps_s, 1.0 / dmodel, None, ALU.mult), None)
        b.op("vector", lambda e: e.tensor_scalar(
            stat[0:1, mcw:2 * mcw], ps_q, 1.0 / dmodel, None, ALU.mult), None)
        b.op("vector", lambda e: e.tensor_tensor(
            stat[0:1, 2 * mcw:3 * mcw], stat[0:1, 0:mcw], stat[0:1, 0:mcw], ALU.mult), None)
        t_stat_read = b.op("vector", lambda e: e.tensor_tensor(
            stat[0:1, mcw:2 * mcw], stat[0:1, mcw:2 * mcw], stat[0:1, 2 * mcw:3 * mcw], ALU.subtract), sv)
        b.wait("scalar", t_stat_read)
        t_sqrt = b.op("scalar", lambda e: e.activation(
            stat[0:1, mcw:2 * mcw], stat[0:1, mcw:2 * mcw], AF.Sqrt, bias=epsb[0:1, 0:1],
            scale=1.0), ss)
        b.wait("vector", t_sqrt)
        b.op("vector", lambda e: e.reciprocal(stat[0:1, mcw:2 * mcw], stat[0:1, mcw:2 * mcw]),
             None)
        t_ab = b.op("vector", lambda e: e.scalar_tensor_tensor(
            stat[0:1, 0:mcw], stat[0:1, 0:mcw], -1.0, stat[0:1, mcw:2 * mcw],
            ALU.mult, ALU.mult), sv)
        if ab_out is not None:
            b.op("vector", lambda e, o=o: e.tensor_copy(
                ab_out[0:1, o:o + mcw], stat[0:1, mcw:2 * mcw]), None)
            t_ab = b.op("vector", lambda e, o=o: e.tensor_copy(
                ab_out[0:1, wfree + o:wfree + o + mcw], stat[0:1, 0:mcw]), sv)
        if not write_xl:
            t_stat_read = t_ab
            t_last = t_ab
            continue
        b.wait("tensor", later(t_ab, t_evac))
        b.op("tensor", lambda e: e.matmul(
            ps_r[:, 0:mcw], onesr[:, :], stat[0:1, mcw:2 * mcw],
            start=True, stop=True), None)
        t_repmm = b.op("tensor", lambda e: e.matmul(
            ps_r[:, mcw:2 * mcw], onesr[:, :], stat[0:1, 0:mcw],
            start=True, stop=True), st)
        b.wait("scalar", later(t_repmm, t_rep_read))
        b.op("scalar", lambda e: e.activation(
            rep[:, 0:mcw], ps_r[:, 0:mcw], AF.Copy, bias=0.0, scale=1.0),
            None)
        t_evac = b.op("scalar", lambda e: e.activation(
            rep[:, mcw:2 * mcw], ps_r[:, mcw:2 * mcw], AF.Copy,
            bias=0.0, scale=1.0), ss)
        b.wait("vector", t_evac)
        for d in range(dtiles):
            b.op("vector", lambda e, d=d, o=o, xt=xt: e.tensor_tensor(
                xl[:, d * wfree + o:d * wfree + o + mcw], xt(d),
                rep[:, 0:mcw], ALU.mult), None)
            t_last = b.op("vector", lambda e, d=d, o=o: e.tensor_tensor(
                xl[:, d * wfree + o:d * wfree + o + mcw],
                xl[:, d * wfree + o:d * wfree + o + mcw],
                rep[:, mcw:2 * mcw], ALU.add), sv)
        t_rep_read = t_last
    return t_last


class Lin:

    def __init__(self, b, pref, kt_n, psums, single_wb=False):
        self.b = b
        self.ws = b.sbuf(pref + "ws", [128, kt_n * 128], F32)
        self.n_wb = 1 if single_wb else 2
        self.wb = b.sbuf(pref + "wb", [128, self.n_wb * kt_n * 128], BF16)
        self.kt_max = kt_n
        self.psums = psums
        self.t_cast = None
        self.t_mm = [None, None]
        self.t_bank = [None] * len(psums)
        self.bank_i = 0
        self.oi = 0

    def otile(self, wdram, kt_n, ot, rhs_fn, evac_fn, ncol=512,
              rhs_ready=None, wait_extra=None, n_mc=1):
        b = self.b
        p = self.oi % self.n_wb
        sw = b.sem("w")
        sg = b.sem("g")
        st = b.sem("t")
        b.wait("sync", self.t_cast)
        t_w = b.dma(
            "sync",
            self.ws[:, 0:kt_n * 128].rearrange("p (k c) -> p k c", c=128),
            wdram[0:kt_n * 128, ot * 128:(ot + 1) * 128]
            .rearrange("(k p) c -> p k c", p=128),
            sw)
        b.wait("gpsimd", t_w)
        b.wait("gpsimd", self.t_mm[p])
        self.t_cast = b.op("gpsimd", lambda e, p=p, kt_n=kt_n: e.tensor_copy(
            self.wb[:, p * self.kt_max * 128:p * self.kt_max * 128 + kt_n * 128],
            self.ws[:, 0:kt_n * 128]), sg)
        b.wait("tensor", self.t_cast)
        if rhs_ready is not None:
            b.wait("tensor", rhs_ready)
        if wait_extra is not None:
            b.wait("tensor", wait_extra)
        t_ev = None
        for mc in range(n_mc):
            bk = self.bank_i % len(self.psums)
            ps = self.psums[bk]
            b.wait("tensor", self.t_bank[bk])
            t_mm = None
            for kt in range(kt_n):
                t_mm = b.op("tensor", lambda e, p=p, kt=kt, ps=ps, kt_n=kt_n,
                            mc=mc, ncol=ncol: e.matmul(
                    ps[:, 0:ncol],
                    self.wb[:, (p * self.kt_max + kt) * 128:
                            (p * self.kt_max + kt) * 128 + 128],
                    rhs_fn(kt, mc), start=(kt == 0), stop=(kt == kt_n - 1)),
                    st if kt == kt_n - 1 else None)
            self.t_mm[p] = t_mm
            t_ev = evac_fn(ps, t_mm, mc)
            self.t_bank[bk] = t_ev
            self.bank_i += 1
        self.oi += 1
        return t_ev


def emit_l0ffn(b, io, xr, xl, t_xl):
    sv = b.sem("v")
    ss = b.sem("s")
    scl = b.sbuf("f0scl", [128, 32 * 4 + 8], F32)
    sd = b.sem("d")
    t_scl = None
    for i, nm in enumerate(("f0_s1", "f0_s2", "f0_h1s", "f0_h2s")):
        t_scl = b.dma("gpsimd", scl[:, i * 32:(i + 1) * 32], io[nm], sd)
    t_scl = b.dma("gpsimd", scl[:, 128:136], io["f0_s3"], sd)
    G = b.sbuf("f0G", [128, 32 * 512], BF16)
    H = b.sbuf("f0H", [128, 32 * 512], BF16)
    ps = [b.pool["a"], b.pool["b"]]
    lin = Lin(b, "f0", 32, ps)
    b.wait("scalar", t_scl)
    b.wait("vector", t_scl)

    t_G_read = None
    t_H_read = None
    t_xr_tok = None
    for mc in range(4):
        mo = mc * 512

        def xlr(kt, mc, mo=mo):
            return xl[:, kt * W_ + mo:kt * W_ + mo + 512]

        b.wait("scalar", later(t_G_read, t_H_read))
        t_a = None
        for ot in range(32):
            def evac_a1(psb, tmm, mc_, ot=ot):
                b.wait("scalar", tmm)
                return b.op("scalar", lambda e, ot=ot, psb=psb: e.activation(
                    G[:, ot * 512:(ot + 1) * 512], psb[:, :], AF.Silu,
                    bias=0.0, scale=scl[:, ot:ot + 1]), ss)
            lin.otile(io["f0_w1T"], 8, ot, xlr, evac_a1,
                      rhs_ready=t_xl if (mc == 0 and ot == 0) else None)
        for ot in range(32):
            def evac_a2(psb, tmm, mc_, ot=ot):
                b.wait("scalar", tmm)
                return b.op("scalar", lambda e, ot=ot, psb=psb: e.activation(
                    H[:, ot * 512:(ot + 1) * 512], psb[:, :], AF.Copy,
                    bias=0.0, scale=scl[:, 32 + ot:32 + ot + 1]), ss)
            t_a = lin.otile(io["f0_w2T"], 8, ot, xlr, evac_a2)
        b.wait("vector", t_a)
        t_g = b.op("vector", lambda e: e.tensor_tensor(
            G[:, :], G[:, :], H[:, :], ALU.mult), sv)
        t_b = None
        for ot in range(32):
            def evac_b(psb, tmm, mc_, ot=ot):
                b.wait("scalar", tmm)
                return b.op("scalar", lambda e, ot=ot, psb=psb: e.activation(
                    H[:, ot * 512:(ot + 1) * 512], psb[:, :], AF.Gelu,
                    bias=0.0, scale=scl[:, 64 + ot:64 + ot + 1]), ss)
            t_b = lin.otile(io["f0_h1T"], 32, ot,
                            lambda kt, mc: G[:, kt * 512:kt * 512 + 512], evac_b,
                            rhs_ready=t_g if ot == 0 else None)
        t_c = None
        for ot in range(32):
            def evac_c(psb, tmm, mc_, ot=ot):
                b.wait("scalar", tmm)
                return b.op("scalar", lambda e, ot=ot, psb=psb: e.activation(
                    G[:, ot * 512:(ot + 1) * 512], psb[:, :], AF.Gelu,
                    bias=0.0, scale=scl[:, 96 + ot:96 + ot + 1]), ss)
            t_c = lin.otile(io["f0_h2T"], 32, ot,
                            lambda kt, mc: H[:, kt * 512:kt * 512 + 512], evac_c,
                            rhs_ready=t_b if ot == 0 else None)
            t_H_read = lin.t_mm[(lin.oi - 1) % 2]
        for ot in range(8):
            def evac_d(psb, tmm, mc_, ot=ot, mo=mo):
                b.wait("vector", tmm)
                return b.op("vector", lambda e, ot=ot, psb=psb, mo=mo:
                            e.scalar_tensor_tensor(
                    xr[:, ot * W_ + mo:ot * W_ + mo + 512], psb[:, :],
                    scl[:, 128 + ot:128 + ot + 1],
                    xr[:, ot * W_ + mo:ot * W_ + mo + 512],
                    ALU.mult, ALU.add), sv)
            t_xr_tok = lin.otile(io["f0_w3T"], 32, ot,
                                 lambda kt, mc: G[:, kt * 512:kt * 512 + 512],
                                 evac_d, rhs_ready=t_c if ot == 0 else None)
            t_G_read = lin.t_mm[(lin.oi - 1) % 2]
    return t_xr_tok


def emit_l1attn(b, io, xr, xl, t_xl, core_group=None):
    nc = b.nc
    sv = b.sem("v")
    ss = b.sem("s")
    st = b.sem("t")
    sg = b.sem("g")
    sw = b.sem("w")
    scc = b.sem("c")
    H, DH = 16, 64

    Q = b.sbuf("a1Q", [128, DT * W_], BF16)
    ATT = b.sbuf("a1ATT", [128, DT * 512], BF16)
    KW = b.sbuf("a1KW", [128, 8 * 512], BF16)
    VW = b.sbuf("a1VW", [128, 8 * 512], BF16)
    WS2 = b.sbuf("a1WS2", [128, 2048], F32)
    browf = WS2[0:1, 0:2048]
    k2 = b.sbuf("a1k2", [128, 2 * 512], BF16)
    v2 = b.sbuf("a1v2", [128, 2 * 512], BF16)
    U = b.sbuf("a1U", [128, 512], F32)
    E = b.sbuf("a1E", [128, 512], F32)
    KVB = b.sbuf("a1KVB", [128, H * 64 + H], F32)
    KVBb = b.sbuf("a1KVBb", [128, H * 64 + H], BF16)
    zrep = b.sbuf("a1zrep", [64, 512], F32)
    zr = b.sbuf("a1zr", [1, 512], F32)
    brow = b.sbuf("a1brow", [1, 2 * 1024], BF16)
    onesr = b.sbuf("a1onesr", [1, 128], BF16)
    onesrf = b.sbuf("a1onesrf", [1, 128], F32)
    onescb = b.sbuf("a1onescb", [128, 1], BF16)
    bocol = b.sbuf("a1bocol", [128, 8], F32)
    bqcol = b.sbuf("a1bqcol", [128, 8], F32)
    T1 = b.sbuf("a1T1", [128, 512], F32)
    ps_kv = b.pool["r2"][0:64, 0:H * 64]
    ps_ks = b.pool["e"][0:64, 0:H]
    ps_k = [b.pool["a"], b.pool["b"]]
    ps_v = [b.pool["c"], b.pool["d"]]
    ps_a = b.pool["c"][0:64, 0:512]
    ps_z = b.pool["e"][0:1, 0:512]
    ps_rz = b.pool["d"][0:64, 0:512]
    kv_in = nc.dram_tensor("a1kv_in", [64, H * 64 + H], F32, kind="Internal")
    kv_out = nc.dram_tensor("a1kv_out", [64, H * 64 + H], F32, kind="Internal")

    sd = b.sem("d")
    t_c = b.dma("gpsimd", browf[0:1, 0:1024], io["a1_bk"], sd)
    t_c = b.dma("gpsimd", browf[0:1, 1024:2048], io["a1_bv"], sd)
    t_c = b.dma("gpsimd", bocol[:, :], io["a1_bo"], sd)
    t_c = b.dma("gpsimd", bqcol[:, :], io["a1_bq"], sd)
    b.wait("vector", t_c)
    b.op("vector", lambda e: e.memset(onesr[:, :], 1.0), sv)
    b.op("vector", lambda e: e.memset(onesrf[:, :], 1.0), sv)
    b.op("vector", lambda e: e.memset(onescb[:, :], 1.0), sv)
    t_cst = b.op("vector", lambda e: e.tensor_copy(brow[:, :], browf[:, :]), sv)
    b.wait("tensor", t_cst)
    b.wait("scalar", t_c)

    ps_q = [b.pool["a"], b.pool["b"]]
    lin = Lin(b, "a1", 8, ps_q)
    ue_read = [None]

    def evac_q(psb, tmm, mc, ot):
        b.wait("scalar", tmm)
        b.wait("scalar", ue_read[0])
        b.op("scalar", lambda e: e.activation(
            U[:, :], psb[:, 0:512], AF.Identity, bias=bqcol[:, ot:ot + 1],
            scale=1.0), ss)
        tU = b.op("scalar", lambda e: e.activation(
            E[:, :], psb[:, 0:512], AF.Identity, bias=bqcol[:, ot:ot + 1],
            scale=1.0), ss)
        b.wait("vector", tU)
        tM = b.op("vector", lambda e: e.tensor_scalar(
            E[:, :], E[:, :], 0.0, None, ALU.min), sv)
        b.wait("scalar", tM)
        tE = b.op("scalar", lambda e: e.activation(
            E[:, :], E[:, :], AF.Exp, bias=0.0, scale=1.0), ss)
        b.wait("vector", tE)
        ue_read[0] = b.op("vector", lambda e, ot=ot, mc=mc: e.scalar_tensor_tensor(
            Q[:, ot * W_ + mc * 512:ot * W_ + mc * 512 + 512],
            U[:, :], 1.0, E[:, :], ALU.add, ALU.max), sv)
        return ue_read[0]

    for ot in range(8):
        lin.otile(io["a1_WqT"], 8, ot,
                  lambda kt, mc: xl[:, kt * W_ + mc * 512:kt * W_ + mc * 512 + 512],
                  lambda ps, tmm, mc, ot=ot: evac_q(ps, tmm, mc, ot),
                  rhs_ready=t_xl if ot == 0 else None, n_mc=4)

    b.barrier()
    t_kv_mm = None
    t_k2_ev = [None, None]
    t_v2_ev = [None, None]
    t_kvmt = [None, None]
    t_kv_groups = []
    for half in range(2):
        st_now = (st, st.n) if st.n > 0 else None
        b.wait("sync", st_now)
        b.wait("gpsimd", st_now)
        t_lw = None
        for wname, dst in (("a1_WkT", KW), ("a1_WvT", VW)):
            for sub in range(2):
                b.wait("sync", lin.t_cast)
                tw = None
                for kt in range(4):
                    tw = b.dma("sync", WS2[:, kt * 512:(kt + 1) * 512],
                               io[wname][(sub * 4 + kt) * 128:
                                         (sub * 4 + kt) * 128 + 128,
                                         half * 512:(half + 1) * 512], sw)
                b.wait("gpsimd", tw)
                lin.t_cast = b.op(
                    "gpsimd", lambda e, dst=dst, sub=sub: e.tensor_copy(
                        dst[:, sub * 4 * 512:(sub * 4 + 4) * 512],
                        WS2[:, :]), sg)
                t_lw = lin.t_cast
        b.wait("tensor", t_lw)
        for mt in range(16):
            p = mt % 2
            for (wbuf, psl, t_ev, browo) in ((KW, ps_k, t_k2_ev, 0),
                                             (VW, ps_v, t_v2_ev, 1024)):
                b.wait("tensor", t_ev[p])
                for d in range(DT):
                    b.op("tensor", lambda e, d=d, mt=mt, p=p, wbuf=wbuf,
                         psl=psl: e.matmul(
                        psl[p][:, :], xl[:, d * W_ + mt * 128:d * W_ + mt * 128 + 128],
                        wbuf[:, d * 512:(d + 1) * 512],
                        start=(d == 0), stop=False), None)
                b.op("tensor", lambda e, p=p, psl=psl, browo=browo, half=half:
                     e.matmul(
                    psl[p][:, :], onesr[0:1, :],
                    brow[0:1, browo + half * 512:browo + half * 512 + 512],
                    start=False, stop=True),
                    st)
            t_kmm = (st, st.n - 1)
            t_vmm = (st, st.n)
            b.wait("scalar", t_kmm)
            b.wait("scalar", ue_read[0])
            b.wait("scalar", t_kvmt[p])
            b.wait("vector", t_kvmt[p])
            b.op("scalar", lambda e, p=p: e.activation(
                U[:, :], ps_k[p][:, :], AF.Copy, bias=0.0, scale=1.0), ss)
            tU = b.op("scalar", lambda e, p=p: e.activation(
                E[:, :], ps_k[p][:, :], AF.Copy, bias=0.0, scale=1.0), ss)
            b.wait("vector", tU)
            tM = b.op("vector", lambda e: e.tensor_scalar(
                E[:, :], E[:, :], 0.0, None, ALU.min), sv)
            b.wait("scalar", tM)
            tE = b.op("scalar", lambda e: e.activation(
                E[:, :], E[:, :], AF.Exp, bias=0.0, scale=1.0), ss)
            b.wait("vector", tE)
            t_k2_ev[p] = b.op("vector", lambda e, p=p: e.scalar_tensor_tensor(
                k2[:, p * 512:(p + 1) * 512], U[:, :], 1.0, E[:, :],
                ALU.add, ALU.max), sv)
            ue_read[0] = t_k2_ev[p]
            b.wait("scalar", t_vmm)
            t_v2_ev[p] = b.op("scalar", lambda e, p=p: e.activation(
                v2[:, p * 512:(p + 1) * 512], ps_v[p][:, :], AF.Copy,
                bias=0.0, scale=1.0), ss)
            b.wait("tensor", later(t_k2_ev[p], t_v2_ev[p]))
            for h8 in range(8):
                h = half * 8 + h8
                b.op("tensor", lambda e, p=p, h8=h8, h=h, mt=mt: e.matmul(
                    ps_kv[:, h * 64:h * 64 + 64],
                    k2[:, p * 512 + h8 * 64:p * 512 + h8 * 64 + 64],
                    v2[:, p * 512 + h8 * 64:p * 512 + h8 * 64 + 64],
                    start=(mt == 0 and h8 == 0), stop=(mt == 15),
                    skip_group_check=True), None)
                t_kv_mm = b.op("tensor", lambda e, p=p, h8=h8, h=h, mt=mt,
                               half=half: e.matmul(
                    ps_ks[:, h:h + 1],
                    k2[:, p * 512 + h8 * 64:p * 512 + h8 * 64 + 64],
                    onescb[:, :],
                    start=(mt == 0 and h8 == 0 and half == 0),
                    stop=(mt == 15), skip_group_check=True),
                    st if h8 == 7 else None)
            t_kvmt[p] = t_kv_mm
        t_kv_groups.append(t_kv_mm)

    b.wait("scalar", later(*t_kv_groups))
    b.op("scalar", lambda e: e.activation(
        KVB[0:64, 0:H * 64], ps_kv[:, :], AF.Copy, bias=0.0, scale=1.0), ss)
    t_kvev = b.op("scalar", lambda e: e.activation(
        KVB[0:64, H * 64:H * 64 + H], ps_ks[:, :], AF.Copy,
        bias=0.0, scale=1.0), ss)
    b.wait("gpsimd", t_kvev)
    t_st = b.dma("gpsimd", kv_in.ap(), KVB[0:64, :], sd)
    b.wait("gpsimd", t_st)
    groups = [[0, 1, 2, 3], [4, 5, 6, 7]]
    t_ar = b.cc("AllReduce", ALU.add, groups,
                [kv_in.ap().opt()], [kv_out.ap().opt()], scc)
    b.wait("gpsimd", t_ar)
    b.dma("gpsimd", KVB[0:64, :], kv_out.ap(), sd)
    t_kvld = b.dma("gpsimd", KVB[64:128, :], kv_out.ap(), sd)
    b.wait("vector", t_kvld)
    t_kvld = b.op("vector", lambda e: e.tensor_copy(KVBb[:, :], KVB[:, :]), sv)

    b.barrier()
    t_psa_read = None
    t_att_free = None
    t_zrep_ev = None
    t_zr_read = None
    t_xr_tok = None
    b.wait("tensor", t_kvld)
    for sc in range(4):
        for h in range(H):
            base = 64 * (h % 2)
            dtile = h // 2
            qsl = lambda h=h, sc=sc, base=base, dtile=dtile: Q[
                base:base + 64, dtile * W_ + sc * 512:dtile * W_ + sc * 512 + 512]
            b.wait("tensor", t_psa_read)
            t_amm = b.op("tensor", lambda e, h=h, base=base, qsl=qsl: e.matmul(
                ps_a[:, :], KVBb[base:base + 64, h * 64:h * 64 + 64],
                qsl(), start=True, stop=True), st)
            b.wait("tensor", t_zr_read)
            t_zmm = b.op("tensor", lambda e, h=h, base=base, qsl=qsl: e.matmul(
                ps_z[:, :], KVBb[base:base + 64, H * 64 + h:H * 64 + h + 1],
                qsl(), start=True, stop=True), st)
            b.wait("vector", t_zmm)
            b.op("vector", lambda e: e.tensor_scalar(
                zr[:, :], ps_z[:, :], 1e-6, None, ALU.add), None)
            t_zr = b.op("vector", lambda e: e.reciprocal(zr[:, :], zr[:, :]),
                        sv)
            b.wait("tensor", later(t_zr, t_zrep_ev))
            t_zr_read = b.op("tensor", lambda e: e.matmul(
                ps_rz[:, :], onesrf[0:1, 0:64], zr[0:1, :],
                start=True, stop=True), st)
            b.wait("scalar", t_zr_read)
            b.wait("scalar", t_psa_read)
            t_zrep_ev = b.op("scalar", lambda e: e.activation(
                zrep[:, :], ps_rz[:, :], AF.Copy, bias=0.0, scale=1.0), ss)
            b.wait("vector", later(t_amm, t_zrep_ev))
            if h == 0:
                b.wait("vector", t_att_free)
            t_psa_read = b.op("vector", lambda e, h=h, base=base: e.tensor_tensor(
                ATT[base:base + 64, (h // 2) * 512:(h // 2) * 512 + 512],
                ps_a[:, :], zrep[:, :], ALU.mult), sv)
        def evac_o(psb, tmm, mc_, ot, sc=sc):
            b.wait("scalar", tmm)
            b.wait("scalar", ue_read[0])
            t1 = b.op("scalar", lambda e, ot=ot: e.activation(
                T1[:, :], psb[:, 0:512], AF.Identity, bias=bocol[:, ot:ot + 1],
                scale=1.0), ss)
            b.wait("vector", t1)
            ue_read[0] = b.op("vector", lambda e, ot=ot, sc=sc: e.tensor_tensor(
                xr[:, ot * W_ + sc * 512:ot * W_ + sc * 512 + 512],
                xr[:, ot * W_ + sc * 512:ot * W_ + sc * 512 + 512],
                T1[:, :], ALU.add), sv)
            return ue_read[0]

        for ot in range(8):
            t_xr_tok = lin.otile(
                io["a1_WoT"], 8, ot,
                lambda kt, mc: ATT[:, kt * 512:kt * 512 + 512],
                lambda ps, tmm, mc, ot=ot: evac_o(ps, tmm, mc, ot),
                rhs_ready=t_psa_read if ot == 0 else None, n_mc=1)
        t_att_free = lin.t_mm[(lin.oi - 1) % 2]
    return t_xr_tok


DEBUG_TAPS = []


def emit_l1ffn(b, io, xr, xl, t_ln):
    sv = b.sem("v")
    ss = b.sem("s")
    st = b.sem("t")
    sd = b.sem("d")
    E = 8

    LNAB = b.tiles["LNAB"]
    LT = b.sbuf("m1LT", [32, W_], F32)
    SH = b.sbuf("m1SH", [32, W_], F32)
    MK = b.sbuf("m1MK", [32, W_], BF16)
    OUT = b.sbuf("m1OUT", [128, DT * 512], F32)
    MX = OUT[0:32, 0:W_]
    AREP = b.sbuf("m1AREP", [128, 512], BF16)
    BREP = b.sbuf("m1BREP", [128, 512], BF16)
    XIT = b.sbuf("m1XIT", [128, DT * 512], BF16)
    HE = b.sbuf("m1HE", [128, 32 * 512], BF16)
    T1 = b.sbuf("m1T1", [128, 512], F32)
    rowm = b.sbuf("m1rowm", [1, 1024], F32)
    mrow = b.sbuf("m1mrow", [1, 512], BF16)
    onesr = b.sbuf("m1onesr", [1, 128], F32)
    ones512 = b.sbuf("m1ones512", [1, 512], F32)
    wsum = b.sbuf("m1wsum", [1, 8], F32)
    brr = b.sbuf("m1brr", [1, 8], F32)
    WRS = b.sbuf("m1WRS", [128, 64], F32)
    b1c = b.sbuf("m1b1c", [128, E * 32], F32)
    b2c = b.sbuf("m1b2c", [128, E * 8], F32)

    t_c = b.dma("gpsimd", WRS[:, :], io["m1_WrP"], sd)
    t_c = b.dma("gpsimd", brr[:, :], io["m1_br"], sd)
    t_c = b.dma("gpsimd", b1c[:, :], io["m1_b1pc"], sd)
    t_c = b.dma("gpsimd", b2c[:, :], io["m1_b2pc"], sd)
    b.wait("vector", t_c)
    b.op("vector", lambda e: e.memset(onesr[:, :], 1.0), sv)
    b.op("vector", lambda e: e.memset(ones512[:, :], 1.0), sv)
    b.op("vector", lambda e: e.memset(LT[:, :], 0.0), sv)
    t_z = b.op("vector", lambda e: e.memset(SH[:, :], 0.0), sv)

    ones_col = b.tiles["lnones"]
    ps_raw = b.pool["c"]
    ps_aff = b.pool["d"]
    ps_ar = b.pool["f"]
    ps_ws = b.pool["e"][0:1, 0:8]
    b.wait("tensor", later(t_ln, t_z))
    b.wait("scalar", t_c)
    t_ws = None
    for d in range(E):
        t_ws = b.op("tensor", lambda e, d=d: e.matmul(
            ps_ws, ones_col[:, :], WRS[:, d * 8:(d + 1) * 8],
            start=(d == 0), stop=(d == DT - 1)), st)
    b.wait("scalar", t_ws)
    t_wse = b.op("scalar", lambda e: e.activation(
        wsum[:, :], ps_ws, AF.Copy, bias=0.0, scale=1.0), ss)
    b.wait("tensor", t_wse)

    t_ps_free = [None, None, None]
    t_sh_free = None
    for mc in range(4):
        mo = mc * 512
        b.wait("tensor", later(*t_ps_free))
        for d in range(DT):
            b.op("tensor", lambda e, d=d, mo=mo: e.matmul(
                ps_raw[0:8, 0:512], WRS[:, d * 8:(d + 1) * 8],
                xr[:, d * W_ + mo:d * W_ + mo + 512],
                start=(d == 0), stop=(d == DT - 1)), None)
        t_raw = b.op("tensor", lambda e, mo=mo: e.matmul(
            ps_ar[0:8, 0:512], onesr[0:1, 0:8], LNAB[0:1, mo:mo + 512],
            start=True, stop=True), st)
        b.op("tensor", lambda e, mo=mo: e.matmul(
            ps_aff[0:8, 0:512], wsum[0:1, :],
            LNAB[0:1, W_ + mo:W_ + mo + 512], start=True, stop=False), None)
        t_aff = b.op("tensor", lambda e: e.matmul(
            ps_aff[0:8, 0:512], brr[0:1, :], ones512[0:1, :],
            start=False, stop=True), st)
        b.wait("scalar", later(t_raw, t_sh_free))
        t_e1 = b.op("scalar", lambda e, mo=mo: e.activation(
            SH[0:8, mo:mo + 512], ps_ar[0:8, 0:512], AF.Copy,
            bias=0.0, scale=1.0), ss)
        b.wait("vector", t_e1)
        t_l1 = b.op("vector", lambda e, mo=mo: e.tensor_tensor(
            LT[0:8, mo:mo + 512], ps_raw[0:8, 0:512], SH[0:8, mo:mo + 512],
            ALU.mult), sv)
        b.wait("scalar", later(t_aff, t_l1))
        t_e2 = b.op("scalar", lambda e, mo=mo: e.activation(
            SH[0:8, mo:mo + 512], ps_aff[0:8, 0:512], AF.Copy,
            bias=0.0, scale=1.0), ss)
        b.wait("vector", t_e2)
        t_l2 = b.op("vector", lambda e, mo=mo: e.tensor_tensor(
            LT[0:8, mo:mo + 512], LT[0:8, mo:mo + 512], SH[0:8, mo:mo + 512],
            ALU.add), sv)
        t_ps_free = [t_l1, t_l2, t_l1]
        t_sh_free = t_l2
    b.wait("vector", t_l2)
    b.op("vector", lambda e: e.tensor_copy(MX[:, :], LT[:, :]), None)
    for sw_ in (1, 2, 4):
        m32 = [(i ^ sw_) for i in range(32)]
        b.op("vector", lambda e, m32=m32: e.stream_shuffle(
            SH[:, :], MX[:, :], m32), None)
        b.op("vector", lambda e: e.tensor_tensor(
            MX[:, :], MX[:, :], SH[:, :], ALU.max), None)
    t_mk = b.op("vector", lambda e: e.tensor_tensor(
        MK[:, :], LT[:, :], MX[:, :], ALU.is_ge), sv)
    b.barrier()

    lin = Lin(b, "m1", 32, [b.pool["a"], b.pool["b"]], single_wb=True)
    ue = [None]
    t_s1mm = None
    t_s2mm = None
    t_xi = None
    t_xr_tok = None
    for mc in range(4):
        mo = mc * 512
        b.op("vector", lambda e: e.memset(OUT[:, :], 0.0), None)
        for ex in range(E):
            t_mr = b.dma("gpsimd", mrow[0:1, 0:512],
                         MK[ex:ex + 1, mo:mo + 512], sd)
            b.wait("vector", t_mr)
            b.op("vector", lambda e, mo=mo: e.tensor_tensor(
                rowm[0:1, 0:512], LNAB[0:1, mo:mo + 512], mrow[0:1, 0:512],
                ALU.mult), None)
            t_rw = b.op("vector", lambda e, mo=mo: e.tensor_tensor(
                rowm[0:1, 512:1024], LNAB[0:1, W_ + mo:W_ + mo + 512],
                mrow[0:1, 0:512], ALU.mult), sv)
            b.wait("tensor", t_rw)
            b.wait("tensor", t_xi)
            b.op("tensor", lambda e: e.matmul(
                b.pool["r2"][:, 0:512], onesr[:, :], rowm[0:1, 0:512],
                start=True, stop=True), None)
            t_rp = b.op("tensor", lambda e: e.matmul(
                b.pool["r2"][:, 512:1024], onesr[:, :], rowm[0:1, 512:1024],
                start=True, stop=True), st)
            b.wait("scalar", later(t_rp, t_xi))
            b.op("scalar", lambda e: e.activation(
                AREP[:, 0:512], b.pool["r2"][:, 0:512],
                AF.Copy, bias=0.0, scale=1.0), None)
            t_ev = b.op("scalar", lambda e: e.activation(
                BREP[:, 0:512], b.pool["r2"][:, 512:1024],
                AF.Copy, bias=0.0, scale=1.0), ss)
            b.wait("vector", later(t_ev, t_s1mm))
            for d in range(DT):
                b.op("vector", lambda e, d=d, mo=mo: e.tensor_tensor(
                    XIT[:, d * 512:(d + 1) * 512],
                    xr[:, d * W_ + mo:d * W_ + mo + 512],
                    AREP[:, 0:512], ALU.mult), None)
                t_xi = b.op("vector", lambda e, d=d: e.tensor_tensor(
                    XIT[:, d * 512:(d + 1) * 512],
                    XIT[:, d * 512:(d + 1) * 512],
                    BREP[:, 0:512], ALU.add), sv)
            b.wait("scalar", t_s2mm)
            t_b = None
            for ot in range(32):
                def ev1(psb, tmm, mc_, ot=ot, ex=ex):
                    b.wait("scalar", tmm)
                    return b.op("scalar", lambda e, ot=ot, ex=ex, psb=psb:
                                e.activation(
                        HE[:, ot * 512:(ot + 1) * 512], psb[:, 0:512],
                        AF.Relu, bias=b1c[:, ex * 32 + ot:ex * 32 + ot + 1],
                        scale=1.0), ss)
                t_b = lin.otile(io["m1_W1"][ex], 8, ot,
                                lambda kt, mc: XIT[:, kt * 512:kt * 512 + 512],
                                ev1, rhs_ready=t_xi if ot == 0 else None)
            t_s1mm = lin.t_mm[(lin.oi - 1) % lin.n_wb]
            for ot in range(DT):
                def ev2(psb, tmm, mc_, ot=ot, ex=ex):
                    b.wait("scalar", tmm)
                    b.wait("scalar", ue[0])
                    t1 = b.op("scalar", lambda e, ot=ot, ex=ex, psb=psb:
                              e.activation(
                        T1[:, :], psb[:, 0:512], AF.Identity,
                        bias=b2c[:, ex * 8 + ot:ex * 8 + ot + 1],
                        scale=1.0), ss)
                    b.wait("vector", t1)
                    ue[0] = b.op("vector", lambda e, ot=ot: e.tensor_tensor(
                        OUT[:, ot * 512:(ot + 1) * 512],
                        OUT[:, ot * 512:(ot + 1) * 512],
                        T1[:, :], ALU.add), sv)
                    return ue[0]
                lin.otile(io["m1_W2"][ex], 32, ot,
                          lambda kt, mc: HE[:, kt * 512:kt * 512 + 512],
                          ev2, rhs_ready=t_b if ot == 0 else None)
            t_s2mm = lin.t_mm[(lin.oi - 1) % lin.n_wb]
        for ot in range(DT):
            t_xr_tok = b.op("vector", lambda e, ot=ot, mo=mo: e.tensor_tensor(
                xr[:, ot * W_ + mo:ot * W_ + mo + 512],
                xr[:, ot * W_ + mo:ot * W_ + mo + 512],
                OUT[:, ot * 512:(ot + 1) * 512], ALU.add), sv)
    return t_xr_tok



def emit_l3ffn(b, io, xr, xl, t_xl):
    sv = b.sem("v")
    ss = b.sem("s")
    st = b.sem("t")
    sd = b.sem("d")
    sw = b.sem("w")
    sg = b.sem("g")

    CF = b.sbuf("f3CF", [128, DT * 1024], BF16)
    SF = b.sbuf("f3SF", [128, DT * 1024], BF16)
    TrR = b.sbuf("f3TrR", [128, DT * 1024], BF16)
    TiR = b.sbuf("f3TiR", [128, DT * 1024], BF16)
    MR = b.sbuf("f3MR", [128, DT * 1024], BF16)
    WW = b.sbuf("f3WW", [128, 3072], BF16)
    WSF = b.sbuf("f3WSF", [128, 1024], F32)
    T1 = b.sbuf("f3T1", [128, 512], F32)
    biasc = b.sbuf("f3biasc", [128, 8], F32)
    sc1024 = b.sbuf("f3sc", [128, 1], F32)

    t_c = b.dma("gpsimd", biasc[:, :], io["f3_bias"], sd)
    b.wait("vector", t_c)
    t_sc = b.op("vector", lambda e: e.memset(sc1024[:, :], 1.0 / 1024.0), sv)
    b.wait("scalar", t_sc)

    t_cast = None
    for nm, dst, neg in (("f3_C", CF, False), ("f3_S", SF, False)):
        for j in range(DT):
            b.wait("sync", t_cast)
            t_w = b.dma("sync", WSF[:, :],
                        io[nm][j * 128:(j + 1) * 128, :], sw)
            b.wait("gpsimd", t_w)
            if neg:
                t_cast = b.op("gpsimd", lambda e, dst=dst, j=j: e.tensor_scalar(
                    dst[:, j * 1024:(j + 1) * 1024], WSF[:, :], -1.0, None,
                    ALU.mult), sg)
            else:
                t_cast = b.op("gpsimd", lambda e, dst=dst, j=j: e.tensor_copy(
                    dst[:, j * 1024:(j + 1) * 1024], WSF[:, :]), sg)
    b.wait("tensor", t_cast)
    b.wait("tensor", t_xl)

    ps = [b.pool["a"], b.pool["b"]]
    t_bank = [None, None]
    bank_i = [0]
    t_ww_cast = t_cast
    t_ww_read = None

    def phase_T():
        nonlocal t_ww_cast, t_ww_read
        for kt in range(DT):
            t_w = None
            for wi, wd in enumerate(("f3_WrT", "f3_WiT")):
                b.wait("sync", t_ww_cast)
                t_w = b.dma("sync",
                            WSF[:, :].rearrange("p (k c) -> p k c", c=128),
                            io[wd][:, kt * 128:(kt + 1) * 128]
                            .rearrange("(k p) c -> p k c", p=128), sw)
                b.wait("gpsimd", t_w)
                b.wait("gpsimd", t_ww_read)
                t_ww_cast = b.op("gpsimd", lambda e, wi=wi: e.tensor_copy(
                    WW[:, wi * 1024:(wi + 1) * 1024], WSF[:, :]), sg)
                if wi == 1:
                    t_ww_cast = b.op("gpsimd", lambda e: e.tensor_scalar(
                        WW[:, 2048:3072], WSF[:, :], -1.0, None, ALU.mult), sg)
            b.wait("tensor", t_ww_cast)
            for comp, (la, ra, lb, rb, dst) in enumerate(
                    ((0, CF, 2, SF, TrR), (0, SF, 1, CF, TiR))):
                for oc in range(2):
                    bk = bank_i[0] % 2
                    psb = ps[bk]
                    b.wait("tensor", t_bank[bk])
                    t_mm = None
                    for kp in range(DT):
                        b.op("tensor", lambda e, kp=kp, oc=oc, psb=psb, ra=ra,
                             la=la: e.matmul(
                            psb[:, 0:512],
                            WW[:, la * 1024 + kp * 128:la * 1024 + kp * 128 + 128],
                            ra[:, kp * 1024 + oc * 512:kp * 1024 + oc * 512 + 512],
                            start=(kp == 0), stop=False), None)
                        t_mm = b.op("tensor", lambda e, kp=kp, oc=oc, psb=psb,
                                    rb=rb, lb=lb: e.matmul(
                            psb[:, 0:512],
                            WW[:, lb * 1024 + kp * 128:lb * 1024 + kp * 128 + 128],
                            rb[:, kp * 1024 + oc * 512:kp * 1024 + oc * 512 + 512],
                            start=False, stop=(kp == DT - 1)),
                            st if kp == DT - 1 else None)
                    t_ww_read = t_mm
                    b.wait("scalar", t_mm)
                    t_ev = b.op("scalar", lambda e, kt=kt, oc=oc, psb=psb,
                                dst=dst: e.activation(
                        dst[:, kt * 1024 + oc * 512:kt * 1024 + oc * 512 + 512],
                        psb[:, 0:512], AF.Identity, bias=0.0,
                        scale=sc1024[:, 0:1]), ss)
                    t_bank[bk] = t_ev
                    bank_i[0] += 1

    def phase_M():
        for dd in range(DT):
            for oc in range(2):
                bk = bank_i[0] % 2
                psb = ps[bk]
                b.wait("tensor", t_bank[bk])
                t_mm = None
                for kk in range(DT):
                    b.op("tensor", lambda e, kk=kk, dd=dd, oc=oc, psb=psb:
                         e.matmul(
                        psb[:, 0:512], CF[:, kk * 1024 + dd * 128:kk * 1024 + dd * 128 + 128],
                        TrR[:, kk * 1024 + oc * 512:kk * 1024 + oc * 512 + 512],
                        start=(kk == 0), stop=False), None)
                    t_mm = b.op("tensor", lambda e, kk=kk, dd=dd, oc=oc,
                                psb=psb: e.matmul(
                        psb[:, 0:512], SF[:, kk * 1024 + dd * 128:kk * 1024 + dd * 128 + 128],
                        TiR[:, kk * 1024 + oc * 512:kk * 1024 + oc * 512 + 512],
                        start=False, stop=(kk == DT - 1)),
                        st if kk == DT - 1 else None)
                b.wait("scalar", t_mm)
                t_ev = b.op("scalar", lambda e, dd=dd, oc=oc, psb=psb:
                            e.activation(
                    MR[:, dd * 1024 + oc * 512:dd * 1024 + oc * 512 + 512],
                    psb[:, 0:512], AF.Copy, bias=0.0, scale=1.0), ss)
                t_bank[bk] = t_ev
                bank_i[0] += 1

    def phase_Y():
        ue = [None]
        t_last = None
        for ot in range(DT):
            for mc in range(4):
                mo = mc * 512
                bk = bank_i[0] % 2
                psb = ps[bk]
                b.wait("tensor", t_bank[bk])
                t_mm = None
                for dd in range(DT):
                    t_mm = b.op("tensor", lambda e, dd=dd, ot=ot, mo=mo,
                                psb=psb: e.matmul(
                        psb[:, 0:512],
                        MR[:, dd * 1024 + ot * 128:dd * 1024 + ot * 128 + 128],
                        xl[:, dd * W_ + mo:dd * W_ + mo + 512],
                        start=(dd == 0), stop=(dd == DT - 1)),
                        st if dd == DT - 1 else None)
                b.wait("scalar", t_mm)
                b.wait("scalar", ue[0])
                t1 = b.op("scalar", lambda e, ot=ot, psb=psb: e.activation(
                    T1[:, :], psb[:, 0:512], AF.Identity,
                    bias=biasc[:, ot:ot + 1], scale=1.0), ss)
                b.wait("vector", t1)
                ue[0] = b.op("vector", lambda e, ot=ot, mo=mo: e.tensor_tensor(
                    xr[:, ot * W_ + mo:ot * W_ + mo + 512],
                    xr[:, ot * W_ + mo:ot * W_ + mo + 512],
                    T1[:, :], ALU.add), sv)
                t_bank[bk] = ue[0]
                bank_i[0] += 1
                t_last = ue[0]
        return t_last

    phase_T()
    b.barrier()
    phase_M()
    b.barrier()
    out = phase_Y()
    return out


def emit_l2attn(b, io, xr, xl, t_xl):
    nc = b.nc
    sv = b.sem("v")
    ss_ = b.sem("s")
    st = b.sem("t")
    sd = b.sem("d")
    sw = b.sem("w")
    sg = b.sem("g")
    scc = b.sem("c")

    OM = b.tiles["XL"]
    QB = b.sbuf("a2QB", [128, DT * W_], BF16)
    KB = b.sbuf("a2KB", [128, DT * W_], BF16)
    VHB = b.sbuf("a2VHB", [128, 4096], BF16)
    WVh = VHB
    QD = b.sbuf("a2QD", [128, 512], BF16)
    PT = b.sbuf("a2PT", [128, 2 * 512], BF16)
    QF = b.sbuf("a2QF", [128, 512], F32)
    T1 = b.sbuf("a2T1", [128, 512], F32)
    RT = T1
    zrep = b.sbuf("a2zrep", [64, 512], F32)
    rz = b.sbuf("a2rz", [1, 512], F32)
    WOPB = b.sbuf("a2WOPB", [128, 2 * 1024], BF16)
    IB = b.sbuf("a2IB", [128, 64], BF16)
    brow = b.sbuf("a2brow", [1, 1024], BF16)
    bqc = b.sbuf("a2bqc", [128, 8], F32)
    bkc = b.sbuf("a2bkc", [128, 8], F32)
    boc = b.sbuf("a2boc", [128, 8], F32)
    ropec = b.sbuf("a2ropec", [128, 2], F32)
    ropes = b.sbuf("a2ropes", [128, 2], F32)
    neg20 = b.sbuf("a2neg20", [128, 1], F32)
    onesr = b.sbuf("a2onesr", [1, 128], BF16)
    onesrf = b.sbuf("a2onesrf", [1, 128], F32)
    onescb = b.sbuf("a2onescb", [128, 1], BF16)
    vloc = nc.dram_tensor("a2vloc", [MTOK, 1024], F32, kind="Internal").ap()
    rs_in = nc.dram_tensor("a2rs_in", [4096, MTOK], F32, kind="Internal")
    rs_out = nc.dram_tensor("a2rs_out", [1024, MTOK], F32, kind="Internal")

    t_c = b.dma("gpsimd", bqc[:, :], io["a2_bq"], sd)
    t_c = b.dma("gpsimd", bkc[:, :], io["a2_bkc"], sd)
    t_c = b.dma("gpsimd", boc[:, :], io["a2_bo"], sd)
    t_c = b.dma("gpsimd", ropec[:, 0:1], io["a2_ropec"], sd)
    t_c = b.dma("gpsimd", ropes[:, 0:1], io["a2_ropes"], sd)
    t_c = b.dma("gpsimd", IB[0:64, :], io["a2_eye64"], sd)
    t_c = b.dma("gpsimd", IB[64:128, :], io["a2_eye64"], sd)
    t_c = b.dma("gpsimd", T1[0:1, 0:512], io["a2_bv"][0:1, 0:512], sd)
    t_c = b.dma("gpsimd", QF[0:1, 0:512], io["a2_bv"][0:1, 512:1024], sd)
    b.wait("vector", t_c)
    b.op("vector", lambda e: e.memset(neg20[:, :], -20.0), sv)
    b.op("vector", lambda e: e.memset(onesr[:, :], 1.0), sv)
    b.op("vector", lambda e: e.memset(onesrf[:, :], 1.0), sv)
    b.op("vector", lambda e: e.memset(onescb[:, :], 1.0), sv)
    b.op("vector", lambda e: e.tensor_scalar(
        ropec[:, 1:2], ropec[:, 0:1], 0.125, None, ALU.mult), sv)
    b.op("vector", lambda e: e.tensor_scalar(
        ropes[:, 1:2], ropes[:, 0:1], 0.125, None, ALU.mult), sv)
    t_br = b.op("vector", lambda e: e.tensor_copy(
        brow[0:1, 0:512], T1[0:1, 0:512]), sv)
    t_br = b.op("vector", lambda e: e.tensor_copy(
        brow[0:1, 512:1024], QF[0:1, 0:512]), sv)
    b.wait("tensor", t_br)
    b.wait("scalar", t_c)

    lin = Lin(b, "a2", 8, [b.pool["e"], b.pool["f"]])
    qf_free = [None]
    qf_free[0] = t_br

    def rope_evac(psb, tmm, biascol, dst_fn, ci):
        b.wait("scalar", tmm)
        b.wait("scalar", qf_free[0])
        t1 = b.op("scalar", lambda e, psb=psb, biascol=biascol: e.activation(
            QF[:, :], psb[:, 0:512], AF.Identity, bias=biascol, scale=1.0),
            ss_)
        b.wait("vector", t1)
        b.op("vector", lambda e: e.stream_shuffle(
            RT[:, :], QF[:, :], [i ^ 1 for i in range(32)]), None)
        b.op("vector", lambda e, ci=ci: e.tensor_scalar(
            RT[:, :], RT[:, :], ropes[:, ci:ci + 1], None, ALU.mult), None)
        out = b.op("vector", lambda e, ci=ci, dst_fn=dst_fn:
                   e.scalar_tensor_tensor(
            dst_fn(), QF[:, :], ropec[:, ci:ci + 1], RT[:, :],
            ALU.mult, ALU.add), sv)
        qf_free[0] = out
        return out

    for ot in range(DT):
        def evq(psb, tmm, mc, ot=ot):
            return rope_evac(
                psb, tmm, bqc[:, ot:ot + 1],
                lambda ot=ot, mc=mc: QB[:, ot * W_ + mc * 512:
                                        ot * W_ + mc * 512 + 512], 1)
        lin.otile(io["a2_wqT"], 8, ot,
                  lambda kt, mc: xl[:, kt * W_ + mc * 512:kt * W_ + mc * 512 + 512],
                  evq, rhs_ready=t_xl if ot == 0 else None, n_mc=4)
    for ot in range(DT):
        def evk(psb, tmm, mc, ot=ot):
            return rope_evac(
                psb, tmm, bkc[:, ot:ot + 1],
                lambda ot=ot, mc=mc: KB[:, ot * W_ + mc * 512:
                                        ot * W_ + mc * 512 + 512], 0)
        lin.otile(io["a2_wkT"], 8, ot,
                  lambda kt, mc: xl[:, kt * W_ + mc * 512:kt * W_ + mc * 512 + 512],
                  evk, n_mc=4)
    for oc in range(2):
        t_wv = None
        for j in range(DT):
            b.wait("sync", later(t_wv, lin.t_cast, qf_free[0]))
            t_w = b.dma("sync", QF[:, :],
                        io["a2_wvT"][j * 128:(j + 1) * 128,
                                     oc * 512:(oc + 1) * 512], sw)
            b.wait("gpsimd", t_w)
            t_wv = b.op("gpsimd", lambda e, j=j: e.tensor_copy(
                WVh[:, j * 512:(j + 1) * 512], QF[:, :]), sg)
        b.wait("tensor", t_wv)
        qf_free[0] = t_wv
        for mt in range(16):
            bk_ = lin.bank_i % 2
            psb = lin.psums[bk_]
            b.wait("tensor", lin.t_bank[bk_])
            for d in range(DT):
                b.op("tensor", lambda e, d=d, mt=mt, psb=psb: e.matmul(
                    psb[:, 0:512],
                    xl[:, d * W_ + mt * 128:d * W_ + mt * 128 + 128],
                    WVh[:, d * 512:(d + 1) * 512],
                    start=(d == 0), stop=False), None)
            t_mm = b.op("tensor", lambda e, oc=oc, psb=psb: e.matmul(
                psb[:, 0:512], onesr[0:1, :],
                brow[0:1, oc * 512:oc * 512 + 512],
                start=False, stop=True), st)
            b.wait("scalar", later(t_mm, qf_free[0]))
            t1 = b.op("scalar", lambda e, psb=psb: e.activation(
                QF[:, :], psb[:, 0:512], AF.Copy, bias=0.0, scale=1.0), ss_)
            b.wait("gpsimd", t1)
            t_vev = b.dma("gpsimd", vloc[mt * 128:(mt + 1) * 128,
                                         oc * 512:oc * 512 + 512],
                          QF[:, :], sd)
            qf_free[0] = t_vev
            lin.t_bank[bk_] = t1
            lin.bank_i += 1

    b.barrier()

    ps_s = [b.pool["a"], b.pool["b"]]
    ps_o = b.pool["c"]
    ps_z = b.pool["d"][0:1, 0:512]
    ps_qd = b.pool["d"]
    ps_r = b.pool["r2"][0:64, 0:512]
    t_vcast = None
    t_vread = None
    t_exp = [None, None]
    t_pt_read = [None, None]
    t_om_tt = None
    t_zrep_read = None
    t_qd_ev = None
    t_kc = None
    for hh in range(4):
        b.wait("sync", t_vread)
        for qtr in range(8):
            b.wait("sync", t_vcast)
            t_w = None
            for f_ in range(2):
                t_w = b.dma("sync",
                            QF[:, f_ * 256:(f_ + 1) * 256]
                            .rearrange("p (tb d) -> p tb d", d=64),
                            vloc[512 * hh:512 * hh + 512,
                                 qtr * 128 + f_ * 64:qtr * 128 + f_ * 64 + 64]
                            .rearrange("(tb t) d -> t tb d", t=128), sw)
            b.wait("gpsimd", t_w)
            t_vcast = b.op("gpsimd", lambda e, qtr=qtr: e.tensor_copy(
                VHB[:, qtr * 512:(qtr + 1) * 512], QF[:, :]), sg)
        b.wait("tensor", t_vcast)
        for uc in range(16):
            fb = uc
            qb_sl = QB[64 * (fb % 2):64 * (fb % 2) + 64,
                       (fb // 2) * W_ + 512 * hh:(fb // 2) * W_ + 512 * hh + 512]
            b.wait("vector", t_qd_ev)
            b.wait("vector", t_kc)
            t_cp = b.op("vector", lambda e, fb=fb, qb_sl=qb_sl: e.tensor_copy(
                QD[64 * (fb % 2):64 * (fb % 2) + 64, :], qb_sl), sv)
            b.wait("tensor", later(t_cp, t_om_tt, t_zrep_read))
            t_mv = b.op("tensor", lambda e, fb=fb, qb_sl=qb_sl: e.matmul(
                ps_qd[64 * ((fb + 1) % 2):64 * ((fb + 1) % 2) + 64, 0:512],
                IB[64 * (fb % 2):64 * (fb % 2) + 64, :], qb_sl,
                start=True, stop=True), st)
            b.wait("scalar", t_mv)
            t_qd_ev = b.op("scalar", lambda e, fb=fb: e.activation(
                QD[64 * ((fb + 1) % 2):64 * ((fb + 1) % 2) + 64, :],
                ps_qd[64 * ((fb + 1) % 2):64 * ((fb + 1) % 2) + 64, 0:512],
                AF.Copy, bias=0.0, scale=1.0), ss_)
            b.wait("tensor", t_qd_ev)
            for kc in range(64):
                fbk, tb = kc // 4, kc % 4
                kbase = 64 * (fbk % 2)
                p = kc % 2
                b.wait("tensor", t_exp[p])
                t_s = b.op("tensor", lambda e, kc=kc, p=p, fbk=fbk, tb=tb,
                           kbase=kbase, hh=hh: e.matmul(
                    ps_s[p][:, 0:512],
                    KB[kbase:kbase + 64,
                       (fbk // 2) * W_ + 512 * hh + tb * 128:
                       (fbk // 2) * W_ + 512 * hh + tb * 128 + 128],
                    QD[kbase:kbase + 64, :], start=True, stop=True), st)
                b.wait("scalar", later(t_s, t_pt_read[p]))
                t_exp[p] = b.op("scalar", lambda e, p=p: e.activation(
                    PT[:, p * 512:(p + 1) * 512], ps_s[p][:, 0:512],
                    AF.Exp, bias=neg20[:, 0:1], scale=1.0), ss_)
                b.wait("tensor", t_exp[p])
                b.op("tensor", lambda e, kc=kc, p=p: e.matmul(
                    ps_o[0:64, 0:512], VHB[:, kc * 64:kc * 64 + 64],
                    PT[:, p * 512:(p + 1) * 512],
                    start=(kc == 0), stop=(kc == 63),
                    skip_group_check=True), None)
                t_kc = b.op("tensor", lambda e, kc=kc, p=p: e.matmul(
                    ps_z[:, :], onescb[:, :], PT[:, p * 512:(p + 1) * 512],
                    start=(kc == 0), stop=(kc == 63),
                    skip_group_check=True), st)
                t_pt_read[p] = t_kc
            b.wait("vector", t_kc)
            t_rz = b.op("vector", lambda e: e.reciprocal(
                rz[:, :], ps_z[:, :]), sv)
            b.wait("tensor", later(t_rz, t_zrep_read))
            t_zmm = b.op("tensor", lambda e: e.matmul(
                ps_r[:, :], onesrf[0:1, 0:64], rz[0:1, :],
                start=True, stop=True), st)
            b.wait("scalar", later(t_zmm, t_om_tt))
            t_ze = b.op("scalar", lambda e: e.activation(
                zrep[:, :], ps_r[:, :], AF.Copy, bias=0.0, scale=1.0), ss_)
            t_zrep_read = t_zmm
            b.wait("vector", t_ze)
            base = 64 * (hh % 2)
            t_om_tt = b.op("vector", lambda e, hh=hh, fb=fb, base=base:
                           e.tensor_tensor(
                OM[base:base + 64,
                   (hh // 2) * 8192 + fb:(hh // 2) * 8192 + fb + 8177:16],
                ps_o[0:64, 0:512], zrep[:, :], ALU.mult), sv)
        t_vread = t_kc

    b.barrier()
    t_wc = None
    for wr in range(4):
        b.wait("sync", t_wc)
        t_w = b.dma("sync", QF[:, :],
                    io["a2_woPT"][(wr // 2) * 128:(wr // 2) * 128 + 128,
                                  (wr % 2) * 512:(wr % 2) * 512 + 512], sw)
        b.wait("gpsimd", t_w)
        t_wc = b.op("gpsimd", lambda e, wr=wr: e.tensor_copy(
            WOPB[:, wr * 512:(wr + 1) * 512], QF[:, :]), sg)
    b.wait("tensor", t_wc)
    b.barrier()
    psp = [b.pool["e"], b.pool["f"]]
    t_pb = [None, None]
    t_ev = None
    for tc in range(16):
        for et in range(DT):
            bk_ = (tc * DT + et) % 2
            psb = psp[bk_]
            b.wait("tensor", t_pb[bk_])
            t_mm = None
            for ft in range(2):
                t_mm = b.op("tensor", lambda e, ft=ft, et=et, tc=tc, psb=psb:
                            e.matmul(
                    psb[:, 0:512],
                    WOPB[:, ft * 1024 + et * 128:ft * 1024 + et * 128 + 128],
                    OM[:, ft * 8192 + tc * 512:ft * 8192 + tc * 512 + 512],
                    start=(ft == 0), stop=(ft == 1)), st)
            b.wait("scalar", later(t_mm, t_ev))
            t1 = b.op("scalar", lambda e, psb=psb: e.activation(
                QF[:, :], psb[:, 0:512], AF.Copy, bias=0.0, scale=1.0), ss_)
            b.wait("gpsimd", t1)
            t_ev = b.dma("gpsimd",
                         rs_in.ap()[(tc // 4) * 1024 + et * 128:
                                    (tc // 4) * 1024 + et * 128 + 128,
                                    (tc % 4) * 512:(tc % 4) * 512 + 512],
                         QF[:, :], sd)
            t_pb[bk_] = t1
    b.barrier(["gpsimd"])
    groups = [[0, 1, 2, 3], [4, 5, 6, 7]]
    t_rs = b.cc("ReduceScatter", ALU.add, groups, [rs_in.ap().opt()],
                [rs_out.ap().opt()], scc)
    b.wait("sync", t_rs)
    ue = [None]
    t_stg = None
    t_xr_tok = None
    for et in range(DT):
        for c4 in range(4):
            b.wait("sync", later(ue[0], t_stg))
            t_stg = b.dma("sync", QF[:, :],
                          rs_out.ap()[et * 128:(et + 1) * 128,
                                      c4 * 512:c4 * 512 + 512], sw)
            b.wait("scalar", later(t_stg, ue[0]))
            t1 = b.op("scalar", lambda e, et=et: e.activation(
                T1[:, :], QF[:, :], AF.Identity, bias=boc[:, et:et + 1],
                scale=1.0), ss_)
            b.wait("vector", t1)
            ue[0] = b.op("vector", lambda e, et=et, c4=c4: e.tensor_tensor(
                xr[:, et * W_ + c4 * 512:et * W_ + c4 * 512 + 512],
                xr[:, et * W_ + c4 * 512:et * W_ + c4 * 512 + 512],
                T1[:, :], ALU.add), sv)
            t_xr_tok = ue[0]
    return t_xr_tok


def emit_l2ffn(b, io, xr, xl, t_xl):
    nc = b.nc
    sv = b.sem("v")
    ss_ = b.sem("s")
    st = b.sem("t")
    sd = b.sem("d")
    sw = b.sem("w")
    sg = b.sem("g")
    scc = b.sem("c")

    PHIB = b.sbuf("m2PHIB", [128, DT * 2048], BF16)
    XTM = b.sbuf("m2XTM", [128, 16 * 1024], BF16)
    C2 = b.tiles["lnrep"]
    LP = b.sbuf("m2LP", [128, 512], F32)
    ML = b.sbuf("m2ML", [128, 512], F32)
    M2 = b.sbuf("m2M2", [128, 512], F32)
    E0 = b.sbuf("m2E0", [128, 512], F32)
    DTI = b.sbuf("m2DTI", [128, 512], BF16)
    DB = b.sbuf("m2DB", [128, 2 * 128], BF16)
    SLB = b.sbuf("m2SLB", [32, 1024], BF16)
    SLT = b.sbuf("m2SLT", [128, 256], BF16)
    YBT = b.sbuf("m2YBT", [128, 2 * 128], BF16)
    CTT = b.sbuf("m2CTT", [128, 2 * 512], BF16)
    zrep = b.sbuf("m2zrep", [128, 512], F32)
    zrow = b.sbuf("m2zrow", [1, 2048], F32)
    bnrow = b.sbuf("m2bnrow", [1, 1024], BF16)
    eye = b.sbuf("m2eye", [128, 128], BF16)
    maskc = b.sbuf("m2maskc", [128, 8], F32)
    b0t = b.sbuf("m2b0t", [128, 1], F32)
    ph2 = zrow
    onesr = b.sbuf("m2onesr", [1, 128], BF16)
    onesrf = b.sbuf("m2onesrf", [1, 128], F32)
    onescbf = b.sbuf("m2onescbf", [128, 1], BF16)
    QF = b.sbuf("m2QF", [128, 512], F32)
    T1 = M2
    lloc = nc.dram_tensor("m2lloc", [MTOK, 2048], F32, kind="Internal")
    lag_out = nc.dram_tensor("m2lag", [8 * MTOK, 2048], F32, kind="Internal")
    dispd = nc.dram_tensor("m2dispd", [MTOK, 2048], BF16, kind="Internal").ap()
    srs_in = nc.dram_tensor("m2srs_in", [2048, 1024], F32, kind="Internal")
    srs_out = nc.dram_tensor("m2srs_out", [512, 1024], F32, kind="Internal")
    yag_in = nc.dram_tensor("m2yag_in", [512, 1024], F32, kind="Internal")
    yag_out = nc.dram_tensor("m2yag_out", [2048, 1024], F32, kind="Internal")
    ctd = nc.dram_tensor("m2ctd", [2048, MTOK], BF16, kind="Internal").ap()

    t_c = b.dma("gpsimd", eye[:, :], io["eye128b"], sd)
    t_c = b.dma("gpsimd", maskc[:, :], io["m2_maskc"], sd)
    b.wait("vector", t_c)
    b.op("vector", lambda e: e.memset(onesr[:, :], 1.0), sv)
    b.op("vector", lambda e: e.memset(onesrf[:, :], 1.0), sv)
    t_o = b.op("vector", lambda e: e.memset(onescbf[:, :], 1.0), sv)

    ps_a = b.pool["a"]
    ps_b = b.pool["b"]
    ps_p2 = b.pool["e"][0:1, 0:512]
    t_cast = None
    t_ph = None
    for j in range(DT):
        for cq in range(4):
            b.wait("sync", t_cast)
            t_w = b.dma("sync", QF[:, :],
                        io["m2_phi"][j * 128:(j + 1) * 128,
                                     cq * 512:(cq + 1) * 512], sw)
            b.wait("gpsimd", t_w)
            t_cast = b.op("gpsimd", lambda e, j=j, cq=cq: e.tensor_copy(
                PHIB[:, j * 2048 + cq * 512:j * 2048 + cq * 512 + 512],
                QF[:, :]), sg)
    b.wait("tensor", later(t_cast, t_o, t_xl))
    RM = b.sbuf("m2RM", [128, 16], F32)
    RMB = b.sbuf("m2RMB", [128, 16], BF16)
    NMB = b.sbuf("m2NMB", [1, 2048], BF16)
    RMc = b.sbuf("m2RMc", [128, 1], F32)
    t_rminit = b.op("vector", lambda e: e.memset(RM[:, :], -1e30), sv)
    b.wait("vector", t_rminit)

    t_pb = [None, None]
    t_ev = None
    for mt in range(16):
        for dt_ in range(DT):
            k = mt * DT + dt_
            bk_ = k % 2
            psb = [ps_a, ps_b][bk_]
            b.wait("tensor", t_pb[bk_])
            t_mm = b.op("tensor", lambda e, mt=mt, dt_=dt_, psb=psb: e.matmul(
                psb[:, 0:128], xl[:, dt_ * W_ + mt * 128:dt_ * W_ + mt * 128 + 128],
                eye[:, :], start=True, stop=True), st)
            b.wait("scalar", t_mm)
            t_ev = b.op("scalar", lambda e, mt=mt, dt_=dt_, psb=psb:
                        e.activation(
                XTM[:, mt * 1024 + dt_ * 128:mt * 1024 + dt_ * 128 + 128],
                psb[:, 0:128], AF.Copy, bias=0.0, scale=1.0), ss_)
            t_pb[bk_] = t_ev

    t_rmx = None
    for mt in range(16):
        for npc in range(4):
            k = mt * 4 + npc
            bk_ = k % 2
            psb = [ps_a, ps_b][bk_]
            b.wait("tensor", t_pb[bk_])
            t_mm = None
            for dt_ in range(DT):
                t_mm = b.op("tensor", lambda e, mt=mt, npc=npc, dt_=dt_,
                            psb=psb: e.matmul(
                    psb[:, 0:512],
                    xl[:, dt_ * W_ + mt * 128:dt_ * W_ + mt * 128 + 128],
                    PHIB[:, dt_ * 2048 + npc * 512:dt_ * 2048 + npc * 512 + 512],
                    start=(dt_ == 0), stop=(dt_ == DT - 1)), st)
            b.wait("scalar", later(t_mm, t_ev, t_rmx))
            t1 = b.op("scalar", lambda e, psb=psb: e.activation(
                QF[:, :], psb[:, 0:512], AF.Copy, bias=0.0, scale=1.0), ss_)
            b.wait("gpsimd", t1)
            t_ev = b.dma("gpsimd",
                         lloc.ap()[mt * 128:(mt + 1) * 128,
                                   npc * 512:(npc + 1) * 512], QF[:, :], sd)
            b.wait("vector", t1)
            b.op("vector", lambda e: e.reduce_max(
                RMc[:, :], QF[:, :], axis=AX.X), None)
            t_rmx = b.op("vector", lambda e, mt=mt: e.tensor_tensor(
                RM[:, mt:mt + 1], RM[:, mt:mt + 1], RMc[:, :], ALU.max), sv)
            t_pb[bk_] = t_ev
    t_rmb = b.op("vector", lambda e: e.tensor_copy(RMB[:, :], RM[:, :]), sv)
    b.wait("tensor", t_rmb)
    t_nme = None
    for mt in range(16):
        bk_ = mt % 2
        psb = [ps_a, ps_b][bk_]
        b.wait("tensor", t_pb[bk_])
        t_tm = b.op("tensor", lambda e, mt=mt, psb=psb: e.matmul(
            psb[0:1, 0:128], RMB[:, mt:mt + 1], eye[:, :],
            start=True, stop=True), st)
        b.wait("scalar", later(t_tm, t_nme))
        t_nme = b.op("scalar", lambda e, mt=mt, psb=psb: e.activation(
            QF[0:1, 0:128], psb[0:1, 0:128], AF.Copy, bias=0.0, scale=1.0),
            ss_)
        b.wait("vector", t_nme)
        t_nme = b.op("vector", lambda e, mt=mt: e.tensor_scalar(
            NMB[0:1, mt * 128:(mt + 1) * 128], QF[0:1, 0:128], -1.0, None,
            ALU.mult), sv)
        t_pb[bk_] = t_nme
    b.barrier(["gpsimd"])
    t_ag = b.cc("AllGather", ALU.bypass, [list(range(8))],
                [lloc.ap().opt()], [lag_out.ap().opt()], scc)
    b.barrier()

    t_lpw = None
    t_dout = None
    for mt in range(16):
        for npc in range(4):
            t_ld = None
            for r in range(8):
                pr = r % 2
                b.wait("sync", t_lpw)
                t_ld = b.dma("sync", C2[:, pr * 512:(pr + 1) * 512],
                             lag_out.ap()[r * MTOK + mt * 128:
                                          r * MTOK + (mt + 1) * 128,
                                          npc * 512:(npc + 1) * 512], sw)
                b.wait("vector", t_ld)
                if r == 0:
                    t_lpw = b.op("vector", lambda e, pr=pr: e.tensor_scalar(
                        LP[:, :], C2[:, pr * 512:(pr + 1) * 512],
                        maskc[:, 0:1], None, ALU.mult), sv)
                else:
                    t_lpw = b.op("vector", lambda e, r=r, pr=pr:
                                 e.scalar_tensor_tensor(
                        LP[:, :], C2[:, pr * 512:(pr + 1) * 512],
                        maskc[:, r:r + 1], LP[:, :], ALU.mult, ALU.add), sv)
            b.wait("sync", t_dout)
            t_ml = b.dma("sync", ML[:, :],
                         lloc.ap()[mt * 128:(mt + 1) * 128,
                                   npc * 512:(npc + 1) * 512], sw)
            b.wait("vector", t_ml)
            b.op("vector", lambda e: e.tensor_tensor(
                M2[:, :], ML[:, :], LP[:, :], ALU.max), None)
            t_s0 = b.op("vector", lambda e: e.tensor_tensor(
                E0[:, :], ML[:, :], M2[:, :], ALU.subtract), sv)
            b.wait("scalar", t_s0)
            t_e0 = b.op("scalar", lambda e: e.activation(
                E0[:, :], E0[:, :], AF.Exp, bias=0.0, scale=1.0), ss_)
            t_s1 = b.op("vector", lambda e: e.tensor_tensor(
                LP[:, :], LP[:, :], M2[:, :], ALU.subtract), sv)
            b.wait("scalar", t_s1)
            t_e1 = b.op("scalar", lambda e: e.activation(
                LP[:, :], LP[:, :], AF.Exp, bias=0.0, scale=1.0), ss_)
            b.wait("vector", later(t_e0, t_e1))
            b.op("vector", lambda e: e.tensor_tensor(
                M2[:, :], E0[:, :], LP[:, :], ALU.add), None)
            b.op("vector", lambda e: e.reciprocal(M2[:, :], M2[:, :]), None)
            t_d = b.op("vector", lambda e: e.tensor_tensor(
                DTI[:, :], E0[:, :], M2[:, :], ALU.mult), sv)
            b.wait("gpsimd", t_d)
            t_dout = b.dma("gpsimd",
                           dispd[mt * 128:(mt + 1) * 128,
                                 npc * 512:(npc + 1) * 512], DTI[:, :], sd)
    b.barrier()

    t_db = [None, None]
    t_ev = None
    t_pb = [None, None]
    t_rmx = None
    for npt in range(16):
        for dc in range(2):
            k = npt * 2 + dc
            bk_ = k % 2
            psb = [ps_a, ps_b][bk_]
            b.wait("tensor", t_pb[bk_])
            t_mm = None
            for mt in range(16):
                pr = mt % 2
                b.wait("sync", t_db[pr])
                t_ld = b.dma("sync", DB[:, pr * 128:(pr + 1) * 128],
                             dispd[mt * 128:(mt + 1) * 128,
                                   npt * 128:(npt + 1) * 128], sw)
                b.wait("tensor", t_ld)
                t_mm = b.op("tensor", lambda e, mt=mt, npt=npt, dc=dc, pr=pr,
                            psb=psb: e.matmul(
                    psb[:, 0:512], DB[:, pr * 128:(pr + 1) * 128],
                    XTM[:, mt * 1024 + dc * 512:mt * 1024 + dc * 512 + 512],
                    start=(mt == 0), stop=(mt == 15)),
                    st)
                t_db[pr] = t_mm
            b.wait("scalar", later(t_mm, t_ev))
            t1 = b.op("scalar", lambda e, psb=psb: e.activation(
                QF[:, :], psb[:, 0:512], AF.Copy, bias=0.0, scale=1.0), ss_)
            b.wait("gpsimd", t1)
            t_ev = b.dma("gpsimd",
                         srs_in.ap()[npt * 128:(npt + 1) * 128,
                                     dc * 512:(dc + 1) * 512], QF[:, :], sd)
            t_pb[bk_] = t1
    b.barrier(["gpsimd"])
    t_rs = b.cc("ReduceScatter", ALU.add, [[0, 1, 2, 3], [4, 5, 6, 7]],
                [srs_in.ap().opt()], [srs_out.ap().opt()], scc)
    b.barrier()

    WnB = XTM[:, 0:8192]
    t_sl = None
    t_ev = None
    t_slc = None
    t_tr = None
    t_pb = [None, None]
    for nl in range(16):
        b.wait("sync", later(t_slc, t_ev))
        t_sl = b.dma("sync", C2[0:32, 0:1024],
                     srs_out.ap()[nl * 32:(nl + 1) * 32, :], sw)
        t_b1 = b.dma("sync", QF[0:1, 0:512],
                     io["m2_bnc"][nl:nl + 1, 0:512], sw)
        t_b2 = b.dma("sync", T1[0:1, 0:512],
                     io["m2_bnc"][nl:nl + 1, 512:1024], sw)
        b.wait("vector", later(t_sl, t_b2, t_tr, t_pb[0], t_pb[1]))
        b.op("vector", lambda e: e.tensor_copy(
            bnrow[0:1, 0:512], QF[0:1, 0:512]), None)
        t_bnr = b.op("vector", lambda e: e.tensor_copy(
            bnrow[0:1, 512:1024], T1[0:1, 0:512]), sv)
        t_slc = b.op("vector", lambda e: e.tensor_copy(
            SLB[:, :], C2[0:32, 0:1024]), sv)
        b.wait("tensor", t_slc)
        for dchunk in range(DT):
            bk_ = dchunk % 2
            psb = [ps_a, ps_b][bk_]
            b.wait("tensor", t_pb[bk_])
            t_mm = b.op("tensor", lambda e, dchunk=dchunk, psb=psb: e.matmul(
                psb[:, 0:32], SLB[:, dchunk * 128:(dchunk + 1) * 128],
                eye[0:32, 0:32], start=True, stop=True), st)
            b.wait("scalar", t_mm)
            t_tr = b.op("scalar", lambda e, dchunk=dchunk, psb=psb:
                        e.activation(
                SLT[:, dchunk * 32:(dchunk + 1) * 32], psb[:, 0:32],
                AF.Copy, bias=0.0, scale=1.0), ss_)
            t_pb[bk_] = t_tr
        t_wl = None
        t_wc = None
        for dt_ in range(DT):
            b.wait("sync", t_wc)
            if dt_ == 0:
                b.wait("sync", later(t_ev, t_bnr))
            t_wl = b.dma("sync", QF[:, :],
                         io["m2_Wc"][nl, dt_ * 128:(dt_ + 1) * 128, 0:512], sw)
            b.wait("gpsimd", t_wl)
            t_wc = b.op("gpsimd", lambda e, dt_=dt_: e.tensor_copy(
                WnB[:, dt_ * 1024:dt_ * 1024 + 512], QF[:, :]), sg)
            b.wait("sync", t_wc)
            t_wl = b.dma("sync", QF[:, :],
                         io["m2_Wc"][nl, dt_ * 128:(dt_ + 1) * 128,
                                     512:1024], sw)
            b.wait("gpsimd", t_wl)
            t_wc = b.op("gpsimd", lambda e, dt_=dt_: e.tensor_copy(
                WnB[:, dt_ * 1024 + 512:dt_ * 1024 + 1024], QF[:, :]),
                sg)
        b.wait("tensor", later(t_wc, t_tr))
        for ec in range(2):
            bk_ = ec % 2
            psb = [ps_a, ps_b][bk_]
            b.wait("tensor", t_pb[bk_])
            for dt_ in range(DT):
                b.op("tensor", lambda e, dt_=dt_, ec=ec, psb=psb: e.matmul(
                    psb[0:32, 0:512], SLT[:, dt_ * 32:(dt_ + 1) * 32],
                    WnB[:, dt_ * 1024 + ec * 512:dt_ * 1024 + ec * 512 + 512],
                    start=(dt_ == 0), stop=False), None)
            t_mm = b.op("tensor", lambda e, ec=ec, psb=psb: e.matmul(
                psb[0:32, 0:512], onesr[0:1, 0:32],
                bnrow[0:1, ec * 512:ec * 512 + 512],
                start=False, stop=True), st)
            b.wait("scalar", later(t_mm, t_ev))
            t1 = b.op("scalar", lambda e, psb=psb: e.activation(
                QF[0:32, :], psb[0:32, 0:512], AF.Copy, bias=0.0, scale=1.0),
                ss_)
            b.wait("gpsimd", t1)
            t_ev = b.dma("gpsimd",
                         yag_in.ap()[nl * 32:(nl + 1) * 32,
                                     ec * 512:(ec + 1) * 512],
                         QF[0:32, :], sd)
            t_pb[bk_] = t1
    b.barrier(["gpsimd"])
    t_yag = b.cc("AllGather", ALU.bypass, [[0, 1, 2, 3], [4, 5, 6, 7]],
                 [yag_in.ap().opt()], [yag_out.ap().opt()], scc)
    b.barrier()

    ps_z2 = b.pool["e"][0:1, 0:512]
    ps_zr = b.pool["f"]
    t_ctev = None
    t_zc = None
    t_zread = [None, None]
    for mc in range(4):
        t_zmm = None
        for npt in range(16):
            bk_ = npt % 2
            psb = [ps_a, ps_b][bk_]
            b.wait("tensor", t_pb[bk_])
            t_mm = None
            for dt_ in range(DT):
                b.op("tensor", lambda e, npt=npt, mc=mc, dt_=dt_,
                     psb=psb: e.matmul(
                    psb[:, 0:512],
                    PHIB[:, dt_ * 2048 + npt * 128:dt_ * 2048 + npt * 128 + 128],
                    xl[:, dt_ * W_ + mc * 512:dt_ * W_ + mc * 512 + 512],
                    start=(dt_ == 0), stop=False), None)
            t_mm = b.op("tensor", lambda e, npt=npt, mc=mc, psb=psb: e.matmul(
                psb[:, 0:512], onesr[0:1, :],
                NMB[0:1, mc * 512:mc * 512 + 512],
                start=False, stop=True), st)
            pr = npt % 2
            b.wait("scalar", later(t_mm, t_ctev, t_zread[pr]))
            t_ct = b.op("scalar", lambda e, pr=pr, psb=psb: e.activation(
                CTT[:, pr * 512:(pr + 1) * 512], psb[:, 0:512],
                AF.Exp, bias=0.0, scale=1.0), ss_)
            b.wait("gpsimd", t_ct)
            t_ctev = b.dma("gpsimd",
                           ctd[npt * 128:(npt + 1) * 128,
                               mc * 512:(mc + 1) * 512],
                           CTT[:, pr * 512:(pr + 1) * 512], sd)
            t_pb[bk_] = t_ctev
            b.wait("tensor", later(t_ct, t_zc))
            t_zmm = b.op("tensor", lambda e, pr=pr, npt=npt: e.matmul(
                ps_z2, onescbf[:, :], CTT[:, pr * 512:(pr + 1) * 512],
                start=(npt == 0), stop=(npt == 15), skip_group_check=True),
                st)
            t_zread[pr] = t_zmm
        b.wait("vector", t_zmm)
        t_zc = b.op("vector", lambda e, mc=mc: e.tensor_copy(
            zrow[0:1, mc * 512:(mc + 1) * 512], ps_z2), sv)
    b.wait("vector", t_zc)
    t_zrec = b.op("vector", lambda e: e.reciprocal(
        zrow[0:1, :], zrow[0:1, :]), sv)
    b.barrier()

    t_yb = [None, None]
    t_cb = [None, None]
    t_yc_last = [None]
    ue = [None]
    t_zev = None
    t_xr_tok = None
    for mc in range(4):
        b.wait("tensor", later(t_zrec, t_zev))
        t_zm = b.op("tensor", lambda e, mc=mc: e.matmul(
            ps_zr[:, 0:512], onesrf[:, :], zrow[0:1, mc * 512:mc * 512 + 512],
            start=True, stop=True), st)
        b.wait("scalar", later(t_zm, ue[0]))
        t_zev = b.op("scalar", lambda e: e.activation(
            zrep[:, :], ps_zr[:, 0:512], AF.Copy, bias=0.0, scale=1.0), ss_)
        for et in range(DT):
            bk_ = et % 2
            psb = [ps_a, ps_b][bk_]
            b.wait("tensor", t_pb[bk_])
            t_mm = None
            for npt in range(16):
                pr = npt % 2
                b.wait("sync", later(t_yb[pr], t_yc_last[0]))
                t_y = b.dma("sync", QF[:, 0:128],
                            yag_out.ap()[npt * 128:(npt + 1) * 128,
                                         et * 128:(et + 1) * 128], sw)
                b.wait("gpsimd", t_y)
                b.wait("gpsimd", t_yb[pr])
                t_yc = b.op("gpsimd", lambda e, pr=pr: e.tensor_copy(
                    YBT[:, pr * 128:(pr + 1) * 128], QF[:, 0:128]), sg)
                t_yc_last[0] = t_yc
                b.wait("sync", t_cb[pr])
                t_ctl = b.dma("sync", CTT[:, pr * 512:(pr + 1) * 512],
                              ctd[npt * 128:(npt + 1) * 128,
                                  mc * 512:(mc + 1) * 512], sw)
                b.wait("tensor", later(t_yc, t_ctl))
                t_mm = b.op("tensor", lambda e, npt=npt, pr=pr, psb=psb:
                            e.matmul(
                    psb[:, 0:512], YBT[:, pr * 128:(pr + 1) * 128],
                    CTT[:, pr * 512:(pr + 1) * 512],
                    start=(npt == 0), stop=(npt == 15)), st)
                t_yb[pr] = t_mm
                t_cb[pr] = t_mm
            b.wait("vector", later(t_mm, t_zev, ue[0]))
            b.op("vector", lambda e, psb=psb: e.tensor_tensor(
                T1[:, :], psb[:, 0:512], zrep[:, :], ALU.mult), None)
            ue[0] = b.op("vector", lambda e, et=et, mc=mc: e.tensor_tensor(
                xr[:, et * W_ + mc * 512:et * W_ + mc * 512 + 512],
                xr[:, et * W_ + mc * 512:et * W_ + mc * 512 + 512],
                T1[:, :], ALU.add), sv)
            t_pb[bk_] = ue[0]
            t_xr_tok = ue[0]
    return t_xr_tok



def emit_l3attn(b, io, xr, xl, t_xl):
    nc = b.nc
    sv = b.sem("v")
    ss_ = b.sem("s")
    st = b.sem("t")
    sd = b.sem("d")
    sw = b.sem("w")
    sg = b.sem("g")
    scc = b.sem("c")

    OM = b.tiles["XL"]
    QB = b.sbuf("a3QB", [128, DT * W_], BF16)
    K2H = b.sbuf("a3K2H", [128, 8192], BF16)
    VHB = b.sbuf("a3VHB", [128, 4096], BF16)
    KSF = b.sbuf("a3KSF", [128, 2048], F32)
    KDG = b.sbuf("a3KDG", [128, 512], F32)
    PT = b.sbuf("a3PT", [128, 1024], BF16)
    QF = b.sbuf("a3QF", [128, 512], F32)
    T1 = b.sbuf("a3T1", [128, 512], F32)
    DG1 = b.sbuf("a3DG1", [128, 512], F32)
    zrep = b.sbuf("a3zrep", [128, 512], F32)
    rz = b.sbuf("a3rz", [1, 1536], F32)
    SAC = b.sbuf("a3SAC", [1, 128], F32)
    RS1 = b.sbuf("a3RS1", [1, 768], F32)
    KB2 = b.sbuf("a3KB2", [128, 64], F32)
    KRW = b.sbuf("a3KRW", [1, 512], F32)
    eighth = b.sbuf("a3eighth", [128, 1], F32)
    neg20 = b.sbuf("a3neg20", [128, 1], F32)
    bigt = b.sbuf("a3bigt", [128, 1], F32)
    nbigt = b.sbuf("a3nbigt", [128, 1], F32)
    onesrf = b.sbuf("a3onesrf", [1, 128], F32)
    onescb = b.sbuf("a3onescb", [128, 1], BF16)
    onescf = b.sbuf("a3onescf", [128, 1], F32)
    boc = b.sbuf("a3boc", [128, 8], F32)
    kag_in = nc.dram_tensor("a3kag_in", [1024, MTOK], F32, kind="Internal")
    kag_out = nc.dram_tensor("a3kag_out", [4096, MTOK], F32, kind="Internal")
    vag_in = nc.dram_tensor("a3vag_in", [MTOK, 1024], F32, kind="Internal")
    vag_out = nc.dram_tensor("a3vag_out", [S_, 1024], F32, kind="Internal")
    dmd = nc.dram_tensor("a3dmd", [16, MTOK], F32, kind="Internal").ap()
    sar_in = nc.dram_tensor("a3sar_in", [1, 128], F32, kind="Internal")
    sar_out = nc.dram_tensor("a3sar_out", [1, 128], F32, kind="Internal")
    kp_in = nc.dram_tensor("a3kp_in", [16, MTOK], F32, kind="Internal")
    kp_out = nc.dram_tensor("a3kp_out", [64, MTOK], F32, kind="Internal")
    groups = [[0, 1, 2, 3], [4, 5, 6, 7]]

    t_c = b.dma("gpsimd", boc[:, :], io["a3_bout"], sd)
    b.wait("vector", t_c)
    b.op("vector", lambda e: e.memset(eighth[:, :], 0.125), sv)
    b.op("vector", lambda e: e.memset(neg20[:, :], -20.0), sv)
    b.op("vector", lambda e: e.memset(bigt[:, :], 1e30), sv)
    b.op("vector", lambda e: e.memset(nbigt[:, :], -1e30), sv)
    b.op("vector", lambda e: e.memset(onesrf[:, :], 1.0), sv)
    b.op("vector", lambda e: e.memset(onescb[:, :], 1.0), sv)
    t_ms = b.op("vector", lambda e: e.memset(onescf[:, :], 1.0), sv)
    b.wait("tensor", t_ms)
    b.wait("scalar", t_ms)

    lin = Lin(b, "a3", 8, [b.pool["e"], b.pool["f"]])
    qf_free = [None]

    for ot in range(DT):
        def evq(psb, tmm, mc, ot=ot):
            b.wait("scalar", tmm)
            return b.op("scalar", lambda e, ot=ot, mc=mc, psb=psb:
                        e.activation(
                QB[:, ot * W_ + mc * 512:ot * W_ + mc * 512 + 512],
                psb[:, 0:512], AF.Copy, bias=0.0, scale=1.0), ss_)
        lin.otile(io["a3_wqkvT"], 8, ot,
                  lambda kt, mc: xl[:, kt * W_ + mc * 512:kt * W_ + mc * 512 + 512],
                  evq, rhs_ready=t_xl if ot == 0 else None, n_mc=4)
    for ot in range(DT):
        def evk(psb, tmm, mc, ot=ot):
            b.wait("scalar", later(tmm, qf_free[0]))
            tk = b.op("scalar", lambda e, psb=psb: e.activation(
                QF[:, :], psb[:, 0:512], AF.Copy, bias=0.0, scale=1.0), ss_)
            b.wait("gpsimd", tk)
            qf_free[0] = b.dma(
                "gpsimd", kag_in.ap()[ot * 128:(ot + 1) * 128,
                                      mc * 512:mc * 512 + 512], QF[:, :], sd)
            return qf_free[0]
        lin.otile(io["a3_wqkvT"][:, 1024:2048], 8, ot,
                  lambda kt, mc: xl[:, kt * W_ + mc * 512:kt * W_ + mc * 512 + 512],
                  evk, n_mc=4)
    t_wvread = None
    for oc in range(2):
        t_wv = None
        for kt in range(DT):
            b.wait("sync", later(qf_free[0], t_wv, t_wvread))
            t_w = b.dma("sync", QF[:, :],
                        io["a3_wqkvT"][kt * 128:(kt + 1) * 128,
                                       2048 + oc * 512:2048 + oc * 512 + 512],
                        sw)
            b.wait("gpsimd", t_w)
            b.wait("gpsimd", t_wvread)
            t_wv = b.op("gpsimd", lambda e, kt=kt: e.tensor_copy(
                VHB[:, kt * 512:(kt + 1) * 512], QF[:, :]), sg)
        b.wait("tensor", t_wv)
        for mt in range(16):
            bk_ = lin.bank_i % 2
            psb = lin.psums[bk_]
            b.wait("tensor", lin.t_bank[bk_])
            t_mm = None
            for kt in range(DT):
                t_mm = b.op("tensor", lambda e, kt=kt, mt=mt, psb=psb:
                            e.matmul(
                    psb[:, 0:512],
                    xl[:, kt * W_ + mt * 128:kt * W_ + mt * 128 + 128],
                    VHB[:, kt * 512:(kt + 1) * 512],
                    start=(kt == 0), stop=(kt == DT - 1)),
                    st if kt == DT - 1 else None)
            if mt == 15:
                t_wvread = t_mm
            b.wait("scalar", later(t_mm, qf_free[0]))
            t1 = b.op("scalar", lambda e, psb=psb: e.activation(
                QF[:, :], psb[:, 0:512], AF.Copy, bias=0.0, scale=1.0), ss_)
            b.wait("gpsimd", t1)
            qf_free[0] = b.dma(
                "gpsimd", vag_in.ap()[mt * 128:(mt + 1) * 128,
                                      oc * 512:oc * 512 + 512], QF[:, :], sd)
            lin.t_bank[bk_] = t1
            lin.bank_i += 1
    b.barrier(["gpsimd"])
    b.cc("AllGather", ALU.bypass, groups, [kag_in.ap().opt()],
         [kag_out.ap().opt()], scc)
    b.cc("AllGather", ALU.bypass, groups, [vag_in.ap().opt()],
         [vag_out.ap().opt()], scc)
    b.barrier()

    t_kvcast = [None]
    t_kvread = [None]

    def load_k2h(j):
        for r in range(4):
            b.wait("sync", t_kvcast[0])
            t_w = b.dma("sync", KSF[:, :],
                        kag_out.ap()[r * 1024 + j * 128:
                                     r * 1024 + (j + 1) * 128, :], sw)
            b.wait("gpsimd", later(t_w, t_kvread[0]))
            t_kvcast[0] = b.op("gpsimd", lambda e, r=r: e.tensor_copy(
                K2H[:, r * 2048:(r + 1) * 2048], KSF[:, :]), sg)

    ps_s = [b.pool["a"], b.pool["b"]]
    ps_z = b.pool["d"][0:1, 0:512]
    ps_dg = b.pool["c"][0:1, 0:512]
    t_exp = [None, None]
    t_ptread = [None, None]
    t_psread = None
    t_dgread = None
    t_kdgread = None
    t_dmw = None
    t_sq = None
    for j in range(DT):
        load_k2h(j)
        b.wait("tensor", t_kvcast[0])
        for hh in range(2):
            base = 64 * hh
            h = 2 * j + hh
            for qc in range(4):
                b.wait("sync", t_kdgread)
                t_kdg = b.dma("sync", KDG[:, :],
                              kag_in.ap()[j * 128:(j + 1) * 128,
                                          qc * 512:qc * 512 + 512], sw)
                qsl = QB[base:base + 64,
                         j * W_ + qc * 512:j * W_ + qc * 512 + 512]
                b.wait("tensor", t_psread)
                t_kc = None
                for kc in range(64):
                    p = kc % 2
                    b.wait("tensor", t_exp[p])
                    t_s = b.op("tensor", lambda e, kc=kc, p=p, base=base,
                               qsl=qsl: e.matmul(
                        ps_s[p][:, 0:512],
                        K2H[base:base + 64, kc * 128:kc * 128 + 128],
                        qsl, start=True, stop=True), st)
                    b.wait("scalar", later(t_s, t_ptread[p]))
                    t_exp[p] = b.op("scalar", lambda e, p=p: e.activation(
                        PT[:, p * 512:(p + 1) * 512], ps_s[p][:, 0:512],
                        AF.Exp, bias=neg20[:, 0:1], scale=eighth[:, 0:1]),
                        ss_)
                    b.wait("tensor", t_exp[p])
                    t_kc = b.op("tensor", lambda e, kc=kc, p=p: e.matmul(
                        ps_z, onescb[:, :], PT[:, p * 512:(p + 1) * 512],
                        start=(kc == 0), stop=(kc == 63),
                        skip_group_check=True), st)
                    t_ptread[p] = t_kc
                b.wait("vector", later(t_kdg, t_dgread))
                t_d1 = b.op("vector", lambda e, base=base, qsl=qsl:
                            e.tensor_tensor(
                    DG1[base:base + 64, :], qsl, KDG[base:base + 64, :],
                    ALU.mult), sv)
                t_kdgread = t_d1
                b.wait("tensor", t_d1)
                t_dg = b.op("tensor", lambda e, base=base: e.matmul(
                    ps_dg, onescf[base:base + 64, 0:1],
                    DG1[base:base + 64, :], start=True, stop=True), st)
                t_dgread = t_dg
                b.wait("vector", later(t_kc, t_dmw))
                t_r1 = b.op("vector", lambda e: e.reciprocal(
                    rz[0:1, 0:512], ps_z), sv)
                b.wait("scalar", later(t_dg, t_sq))
                t_e1 = b.op("scalar", lambda e: e.activation(
                    rz[0:1, 512:1024], ps_dg, AF.Exp,
                    bias=neg20[0:1, 0:1], scale=eighth[0:1, 0:1]), ss_)
                t_psread = later(t_r1, t_e1)
                b.wait("vector", t_e1)
                t_dm = b.op("vector", lambda e: e.tensor_tensor(
                    rz[0:1, 1024:1536], rz[0:1, 512:1024], rz[0:1, 0:512],
                    ALU.mult), sv)
                t_sq = b.op("vector", lambda e: e.tensor_tensor(
                    rz[0:1, 512:1024], rz[0:1, 1024:1536],
                    rz[0:1, 1024:1536], ALU.mult), sv)
                b.op("vector", lambda e, h=h, qc=qc: e.reduce_sum(
                    SAC[0:1, qc * 16 + h:qc * 16 + h + 1],
                    rz[0:1, 1024:1536], axis=AX.X), sv)
                t_s2 = b.op("vector", lambda e, h=h, qc=qc: e.reduce_sum(
                    SAC[0:1, 64 + qc * 16 + h:64 + qc * 16 + h + 1],
                    rz[0:1, 512:1024], axis=AX.X), sv)
                b.wait("gpsimd", t_dm)
                t_dmw = b.dma("gpsimd", dmd[h:h + 1, qc * 512:qc * 512 + 512],
                              rz[0:1, 1024:1536], sd)
                t_kvread[0] = t_kc
    b.barrier(["gpsimd"])
    b.dma("gpsimd", sar_in.ap(), SAC[0:1, :], sd)
    b.cc("AllReduce", ALU.add, groups, [sar_in.ap().opt()],
         [sar_out.ap().opt()], scc)
    b.barrier()
    t_sl = b.dma("sync", SAC[0:1, :], sar_out.ap(), sw)
    b.wait("vector", t_sl)
    def _vstep(fn):
        tok = b.op("vector", fn, sv)
        b.wait("vector", tok)
        return tok
    _vstep(lambda e: e.tensor_tensor(
        RS1[0:1, 0:16], SAC[0:1, 0:16], SAC[0:1, 16:32], ALU.add))
    _vstep(lambda e: e.tensor_tensor(
        RS1[0:1, 64:80], SAC[0:1, 32:48], SAC[0:1, 48:64], ALU.add))
    _vstep(lambda e: e.tensor_tensor(
        RS1[0:1, 0:16], RS1[0:1, 0:16], RS1[0:1, 64:80], ALU.add))
    _vstep(lambda e: e.tensor_tensor(
        RS1[0:1, 128:144], SAC[0:1, 64:80], SAC[0:1, 80:96], ALU.add))
    _vstep(lambda e: e.tensor_tensor(
        RS1[0:1, 192:208], SAC[0:1, 96:112], SAC[0:1, 112:128], ALU.add))
    _vstep(lambda e: e.tensor_tensor(
        RS1[0:1, 128:144], RS1[0:1, 128:144], RS1[0:1, 192:208], ALU.add))
    _vstep(lambda e: e.tensor_scalar(
        RS1[0:1, 256:272], RS1[0:1, 0:16], 1.0 / 8192.0, None, ALU.mult))
    _vstep(lambda e: e.tensor_tensor(
        RS1[0:1, 320:336], RS1[0:1, 256:272], RS1[0:1, 256:272], ALU.mult))
    _vstep(lambda e: e.tensor_scalar(
        RS1[0:1, 384:400], RS1[0:1, 128:144], 1.0 / 8191.0, None, ALU.mult))
    _vstep(lambda e: e.tensor_scalar(
        RS1[0:1, 448:464], RS1[0:1, 320:336], -8192.0 / 8191.0, None,
        ALU.mult))
    t_var = _vstep(lambda e: e.tensor_tensor(
        RS1[0:1, 384:400], RS1[0:1, 384:400], RS1[0:1, 448:464], ALU.add))
    b.wait("scalar", t_var)
    t_sd = b.op("scalar", lambda e: e.activation(
        RS1[0:1, 512:528], RS1[0:1, 384:400], AF.Sqrt, bias=0.0,
        scale=1.0), ss_)
    b.wait("vector", t_sd)
    _vstep(lambda e: e.reciprocal(
        RS1[0:1, 576:592], RS1[0:1, 512:528]))
    _vstep(lambda e: e.tensor_tensor(
        RS1[0:1, 640:656], RS1[0:1, 256:272], RS1[0:1, 576:592], ALU.mult))
    t_nmi = _vstep(lambda e: e.tensor_scalar(
        RS1[0:1, 640:656], RS1[0:1, 640:656], -1.0, None, ALU.mult))
    b.wait("scalar", t_nmi)
    t_kw = None
    t_krd = None
    for h in range(16):
        for qc in range(4):
            b.wait("sync", t_krd)
            t_ld = b.dma("sync", rz[0:1, 1024:1536],
                         dmd[h:h + 1, qc * 512:qc * 512 + 512], sw)
            b.wait("scalar", later(t_ld, t_kw))
            t_az = b.op("scalar", lambda e, h=h: e.activation(
                rz[0:1, 0:512], rz[0:1, 1024:1536], AF.Abs,
                bias=RS1[0:1, 640 + h:641 + h],
                scale=RS1[0:1, 576 + h:577 + h]), ss_)
            t_krd = t_az
            b.wait("vector", t_az)
            t_kp = b.op("vector", lambda e: e.tensor_scalar(
                rz[0:1, 512:1024], rz[0:1, 0:512], 1.5, None, ALU.is_le),
                sv)
            b.wait("gpsimd", t_kp)
            t_kw = b.dma("gpsimd",
                         kp_in.ap()[h:h + 1, qc * 512:qc * 512 + 512],
                         rz[0:1, 512:1024], sd)
    b.barrier(["gpsimd"])
    b.cc("AllGather", ALU.bypass, groups, [kp_in.ap().opt()],
         [kp_out.ap().opt()], scc)
    b.barrier()

    psO = b.pool["r2"][:, 0:512]
    ps_c = b.pool["c"]
    t_exp = [None, None]
    t_ptread = [None, None]
    t_psread = None
    t_kb2read = None
    t_kbps = None
    t_krwread = None
    t_vread = None
    t_omw = None
    t_zev = None
    t_rzread = None
    t_kvread[0] = None
    t_kvcast[0] = t_kvcast[0]
    for j in range(DT):
        load_k2h(j)
        b.wait("tensor", t_kvcast[0])
        for hh in range(2):
            base = 64 * hh
            h = 2 * j + hh
            for grp in range(4):
                b.wait("sync", t_kvcast[0])
                t_w = b.dma("sync",
                            KSF[:, 0:1024].rearrange("p (c d) -> p c d", d=64),
                            vag_out.ap()[grp * 2048:(grp + 1) * 2048,
                                         64 * h:64 * h + 64]
                            .rearrange("(c t) d -> t c d", t=128), sw)
                b.wait("gpsimd", later(t_w, t_vread))
                t_kvcast[0] = b.op("gpsimd", lambda e, grp=grp: e.tensor_copy(
                    VHB[:, grp * 1024:(grp + 1) * 1024], KSF[:, 0:1024]), sg)
            b.wait("tensor", later(t_zev, t_kb2read))
            t_bm = None
            for r in range(4):
                for pc_ in range(4):
                    b.wait("sync", t_krwread)
                    t_kr = b.dma("sync", KRW[0:1, :],
                                 kp_out.ap()[16 * r + h:16 * r + h + 1,
                                             pc_ * 512:pc_ * 512 + 512], sw)
                    b.wait("tensor", t_kr)
                    for i in range(4):
                        kc = 16 * r + pc_ * 4 + i
                        t_bm = b.op("tensor", lambda e, kc=kc, i=i: e.matmul(
                            ps_c[:, kc:kc + 1],
                            KRW[0:1, i * 128:(i + 1) * 128],
                            onescf[0:1, 0:1], start=(kc == 0), stop=True,
                            skip_group_check=True), st)
                    t_krwread = t_bm
            b.wait("scalar", t_bm)
            t_kbev = b.op("scalar", lambda e: e.activation(
                KB2[:, :], ps_c[:, 0:64], AF.Identity,
                bias=nbigt[:, 0:1], scale=bigt[:, 0:1]), ss_)
            t_kbps = t_kbev
            b.wait("tensor", t_kvcast[0])
            for qc in range(4):
                b.wait("sync", t_rzread)
                t_kl = b.dma("sync", rz[0:1, 1024:1536],
                             kp_in.ap()[h:h + 1, qc * 512:qc * 512 + 512],
                             sw)
                qsl = QB[base:base + 64,
                         j * W_ + qc * 512:j * W_ + qc * 512 + 512]
                b.wait("tensor", later(t_psread, t_omw))
                b.wait("scalar", t_kbev)
                t_kc = None
                t_pv = None
                for kc in range(64):
                    p = kc % 2
                    b.wait("tensor", t_exp[p])
                    t_s = b.op("tensor", lambda e, kc=kc, p=p, base=base,
                               qsl=qsl: e.matmul(
                        ps_s[p][:, 0:512],
                        K2H[base:base + 64, kc * 128:kc * 128 + 128],
                        qsl, start=True, stop=True), st)
                    b.wait("scalar", later(t_s, t_ptread[p]))
                    t_exp[p] = b.op("scalar", lambda e, p=p, kc=kc:
                                    e.activation(
                        PT[:, p * 512:(p + 1) * 512], ps_s[p][:, 0:512],
                        AF.Exp, bias=KB2[:, kc:kc + 1],
                        scale=eighth[:, 0:1]), ss_)
                    t_kb2read = t_exp[p]
                    b.wait("tensor", t_exp[p])
                    t_pv = b.op("tensor", lambda e, kc=kc, p=p, base=base:
                                e.matmul(
                        psO[base:base + 64, 0:512] if False else
                        b.pool["r2"][base:base + 64, 0:512],
                        VHB[:, kc * 64:kc * 64 + 64],
                        PT[:, p * 512:(p + 1) * 512],
                        start=(kc == 0), stop=(kc == 63),
                        skip_group_check=True), st)
                    t_kc = b.op("tensor", lambda e, kc=kc, p=p: e.matmul(
                        ps_z, onescb[:, :], PT[:, p * 512:(p + 1) * 512],
                        start=(kc == 0), stop=(kc == 63),
                        skip_group_check=True), st)
                    t_ptread[p] = later(t_kc, t_pv)
                t_vread = t_pv
                t_kvread[0] = t_kc
                b.wait("vector", later(t_kc, t_kl))
                b.op("vector", lambda e: e.tensor_scalar(
                    rz[0:1, 0:512], ps_z, 1e-30, None, ALU.add), sv)
                b.op("vector", lambda e: e.reciprocal(
                    rz[0:1, 512:1024], rz[0:1, 0:512]), sv)
                t_z2 = b.op("vector", lambda e: e.tensor_tensor(
                    rz[0:1, 0:512], rz[0:1, 512:1024], rz[0:1, 1024:1536],
                    ALU.mult), sv)
                t_rzread = t_z2
                t_psread = t_z2
                b.wait("tensor", later(t_z2, t_zev, t_kbps))
                t_rp = b.op("tensor", lambda e, base=base: e.matmul(
                    ps_c[base:base + 64, 0:512], onesrf[0:1, 0:64],
                    rz[0:1, 0:512], start=True, stop=True), st)
                b.wait("scalar", t_rp)
                t_ze = b.op("scalar", lambda e, base=base: e.activation(
                    zrep[base:base + 64, :], ps_c[base:base + 64, 0:512],
                    AF.Copy, bias=0.0, scale=1.0), ss_)
                t_zev = t_ze
                b.wait("vector", later(t_ze, t_pv))
                t_om = b.op("vector", lambda e, base=base, j=j, qc=qc:
                            e.tensor_tensor(
                    OM[base:base + 64,
                       j * W_ + qc * 512:j * W_ + qc * 512 + 512],
                    b.pool["r2"][base:base + 64, 0:512],
                    zrep[base:base + 64, :], ALU.mult), sv)
                t_omw = t_om
    b.barrier()

    t_t1read = [None]
    t_last = None
    for ot in range(DT):
        def evo(psb, tmm, mc, ot=ot):
            b.wait("scalar", later(tmm, t_t1read[0]))
            t1 = b.op("scalar", lambda e, psb=psb, ot=ot: e.activation(
                T1[:, :], psb[:, 0:512], AF.Identity,
                bias=boc[:, ot:ot + 1], scale=1.0), ss_)
            b.wait("vector", t1)
            tok = b.op("vector", lambda e, ot=ot, mc=mc: e.tensor_tensor(
                xr[:, ot * W_ + mc * 512:ot * W_ + mc * 512 + 512],
                xr[:, ot * W_ + mc * 512:ot * W_ + mc * 512 + 512],
                T1[:, :], ALU.add), sv)
            t_t1read[0] = tok
            return tok
        t_last = lin.otile(io["a3_woT"], 8, ot,
                           lambda kt, mc: OM[:, kt * W_ + mc * 512:
                                             kt * W_ + mc * 512 + 512],
                           evo, n_mc=4)
    return t_last



def run_l0attn(in_maps, xchunks, trace=False):
    nc = bass.Bass("TRN2", target_bir_lowering=False, debug=False)
    io = {}
    io["xin"] = nc.dram_tensor("xin", [D_, MTOK], F32,
                               kind="ExternalInput").ap()
    wkeys = ["m0_xT", "m0_inwT", "m0_convw", "m0_convb", "m0_xprojT",
             "m0_dtwT", "m0_dtb", "m0_Alog", "m0_D", "m0_outwT", "m0_flip",
             "m0_oh"]
    for k in wkeys:
        a = in_maps[0][k]
        dt_ = BF16 if a.dtype == BFNP else F32
        io[k] = nc.dram_tensor(k, list(a.shape), dt_,
                               kind="ExternalInput").ap()
    io["xout"] = nc.dram_tensor("xout", [D_, MTOK], F32,
                                kind="ExternalOutput").ap()
    yloc = nc.dram_tensor("m0yloc", [1024, S_], F32, kind="Internal").ap()
    rs_in = nc.dram_tensor("m0rsin", [4096, MTOK], F32, kind="Internal")
    rs_out = nc.dram_tensor("m0rsout", [1024, MTOK], F32, kind="Internal")

    with ExitStack() as stack:
        b = Bld(nc, stack)
        pool = {k: b.psum("P" + k, [128, 512], F32) for k in "abcdef"}
        pool["r2"] = b.psum("Pr2", [128, 1024], F32)
        b.pool = pool
        sv = b.sem("v")
        ss_ = b.sem("s")
        st = b.sem("t")
        sd = b.sem("d")
        sw = b.sem("w")
        sg = b.sem("g")
        scc = b.sem("c")
        sx = b.sem("x")
        so = b.sem("o")

        XRC = b.sbuf("XRC", [128, 8 * 512], F32)
        XLC = b.sbuf("XLC", [128, 8 * 512], BF16)
        XHC = b.sbuf("XHC", [128, 16 * 516], BF16)
        XC = b.sbuf("XC", [128, 16 * 512], BF16)
        ZHC = b.sbuf("ZHC", [128, 8 * 512], BF16)
        XCF = b.sbuf("XCF", [128, 512], F32)
        XPJ = b.sbuf("XPJ", [128, 16 * 96], BF16)
        DTWB = b.sbuf("DTWB", [64, 1024], BF16)
        OWB = b.sbuf("OWB", [128, 64 * 128], BF16)
        INWB = b.sbuf("INWB", [128, 192 * 128], BF16)
        OHB = b.sbuf("OHB", [16, 16 * 128], BF16)
        PRJB = b.sbuf("PRJB", [64, 512], BF16)
        BPS = b.sbuf("BPS", [16, 1024], BF16)
        DL3 = b.sbuf("DL3", [128, 512], F32)
        DLT = b.sbuf("DLT", [128, 512], F32)
        U = b.sbuf("U", [128, 512], F32)
        DA = b.sbuf("DA", [128, 512], F32)
        DBU = b.sbuf("DBU", [128, 512], F32)
        HS = b.sbuf("HS", [128, 512], F32)
        Y = b.sbuf("Y", [128, 512], F32)
        TY = b.sbuf("TY", [128, 512], F32)
        SZ = b.sbuf("SZ", [128, 512], F32)
        YB = b.sbuf("YB", [128, 8 * 512], BF16)
        QFM = b.sbuf("QFM", [128, 512], F32)
        DL1 = XCF
        DL2 = QFM
        WSM = b.sbuf("WSM", [128, 1536], F32)
        HCAR = b.sbuf("HCAR", [128, 128], F32)
        ACOL = b.sbuf("ACOL", [128, 128], F32)
        CW = b.sbuf("CW", [128, 64], F32)
        CB = b.sbuf("CB", [128, 16], F32)
        DTB = b.sbuf("DTB", [128, 8], F32)
        DCOL = b.sbuf("DCOL", [128, 8], F32)
        FLC = b.sbuf("FLC", [128, 1], F32)
        negone = b.sbuf("negone", [128, 1], F32)
        BL1 = b.sbuf("BL1", [128, 2048], F32)
        BL2 = b.sbuf("BL2", [128, 2048], F32)

        b.dma("gpsimd", CW[:, :], io["m0_convw"], sd)
        b.dma("gpsimd", CB[:, :], io["m0_convb"], sd)
        b.dma("gpsimd", DTB[:, :], io["m0_dtb"], sd)
        b.dma("gpsimd", DCOL[:, :], io["m0_D"], sd)
        b.dma("gpsimd", FLC[:, :], io["m0_flip"], sd)
        t_al = b.dma("gpsimd", ACOL[:, :], io["m0_Alog"], sd)
        t_oh = b.dma("gpsimd", OHB[:, :], io["m0_oh"], sd)
        b.wait("scalar", t_al)
        t_ae = b.op("scalar", lambda e: e.activation(
            ACOL[:, :], ACOL[:, :], AF.Exp, bias=0.0, scale=1.0), ss_)
        b.wait("vector", t_ae)
        b.op("vector", lambda e: e.memset(negone[:, :], -1.0), sv)
        b.op("vector", lambda e: e.memset(HCAR[:, :], 0.0), sv)
        for ti in range(16):
            b.op("vector", lambda e, ti=ti: e.memset(
                XHC[:, ti * 516:ti * 516 + 3], 0.0), sv)
        t_an = b.op("vector", lambda e: e.tensor_scalar(
            ACOL[:, :], ACOL[:, :], -1.0, None, ALU.mult), sv)
        b.wait("scalar", t_an)
        t_xpw = b.dma("sync", WSM[:, 0:1536].rearrange(
            "p (k c) -> p k c", c=96),
            io["m0_xprojT"][0:2048, 0:96].rearrange("(k p) c -> p k c", p=128),
            sw)
        b.wait("gpsimd", t_xpw)
        t_xpc = b.op("gpsimd", lambda e: e.tensor_copy(
            XPJ[:, :], WSM[:, 0:1536]), sg)
        b.wait("sync", t_xpc)
        t_dtw = b.dma("sync", WSM[0:64, 0:1024], io["m0_dtwT"], sw)
        b.wait("gpsimd", t_dtw)
        t_dtc = b.op("gpsimd", lambda e: e.tensor_copy(
            DTWB[:, :], WSM[0:64, 0:1024]), sg)
        t_owc = t_dtc
        for kt in range(8):
            b.wait("sync", t_owc)
            t_w = b.dma("sync", WSM[:, 0:1024],
                        io["m0_outwT"][kt * 128:(kt + 1) * 128, :], sw)
            b.wait("gpsimd", t_w)
            t_owc = b.op("gpsimd", lambda e, kt=kt: e.tensor_copy(
                OWB[:, kt * 1024:(kt + 1) * 1024], WSM[:, 0:1024]), sg)
        t_iwc = t_owc
        for ot in range(24):
            b.wait("sync", t_iwc)
            t_w = b.dma("sync", WSM[:, 0:1024].rearrange(
                "p (k c) -> p k c", c=128),
                io["m0_inwT"][0:1024, ot * 128:(ot + 1) * 128]
                .rearrange("(k p) c -> p k c", p=128), sw)
            b.wait("gpsimd", t_w)
            t_iwc = b.op("gpsimd", lambda e, ot=ot: e.tensor_copy(
                INWB[:, ot * 1024:(ot + 1) * 1024], WSM[:, 0:1024]), sg)
        b.barrier()

        t_ipb = [None, None]
        t_carry = [None] * 16
        t_xc = [None] * 16
        t_yb = [None] * 8
        t_prj_read = None
        t_bps_read = None
        t_dl_read = None
        t_psd_read = None
        t_psa_read = None
        t_psb_read = None
        t_psc_read = None
        t_da_read = None
        t_ev_ow = [None, None]
        t_qfm = [None]

        for ci in range(16):
            if ci > 0:
                for ti in range(16):
                    t_carry[ti] = b.op("vector", lambda e, ti=ti:
                                       e.tensor_copy(
                        XHC[:, ti * 516:ti * 516 + 3],
                        XHC[:, ti * 516 + 512:ti * 516 + 515]), sv)
            t_in = None
            for d in range(8):
                t_in = b.dma("sync", XRC[:, d * 512:(d + 1) * 512],
                             io["m0_xT"][d * 128:(d + 1) * 128,
                                         ci * 512:ci * 512 + 512], sx)
            t_ln = emit_ln(b, "mn", XRC, XLC, after=[t_in], mcw=512,
                           dtiles=8, dmodel=1024.0, wfree=512)
            for ot in range(24):
                pp = ot % 2
                psb = pool["e"] if pp == 0 else pool["f"]
                b.wait("tensor", t_ipb[pp])
                if ot == 0:
                    b.wait("tensor", t_ln)
                t_mm = None
                for kt in range(8):
                    t_mm = b.op("tensor", lambda e, kt=kt, ot=ot, psb=psb:
                                e.matmul(
                        psb[:, 0:512],
                        INWB[:, (ot * 8 + kt) * 128:(ot * 8 + kt) * 128 + 128],
                        XLC[:, kt * 512:(kt + 1) * 512],
                        start=(kt == 0), stop=(kt == 7)),
                        st if kt == 7 else None)
                b.wait("scalar", t_mm)
                if ot < 16:
                    b.wait("scalar", t_carry[ot])
                    t_ev = b.op("scalar", lambda e, ot=ot, psb=psb:
                                e.activation(
                        XHC[:, ot * 516 + 3:ot * 516 + 515],
                        psb[:, 0:512], AF.Copy, bias=0.0, scale=1.0), ss_)
                else:
                    t_ev = b.op("scalar", lambda e, ot=ot, psb=psb:
                                e.activation(
                        ZHC[:, (ot - 16) * 512:(ot - 15) * 512],
                        psb[:, 0:512], AF.Copy, bias=0.0, scale=1.0), ss_)
                t_ipb[pp] = t_ev
            for ti in range(16):
                b.wait("vector", (b.sems["s"], b.sems["s"].n))
                t_c0 = b.op("vector", lambda e, ti=ti: e.tensor_scalar(
                    XCF[:, :], XHC[:, ti * 516:ti * 516 + 512],
                    CW[:, ti * 4:ti * 4 + 1], None, ALU.mult), sv)
                for k_ in range(1, 4):
                    t_c0 = b.op("vector", lambda e, ti=ti, k_=k_:
                                e.scalar_tensor_tensor(
                        XCF[:, :], XHC[:, ti * 516 + k_:ti * 516 + k_ + 512],
                        CW[:, ti * 4 + k_:ti * 4 + k_ + 1], XCF[:, :],
                        ALU.mult, ALU.add), sv)
                b.wait("scalar", t_c0)
                t_xc[ti] = b.op("scalar", lambda e, ti=ti: e.activation(
                    XC[:, ti * 512:(ti + 1) * 512], XCF[:, :], AF.Silu,
                    bias=CB[:, ti:ti + 1], scale=1.0), ss_)
            b.wait("tensor", later(*t_xc))
            b.wait("tensor", later(t_psa_read, t_psb_read, t_psc_read))
            t_ma = t_mb = t_mc = None
            for kt in range(16):
                t_ma = b.op("tensor", lambda e, kt=kt: e.matmul(
                    pool["a"][0:64, 0:512], XPJ[:, kt * 96:kt * 96 + 64],
                    XC[:, kt * 512:(kt + 1) * 512],
                    start=(kt == 0), stop=(kt == 15)), st)
            for kt in range(16):
                t_mb = b.op("tensor", lambda e, kt=kt: e.matmul(
                    pool["b"][0:16, 0:512], XPJ[:, kt * 96 + 64:kt * 96 + 80],
                    XC[:, kt * 512:(kt + 1) * 512],
                    start=(kt == 0), stop=(kt == 15)), st)
            for kt in range(16):
                t_mc = b.op("tensor", lambda e, kt=kt: e.matmul(
                    pool["c"][0:16, 0:512], XPJ[:, kt * 96 + 80:kt * 96 + 96],
                    XC[:, kt * 512:(kt + 1) * 512],
                    start=(kt == 0), stop=(kt == 15)), st)
            b.wait("scalar", later(t_ma, t_prj_read))
            t_prjb = b.op("scalar", lambda e: e.activation(
                PRJB[:, :], pool["a"][0:64, 0:512], AF.Copy, bias=0.0,
                scale=1.0), ss_)
            b.wait("scalar", later(t_mb, t_mc, t_bps_read))
            b.op("scalar", lambda e: e.activation(
                BPS[0:16, 0:512], pool["b"][0:16, 0:512], AF.Copy, bias=0.0,
                scale=1.0), ss_)
            t_bps = b.op("scalar", lambda e: e.activation(
                BPS[0:16, 512:1024], pool["c"][0:16, 0:512], AF.Copy,
                bias=0.0, scale=1.0), ss_)
            t_psa_read = t_prjb
            for ct in range(8):
                b.wait("tensor", later(t_prjb, t_psd_read))
                t_md = None
                for _ in range(1):
                    t_md = b.op("tensor", lambda e, ct=ct: e.matmul(
                        pool["d"][:, 0:512], DTWB[0:64, ct * 128:ct * 128 + 128],
                        PRJB[:, :], start=True, stop=True), st)
                t_prj_read = t_md
                b.wait("scalar", later(t_md, t_dl_read))
                t_x1 = b.op("scalar", lambda e, ct=ct: e.activation(
                    DL1[:, :], pool["d"][:, 0:512], AF.Identity,
                    bias=DTB[:, ct:ct + 1], scale=1.0), ss_)
                t_x2 = b.op("scalar", lambda e, ct=ct: e.activation(
                    DL2[:, :], pool["d"][:, 0:512], AF.Abs,
                    bias=DTB[:, ct:ct + 1], scale=1.0), ss_)
                t_psd_read = later(t_x1, t_x2)
                t_e = b.op("scalar", lambda e: e.activation(
                    DL3[:, :], DL2[:, :], AF.Exp, bias=0.0,
                    scale=negone[:, 0:1]), ss_)
                t_l = b.op("scalar", lambda e: e.activation(
                    DL2[:, :], DL3[:, :], AF.Ln, bias=1.0, scale=1.0), ss_)
                b.wait("vector", t_x1)
                b.op("vector", lambda e: e.tensor_scalar(
                    DL1[:, :], DL1[:, :], 0.0, None, ALU.max), sv)
                b.wait("vector", t_l)
                t_dlt = b.op("vector", lambda e: e.tensor_tensor(
                    DLT[:, :], DL1[:, :], DL2[:, :], ALU.add), sv)
                t_dl_read = t_dlt
                t_u = b.op("vector", lambda e, ct=ct: e.tensor_tensor(
                    U[:, :], DLT[:, :], XC[:, ct * 512:(ct + 1) * 512],
                    ALU.mult), sv)
                t_y = b.op("vector", lambda e, ct=ct: e.tensor_scalar(
                    Y[:, :], XC[:, ct * 512:(ct + 1) * 512],
                    DCOL[:, ct:ct + 1], None, ALU.mult), sv)
                for n in range(16):
                    b.wait("tensor", later(t_bps, t_psc_read, t_u))
                    t_br = b.op("tensor", lambda e, n=n: e.matmul(
                        pool["c"][:, 0:512], OHB[:, n * 128:(n + 1) * 128],
                        BPS[0:16, 0:512], start=True, stop=True), st)
                    t_bps_read = t_br
                    b.wait("vector", t_br)
                    t_dbu = b.op("vector", lambda e: e.tensor_tensor(
                        DBU[:, :], U[:, :], pool["c"][:, 0:512], ALU.mult),
                        sv)
                    t_psc_read = t_dbu
                    b.wait("scalar", later(t_dlt, t_da_read))
                    t_da = b.op("scalar", lambda e, ct=ct, n=n: e.activation(
                        DA[:, :], DLT[:, :], AF.Exp, bias=0.0,
                        scale=ACOL[:, ct * 16 + n:ct * 16 + n + 1]), ss_)
                    b.wait("vector", t_da)
                    t_scan = b.op("vector", lambda e, ct=ct, n=n:
                                  e.tensor_tensor_scan(
                        HS[:, :], DA[:, :], DBU[:, :],
                        HCAR[:, ct * 16 + n:ct * 16 + n + 1],
                        ALU.mult, ALU.add), sv)
                    t_da_read = t_scan
                    b.wait("tensor", later(t_scan, t_psb_read))
                    t_cr = b.op("tensor", lambda e, n=n: e.matmul(
                        pool["b"][:, 0:512], OHB[:, n * 128:(n + 1) * 128],
                        BPS[0:16, 512:1024], start=True, stop=True), st)
                    b.wait("vector", t_cr)
                    t_ty = b.op("vector", lambda e: e.tensor_tensor(
                        TY[:, :], HS[:, :], pool["b"][:, 0:512], ALU.mult),
                        sv)
                    t_psb_read = t_ty
                    t_y = b.op("vector", lambda e: e.tensor_tensor(
                        Y[:, :], Y[:, :], TY[:, :], ALU.add), sv)
                    b.op("vector", lambda e, ct=ct, n=n: e.tensor_copy(
                        HCAR[:, ct * 16 + n:ct * 16 + n + 1],
                        HS[:, 511:512]), sv)
                b.wait("scalar", t_y)
                t_sz = b.op("scalar", lambda e, ct=ct: e.activation(
                    SZ[:, :], ZHC[:, ct * 512:(ct + 1) * 512], AF.Silu,
                    bias=0.0, scale=1.0), ss_)
                b.wait("vector", t_sz)
                t_yb[ct] = b.op("vector", lambda e, ct=ct: e.tensor_tensor(
                    YB[:, ct * 512:(ct + 1) * 512], Y[:, :], SZ[:, :],
                    ALU.mult), sv)
            for ot in range(8):
                pp = ot % 2
                psb = pool["e"] if pp == 0 else pool["f"]
                b.wait("tensor", later(t_ev_ow[pp], *t_yb))
                t_mo = None
                for kt in range(8):
                    t_mo = b.op("tensor", lambda e, kt=kt, ot=ot, psb=psb:
                                e.matmul(
                        psb[:, 0:512],
                        OWB[:, (kt * 8 + ot) * 128:(kt * 8 + ot) * 128 + 128],
                        YB[:, kt * 512:(kt + 1) * 512],
                        start=(kt == 0), stop=(kt == 7)), st)
                b.wait("scalar", later(t_mo, t_qfm[0]))
                t_ev = b.op("scalar", lambda e, psb=psb: e.activation(
                    QFM[:, :], psb[:, 0:512], AF.Copy, bias=0.0, scale=1.0),
                    ss_)
                t_ev_ow[pp] = t_ev
                b.wait("gpsimd", t_ev)
                t_qfm[0] = b.dma("gpsimd",
                                 yloc[ot * 128:(ot + 1) * 128,
                                      ci * 512:ci * 512 + 512], QFM[:, :],
                                 sd)
            b.barrier()
        t_bl = None
        for tc2 in range(4):
            for ot in range(8):
                b.wait("sync", t_bl)
                b.dma("sync", BL1[:, :],
                      yloc[ot * 128:(ot + 1) * 128,
                           tc2 * 2048:tc2 * 2048 + 2048], sw)
                t_l2 = b.dma("sync", BL2[:, :],
                             yloc[ot * 128:(ot + 1) * 128,
                                  S_ - tc2 * 2048 - 2048:S_ - tc2 * 2048],
                             sw)
                b.wait("vector", t_l2)
                b.op("vector", lambda e: e.tensor_copy(
                    XRC[:, 0:2048], BL2[:, ::-1]), sv)
                b.op("vector", lambda e: e.tensor_tensor(
                    BL2[:, :], XRC[:, 0:2048], BL1[:, :], ALU.subtract), sv)
                t_bd = b.op("vector", lambda e: e.scalar_tensor_tensor(
                    BL1[:, :], BL2[:, :], FLC[:, 0:1], BL1[:, :],
                    ALU.mult, ALU.add), sv)
                b.wait("gpsimd", t_bd)
                t_bl = b.dma("gpsimd",
                             rs_in.ap()[tc2 * 1024 + ot * 128:
                                        tc2 * 1024 + (ot + 1) * 128, :],
                             BL1[:, :], sd)
        b.barrier(["gpsimd"])
        b.cc("ReduceScatter", ALU.add, [[0, 1, 2, 3], [4, 5, 6, 7]],
             [rs_in.ap().opt()], [rs_out.ap().opt()], scc)
        b.barrier()
        t_out = None
        t_pr = None
        for d in range(8):
            b.wait("sync", later(t_pr, t_out))
            b.dma("sync", BL1[:, :], io["xin"][d * 128:(d + 1) * 128, :], sw)
            t_r = b.dma("sync", BL2[:, :],
                        rs_out.ap()[d * 128:(d + 1) * 128, :], sw)
            b.wait("vector", t_r)
            t_pr = b.op("vector", lambda e: e.tensor_tensor(
                BL1[:, :], BL1[:, :], BL2[:, :], ALU.add), sv)
            b.wait("gpsimd", t_pr)
            t_out = b.dma("gpsimd", io["xout"][d * 128:(d + 1) * 128, :],
                          BL1[:, :], sd)
        dbg_names = []
        if DEBUG_TAPS:
            b.barrier(["sync"])
            for i_, (tn, plo, phi, clo, chi) in enumerate(DEBUG_TAPS):
                t = b.tiles[tn]
                dn = f"dbg{i_}_{tn}"
                od = nc.dram_tensor(dn, [phi - plo, chi - clo],
                                    t.ap().dtype, kind="ExternalOutput").ap()
                t_out = b.dma("sync", od, t[plo:phi, clo:chi], so)
                dbg_names.append(dn)
        b.wait("sync", t_out)
        b.finalize()
    maps = []
    for c in range(NCORES):
        m = {"xin": xchunks[c]}
        for k in wkeys:
            m[k] = in_maps[c][k]
        maps.append(m)
    res = run_bass_kernel_spmd(nc, maps, core_ids=list(range(NCORES)),
                               trace=trace)
    if DEBUG_TAPS:
        run_l0attn.dbg = [{dn: res.results[c][dn] for dn in dbg_names}
                          for c in range(NCORES)]
    return [res.results[c]["xout"] for c in range(NCORES)]


def _launch_block(name, emit_fn, weight_keys, in_maps, xchunks,
                  extra_loads=(), trace=False, ln_mode="xl"):
    nc = bass.Bass("TRN2", target_bir_lowering=False, debug=False)
    io = {}
    io["xin"] = nc.dram_tensor("xin", [D_, MTOK], F32, kind="ExternalInput").ap()
    for k in weight_keys:
        a = in_maps[0][k]
        dt_ = BF16 if a.dtype == BFNP else F32
        io[k] = nc.dram_tensor(k, list(a.shape), dt_, kind="ExternalInput").ap()
    io["xout"] = nc.dram_tensor("xout", [D_, MTOK], F32,
                                kind="ExternalOutput").ap()
    with ExitStack() as stack:
        b = Bld(nc, stack)
        xr = b.sbuf("XR", [128, DT * W_], F32)
        xl = b.sbuf("XL", [128, DT * W_], BF16) if ln_mode == "xl" else None
        pool = {k: b.psum("P" + k, [128, 512], F32) for k in "abcdef"}
        pool["r2"] = b.psum("Pr2", [128, 1024], F32)
        b.pool = pool
        sx = b.sem("x")
        t_in = None
        for d in range(DT):
            t_in = b.dma("sync", xr[:, d * W_:(d + 1) * W_],
                         io["xin"][d * 128:(d + 1) * 128, :], sx)
        if ln_mode == "ab":
            ab = b.sbuf("LNAB", [1, 2 * W_], F32)
            t_ln = emit_ln(b, "ln", xr, None, after=[t_in], ab_out=ab,
                           write_xl=False)
        else:
            t_ln = emit_ln(b, "ln", xr, xl, after=[t_in])
        b.barrier()
        t_done = emit_fn(b, io, xr, xl, t_ln)
        b.wait("sync", t_done)
        for snm in ("v", "s", "t"):
            if snm in b.sems:
                b.wait("sync", (b.sems[snm], b.sems[snm].n))
        so = b.sem("o")
        t_out = None
        for d in range(DT):
            t_out = b.dma("sync", io["xout"][d * 128:(d + 1) * 128, :],
                          xr[:, d * W_:(d + 1) * W_], so)
        dbg_names = []
        if DEBUG_TAPS:
            b.barrier(["sync"])
            for i, (tn, plo, phi, clo, chi) in enumerate(DEBUG_TAPS):
                t = b.tiles[tn]
                dn = f"dbg{i}_{tn}"
                dt_ = t.ap().dtype
                od = nc.dram_tensor(dn, [phi - plo, chi - clo], dt_,
                                    kind="ExternalOutput").ap()
                t_out = b.dma("sync", od, t[plo:phi, clo:chi], so)
                dbg_names.append(dn)
        b.wait("sync", t_out)
        b.finalize()
    maps = []
    for c in range(NCORES):
        m = {"xin": xchunks[c]}
        for k in weight_keys:
            m[k] = in_maps[c][k]
        maps.append(m)
    res = run_bass_kernel_spmd(nc, maps, core_ids=list(range(NCORES)),
                               trace=trace)
    if DEBUG_TAPS:
        _launch_block.dbg = [{dn: res.results[c][dn] for dn in dbg_names}
                             for c in range(NCORES)]
    return [res.results[c]["xout"] for c in range(NCORES)]


def run_l0ffn(in_maps, xchunks):
    keys = ["f0_w1T", "f0_w2T", "f0_h1T", "f0_h2T", "f0_w3T",
            "f0_s1", "f0_s2", "f0_h1s", "f0_h2s", "f0_s3"]
    return _launch_block("l0ffn", emit_l0ffn, keys, in_maps, xchunks)


def run_l1attn(in_maps, xchunks):
    keys = ["a1_WqT", "a1_WkT", "a1_WvT", "a1_WoT", "a1_bq", "a1_bo",
            "a1_bk", "a1_bv"]
    return _launch_block("l1attn", emit_l1attn, keys, in_maps, xchunks)


def run_l1ffn(in_maps, xchunks):
    keys = ["m1_W1", "m1_W2", "m1_WrP", "m1_br", "m1_b1pc", "m1_b2pc"]
    return _launch_block("l1ffn", emit_l1ffn, keys, in_maps, xchunks,
                         ln_mode="ab")


def run_l2ffn(in_maps, xchunks):
    keys = ["m2_phi", "eye128b", "m2_Wc", "m2_bnc", "m2_maskc"]
    return _launch_block("l2ffn", emit_l2ffn, keys, in_maps, xchunks)


def run_l2attn(in_maps, xchunks):
    keys = ["a2_wqT", "a2_wkT", "a2_wvT", "a2_woPT", "a2_bq", "a2_bkc",
            "a2_bo", "a2_bv", "a2_ropec", "a2_ropes", "a2_eye64"]
    return _launch_block("l2attn", emit_l2attn, keys, in_maps, xchunks)


def run_l3attn(in_maps, xchunks):
    keys = ["a3_wqkvT", "a3_woT", "a3_bout"]
    return _launch_block("l3attn", emit_l3attn, keys, in_maps, xchunks)


def run_l3ffn(in_maps, xchunks):
    keys = ["f3_C", "f3_S", "f3_WrT", "f3_WiT", "f3_bias"]
    return _launch_block("l3ffn", emit_l3ffn, keys, in_maps, xchunks)


def kernel(**inputs):
    in_maps = prep_inputs(inputs)
    x0 = np.asarray(inputs["L0_attn_x"], np.float32)
    x0T = np.ascontiguousarray(x0.transpose(0, 2, 1))
    chs = [np.ascontiguousarray(
        x0T[c // 4][:, MTOK * (c % 4):MTOK * (c % 4 + 1)])
        for c in range(NCORES)]
    chs = run_l0attn(in_maps, chs)
    chs = run_l0ffn(in_maps, chs)
    chs = run_l1attn(in_maps, chs)
    chs = run_l1ffn(in_maps, chs)
    chs = run_l2attn(in_maps, chs)
    chs = run_l2ffn(in_maps, chs)
    chs = run_l3attn(in_maps, chs)
    chs = run_l3ffn(in_maps, chs)
    out = np.empty((2, 8192, 1024), np.float32)
    for c in range(NCORES):
        out[c // 4, MTOK * (c % 4):MTOK * (c % 4 + 1)] = \
            np.asarray(chs[c], np.float32).T
    return out
